# version 03
# baseline (speedup 1.0000x reference)
import numpy as np
import ml_dtypes
import concourse.bass as bass
import concourse.mybir as mybir
from concourse.bass_utils import run_bass_kernel_spmd

F32 = mybir.dt.float32
BF16 = mybir.dt.bfloat16
I8 = mybir.dt.int8
ALU = mybir.AluOpType
ACT = mybir.ActivationFunctionType
AX = mybir.AxisListType

P = 128
NW = 8
TW = 1024
NDT = 2
NN = 16
D = 2048
M1 = 8192
Q_B = 128.0
CLIP_HI = float(np.float32(np.float32(128.0) - np.float32(1e-6)))
CLIP_LO = -CLIP_HI
EPS1 = float(np.float32(1e-5))
EPS2 = float(np.float32(1e-6))
GEPS = float(np.float32(1e-6))
CONST_VALS = [EPS1, EPS2, -1.0, -0.5]


def build(dbg=False, noffn=False):
    nc = bass.Bass("TRN2", target_bir_lowering=False)

    i_xsc = nc.dram_tensor("xsc", [NW, P, NDT, 1027], F32, kind="ExternalInput").ap()
    i_xpt = nc.dram_tensor("xpt", [NW, P, 16, P], F32, kind="ExternalInput").ap()
    i_ws1 = nc.dram_tensor("ws1", [P, 16, 1024], F32, kind="ExternalInput").ap()
    i_ws2 = nc.dram_tensor("ws2", [P, 64, 256], F32, kind="ExternalInput").ap()
    i_xpw = nc.dram_tensor("xpw", [P, NDT, 160], F32, kind="ExternalInput").ap()
    i_dtw = nc.dram_tensor("dtw", [P, NDT, P], F32, kind="ExternalInput").ap()
    i_dtb = nc.dram_tensor("dtb", [P, NDT], F32, kind="ExternalInput").ap()
    i_alog = nc.dram_tensor("alog", [P, NDT, NN], F32, kind="ExternalInput").ap()
    i_cw = nc.dram_tensor("cwv", [P, NDT, 4], F32, kind="ExternalInput").ap()
    i_cb = nc.dram_tensor("cbv", [P, NDT], F32, kind="ExternalInput").ap()
    i_dp = nc.dram_tensor("dpv", [P, NDT], F32, kind="ExternalInput").ap()
    i_ones = nc.dram_tensor("onesf", [P, 1], F32, kind="ExternalInput").ap()
    i_cvals = nc.dram_tensor("cvals", [P, 8], F32, kind="ExternalInput").ap()
    i_id = nc.dram_tensor("identf", [P, P], F32, kind="ExternalInput").ap()

    o_out = nc.dram_tensor("out", [NW, P, D], F32, kind="ExternalOutput").ap()
    o_x2 = nc.dram_tensor("dbg_x2", [NW, P, D], F32, kind="ExternalOutput").ap() if dbg else None
    o_dst = nc.dram_tensor("dbg_st", [NW, 1, 2048], F32, kind="ExternalOutput").ap() if dbg else None
    o_dxh = nc.dram_tensor("dbg_xh", [NW, P, 2054], F32, kind="ExternalOutput").ap() if dbg else None
    o_dxc = nc.dram_tensor("dbg_xc", [NW, P, 2048], F32, kind="ExternalOutput").ap() if dbg else None
    o_ddl = nc.dram_tensor("dbg_dl", [NW, P, 2048], F32, kind="ExternalOutput").ap() if dbg else None
    o_dyy = nc.dram_tensor("dbg_yy", [NW, P, 2048], F32, kind="ExternalOutput").ap() if dbg else None
    o_dpj = nc.dram_tensor("dbg_pj", [NW, P, 1024], F32, kind="ExternalOutput").ap() if dbg else None
    o_dbo = nc.dram_tensor("dbg_bo", [NW, 160, 1024], F32, kind="ExternalOutput").ap() if dbg else None
    o_dab = nc.dram_tensor("dbg_ab", [NW, P, 4096], F32, kind="ExternalOutput").ap() if dbg else None
    o_dhc = nc.dram_tensor("dbg_hc", [NW, P, 2048+64], F32, kind="ExternalOutput").ap() if dbg else None
    o_ds2 = nc.dram_tensor("dbg_s2", [NW, 2, P, 64], F32, kind="ExternalOutput").ap() if dbg else None
    o_dh0 = nc.dram_tensor("dbg_h0", [NW, P, 6144], F32, kind="ExternalOutput").ap() if dbg else None
    o_ds3 = nc.dram_tensor("dbg_s3", [NW, P, 64], F32, kind="ExternalOutput").ap() if dbg else None
    o_dxq = nc.dram_tensor("dbg_xq", [NW, P, 2048], BF16, kind="ExternalOutput").ap() if dbg else None
    o_fh1 = nc.dram_tensor("dbg_fh1", [NW, P, M1], BF16, kind="ExternalOutput").ap() if dbg else None
    o_fh2 = nc.dram_tensor("dbg_fh2", [NW, P, M1], BF16, kind="ExternalOutput").ap() if dbg else None
    o_fgs = nc.dram_tensor("dbg_fgs", [NW, P, 32], F32, kind="ExternalOutput").ap() if dbg else None
    o_dscl = nc.dram_tensor("dbg_scl", [NW, P, 64], F32, kind="ExternalOutput").ap() if dbg else None
    o_daq = nc.dram_tensor("dbg_aq", [NW, 1, 2048], F32, kind="ExternalOutput").ap() if dbg else None
    o_dw1 = nc.dram_tensor("dbg_w1", [8, 16, P, 1024], I8, kind="ExternalOutput").ap() if dbg else None
    o_dth = nc.dram_tensor("dbg_th", [P, 6], F32, kind="ExternalOutput").ap() if dbg else None

    cc_a_in = nc.dram_tensor("cc_a_in", [2, TW], F32, kind="Internal").ap()
    cc_a_out = nc.dram_tensor("cc_a_out", [2, TW], F32, kind="Internal").ap()
    cc_b_in = nc.dram_tensor("cc_b_in", [160, TW], F32, kind="Internal").ap()
    cc_b_out = nc.dram_tensor("cc_b_out", [160, TW], F32, kind="Internal").ap()
    cc_y_in = nc.dram_tensor("cc_y_in", [256, TW], BF16, kind="Internal").ap()
    cc_y_out = nc.dram_tensor("cc_y_out", [8, 256, TW], BF16, kind="Internal").ap()
    cc_g_in = nc.dram_tensor("cc_g_in", [1, 2], F32, kind="Internal").ap()
    cc_g_out = nc.dram_tensor("cc_g_out", [8, 1, 2], F32, kind="Internal").ap()
    cc_w1_in = nc.dram_tensor("cc_w1_in", [16, P, 1024], I8, kind="Internal").ap()
    cc_w1_out = nc.dram_tensor("cc_w1_out", [8, 16, P, 1024], I8, kind="Internal").ap()
    cc_w2_in = nc.dram_tensor("cc_w2_in", [64, P, 256], I8, kind="Internal").ap()
    cc_w2_out = nc.dram_tensor("cc_w2_out", [8, 64, P, 256], I8, kind="Internal").ap()
    st_scr = nc.dram_tensor("st_scr", [NW, 2, TW], F32, kind="Internal").ap()
    g_scr = nc.dram_tensor("g_scr", [1, 8], F32, kind="Internal").ap()
    x2_scr = nc.dram_tensor("x2_scr", [NW, P, D], F32, kind="Internal").ap()
    qs_a = nc.dram_tensor("qs_a", [4, P], F32, kind="Internal").ap()
    qs_b = nc.dram_tensor("qs_b", [3, P], F32, kind="Internal").ap()

    import contextlib
    with contextlib.ExitStack() as _st:
        STATE_t = _st.enter_context(nc.sbuf_tensor("STATE", [P, 64], F32))
        ARA_t = _st.enter_context(nc.sbuf_tensor("ARA", [P, 16384], F32))
        ARB_t = _st.enter_context(nc.sbuf_tensor("ARB", [P, 16384], F32))
        XQS_t = _st.enter_context(nc.sbuf_tensor("XQS", [P, 16384], BF16))
        FSC_t = _st.enter_context(nc.sbuf_tensor("FSC", [P, 4096], F32))
        GST_t = _st.enter_context(nc.sbuf_tensor("GST", [P, 128], F32))
        SCL_t = _st.enter_context(nc.sbuf_tensor("SCL", [P, 64], F32))
        GRD1_t = _st.enter_context(nc.sbuf_tensor("GRD1", [P, 64], F32))
        GRD2_t = _st.enter_context(nc.sbuf_tensor("GRD2", [P, 64], F32))
        CON_t = _st.enter_context(nc.sbuf_tensor("CON", [P, 256], F32))
        CONB_t = _st.enter_context(nc.sbuf_tensor("CONB", [P, 640], BF16))
        SG_t = _st.enter_context(nc.sbuf_tensor("SG", [1, 32], F32))
        STT2_t = _st.enter_context(nc.sbuf_tensor("STT2", [1, 6144], F32))
        PSA_t = _st.enter_context(nc.psum_tensor("PSA", [P, 1024], F32))
        PSB_t = _st.enter_context(nc.psum_tensor("PSB", [P, 1024], F32))
        PSC_t = _st.enter_context(nc.psum_tensor("PSC", [P, 1024], F32))
        PSD_t = _st.enter_context(nc.psum_tensor("PSD", [P, 1024], F32))
        h_LD = _st.enter_context(nc.semaphore("sLD"))
        h_LS = _st.enter_context(nc.semaphore("sLS"))
        h_LG = _st.enter_context(nc.semaphore("sLG"))
        h_CC = _st.enter_context(nc.semaphore("sCC"))
        h_V = _st.enter_context(nc.semaphore("sV"))
        h_S = _st.enter_context(nc.semaphore("sS"))
        h_T = _st.enter_context(nc.semaphore("sT"))
        h_SO = _st.enter_context(nc.semaphore("sSO"))
        blk = _st.enter_context(nc.Block())

        ARA = ARA_t[:]
        ARB = ARB_t[:]
        ARAb = ARA.bitcast(BF16)
        ARBb = ARB.bitcast(BF16)
        ARAi = ARA.bitcast(I8)
        ARBi = ARB.bitcast(I8)
        XQS = XQS_t[:]
        FSC = FSC_t[:]
        GST = GST_t[:]
        SCL = SCL_t[:]
        STATE = STATE_t[:]
        CON = CON_t[:]
        CONB = CONB_t[:]
        SG = SG_t[:]
        PSA, PSB, PSC, PSD = PSA_t[:], PSB_t[:], PSC_t[:], PSD_t[:]
        STT2 = STT2_t[:]
        GRD1, GRD2 = GRD1_t[:], GRD2_t[:]
        c_idb = GRD1.bitcast(BF16)[:, 0:128]
        def stb(i):
            return STT2[0:1, i*1024:(i+1)*1024]
        def aq(i):
            return STT2[0:1, 4096+i*128:4096+(i+1)*128]
        YASB = ARBb[:, 2*8192:2*8192+2048]
        XPT = ARB[:, 9216:11264]
        OUTP = FSC[:, 0:1024]
        X2C = FSC[:, 1024:2048]

        sems = {"LD": h_LD, "LS": h_LS, "LG": h_LG, "CC": h_CC,
                "V": h_V, "S": h_S, "T": h_T, "SO": h_SO}

        c_ones = CON[:, 0:1]
        c_id = CON[:, 1:129]
        c_A = CON[:, 129:161]
        c_cw = CON[:, 161:169]
        c_cb = CON[:, 169:171]
        c_dp = CON[:, 171:173]
        c_dtb = CON[:, 173:175]
        c_TH = CON[:, 175:181]
        c_Aexp = CON[:, 181:213]
        c_cvals = CON[:, 213:221]
        for _i, _v in enumerate(CONST_VALS):
            nc.const_aps.register(F32, float(_v), c_cvals[:, _i:_i+1])
        cb_xpw = CONB[:, 0:320]
        cb_dtw = CONB[:, 320:576]

        XHAT = ARA[:, 0:2054]
        def XHATd(dt): return ARA[:, dt*1027:(dt+1)*1027]
        def XSQd(dt): return FSC[:, dt*1024:(dt+1)*1024]
        XHs = ARA[:, 2054:4108]
        def XHsd(dt): return ARA[:, 2054+dt*1027:2054+(dt+1)*1027]
        def XCd(dt): return ARA[:, 4108+dt*1024:4108+(dt+1)*1024]
        XC = ARA[:, 4108:6156]
        def DLTd(dt): return ARA[:, 6156+dt*1024:6156+(dt+1)*1024]
        def UUd(dt): return ARA[:, 8204+dt*1024:8204+(dt+1)*1024]
        def YYd(dt): return ARA[:, 10252+dt*1024:10252+(dt+1)*1024]
        YY = ARA[:, 10252:12300]
        CV1 = ARA[:, 12300:13324]
        CV2 = ARA[:, 13324:14348]
        SP1 = CV1
        SP2 = CV2
        XCB = ARAb[:, 2*14348:2*14348+2048]
        YYB = XCB
        DTI = ARAb[:, 2*15372:2*15372+1024]
        A_X2 = ARA[:, 0:2048]
        A_SQ = ARA[:, 2048:4096]
        A_XQF = ARA[:, 4096:6144]
        A_XQB = ARAb[:, 2*6144:2*6144+2048]
        A_X2DT = YY

        AT = [ARB[:, 0:1024], ARB[:, 1024:2048]]
        BT = [ARB[:, 2048:3072], ARB[:, 3072:4096]]
        HT = [ARB[:, 4096:5120], ARB[:, 5120:6144]]
        HCT = [ARBb[:, 12288:13312], ARBb[:, 13312:14336]]
        BRO = [ARB[:, 8192:9216], ARB[:, 9216:10240]]
        CRO = [ARB[:, 10240:11264], ARB[:, 11264:12288]]
        MB = ARB[:, 12288:13315]
        RB = ARB[:, 13315:14342]
        SB0 = 14342
        SROW = lambda r: ARB[r:r+1, SB0:SB0+1024]
        SROW2 = ARB[2:4, SB0:SB0+1024]
        PJ1 = ARB[:, 0:1024]
        PJ2 = ARB[0:32, 1024:2048]

        def HBs(s): return ARAb[:, s*8192:(s+1)*8192]
        def HBc(s, o): return ARAb[:, s*8192+o*1024:s*8192+(o+1)*1024]
        WS = [ARBb[:, 0:16384], ARBb[:, 16384:32768]]
        def HQTt(s, c): return ARBb[:, s*8192+c*128:s*8192+(c+1)*128]
        WS2 = [ARAb[:, 0:16384], ARAb[:, 16384:32768]]
        F_A = FSC[:, 0:2048]
        F_B = FSC[:, 2048:4096]
        F_A1 = FSC[:, 0:1024]

        def scl(base, i): return SCL[:, base+i:base+i+1]

        steps = []
        M = {}
        cnt = {k: 0 for k in sems}
        mode = {"dry": True}

        class IncProxy:
            pass

        def inc(ins, sem, n=None):
            if n is None:
                n = 16 if sem in ("LD", "LS", "LG", "SO") else 1
            cnt[sem] += n
            if ins is not None:
                ins.then_inc(sems[sem], n)

        def mark(key, sem):
            if mode["dry"]:
                assert key not in M, f"dup mark {key}"
                M[key] = (sem, cnt[sem])
            else:
                assert M[key] == (sem, cnt[sem]), (key, M[key], sem, cnt[sem])

        def step(eng, waits, fn):
            if not mode["dry"]:
                pass
            steps.append((eng, list(waits), fn))

        def w(sem, key, delta=0):
            return (sem, key, delta)

        hooks = {}

        def p0():
            def ld(e):
                pairs = [
                    (ARA[:, 0:16384].rearrange("p (k m) -> p k m", k=16), i_ws1),
                    (c_ones, i_ones),
                    (c_cvals, i_cvals),
                    (c_id, i_id),
                    (FSC[:, 576:608].rearrange("p (a n) -> p a n", a=2), i_alog),
                    (c_cw.rearrange("p (a j) -> p a j", a=2), i_cw),
                    (c_cb, i_cb),
                    (c_dp, i_dp),
                    (c_dtb, i_dtb),
                    (FSC[:, 0:320].rearrange("p (a q) -> p a q", a=2), i_xpw),
                    (FSC[:, 320:576].rearrange("p (a f) -> p a f", a=2), i_dtw),
                ]
                for o_, i_ in pairs:
                    inc(e.dma_start(out=o_, in_=i_) if e else None, "LD")
                mark("p0_loads", "LD")
            step("sy", [], ld)

            def ldB(e):
                inc(e.dma_start(out=ARB[:, 0:16384].rearrange("p (k m) -> p k m", k=64),
                                in_=i_ws2) if e else None, "LS")
                mark("p0_loadB", "LS")
            step("sc", [], ldB)

            def vconsts(e):
                inc(e.tensor_copy(cb_xpw, FSC[:, 0:320]) if e else None, "V")
                inc(e.tensor_copy(cb_dtw, FSC[:, 320:576]) if e else None, "V")
                inc(e.memset(STATE[:, 0:64], 0.0) if e else None, "V")
                inc(e.tensor_copy(c_idb, c_id) if e else None, "V")
                inc(e.memset(GRD2[:, 0:64], 7.0) if e else None, "V")
                inc(e.reduce_sum(SCL[:, 60:61], ARA[:, 0:16384], axis=AX.X,
                                 apply_absolute_value=True) if e else None, "V")
                inc(e.reduce_sum(SCL[:, 61:62], ARB[:, 0:16384], axis=AX.X,
                                 apply_absolute_value=True) if e else None, "V")
                mark("p0_abs", "V")
            step("ve", [w("LD", "p0_loads"), w("LS", "p0_loadB")], vconsts)

            def sexp(e):
                inc(e.activation(c_Aexp, FSC[:, 576:608], ACT.Exp, bias=0.0, scale=1.0)
                    if e else None, "S")
                mark("p0_aexp", "S")
            step("sc", [w("LD", "p0_loads")], sexp)

            def vneg(e):
                inc(e.tensor_scalar_mul(c_A, c_Aexp, -1.0) if e else None, "V")
                mark("p0_A", "V")
            step("ve", [w("S", "p0_aexp")], vneg)

            def gmm(e):
                ins = e.matmul(PSA[0:1, 0:2], c_ones, SCL[:, 60:62], start=True, stop=True) if e else None
                inc(ins, "T")
                mark("p0_gmm", "T")
            step("te", [w("V", "p0_abs")], gmm)

            def gevac(e):
                inc(e.copy(SG[0:1, 0:2], PSA[0:1, 0:2]) if e else None, "S")
                mark("p0_gpart", "S")
            step("sc", [w("T", "p0_gmm")], gevac)

            def gin(e):
                inc(e.dma_start(out=cc_g_in[:], in_=SG[0:1, 0:2]) if e else None, "LD")
                mark("p0_gin", "LD")
            step("sy", [w("S", "p0_gpart")], gin)

            def gag(e):
                ins = e.collective_compute("AllGather", ALU.bypass,
                                           replica_groups=[list(range(8))],
                                           ins=[cc_g_in[:]], outs=[cc_g_out[:]]) if e else None
                inc(ins, "CC")
                mark("ag_g", "CC")
            step("gp", [w("LD", "p0_gin")], gag)

            def gall(e):
                inc(e.dma_start(out=SG[0:1, 16:32],
                                in_=cc_g_out.rearrange("r o c -> o (r c)")) if e else None, "LD")
                mark("p0_gall", "LD")
            step("sy", [w("CC", "ag_g")], gall)

            def gmath(e):
                if e:
                    e.reduce_sum(SG[0:1, 8:9], SG[0:1, 16:32].rearrange("o (r c) -> o r c", c=2)[:, :, 0], axis=AX.X)
                    e.reduce_sum(SG[0:1, 9:10], SG[0:1, 16:32].rearrange("o (r c) -> o r c", c=2)[:, :, 1], axis=AX.X)
                    e.tensor_scalar_mul(SG[0:1, 8:9], SG[0:1, 8:9], 1.0 / (8192.0 * 2048.0))
                    e.tensor_scalar_mul(SG[0:1, 9:10], SG[0:1, 9:10], 1.0 / (8192.0 * 2048.0))
                    e.tensor_scalar(SG[0:1, 0:1], SG[0:1, 8:9], GEPS, 0.5, ALU.add, ALU.mult)
                    e.tensor_scalar(SG[0:1, 2:3], SG[0:1, 9:10], GEPS, 0.5, ALU.add, ALU.mult)
                    e.tensor_scalar(SG[0:1, 1:2], SG[0:1, 8:9], GEPS, -0.5, ALU.add, ALU.mult)
                    e.tensor_scalar(SG[0:1, 3:4], SG[0:1, 9:10], GEPS, -0.5, ALU.add, ALU.mult)
                    e.tensor_scalar_mul(SG[0:1, 4:5], SG[0:1, 8:9], 1.0 / 128.0)
                    ins = e.tensor_scalar_mul(SG[0:1, 5:6], SG[0:1, 9:10], 1.0 / 128.0)
                else:
                    ins = None
                inc(ins, "V")
                mark("p0_gmath", "V")
            step("ve", [w("LD", "p0_gall")], gmath)

            def gscr_w(e):
                inc(e.dma_start(out=g_scr[:], in_=SG[0:1, 0:8]) if e else None, "LD")
                mark("p0_gscr", "LD")
            step("sy", [w("V", "p0_gmath")], gscr_w)

            def thb(e):
                inc(e.dma_start(out=c_TH, in_=g_scr[0:1, 0:6].squeeze(0).partition_broadcast(P))
                    if e else None, "LS")
                mark("p0_th", "LS")
            step("sc", [w("LD", "p0_gscr")], thb)

            for mat in range(2):
                src_f = ARA if mat == 0 else ARB
                dst_i = ARAi if mat == 0 else ARBi
                th_ap = c_TH[:, 2*mat:2*mat+1]
                nth_ap = c_TH[:, 2*mat+1:2*mat+2]
                cc_w = cc_w1_in if mat == 0 else cc_w2_in
                for ch in range(8):
                    def tern_v(e, src_f=src_f, th_ap=th_ap, nth_ap=nth_ap, mat=mat, ch=ch):
                        chunk = src_f[:, ch*2048:(ch+1)*2048]
                        if e:
                            e.tensor_scalar(F_A, chunk, th_ap, nth_ap, ALU.min, ALU.max)
                            ins = e.tensor_sub(F_B, chunk, F_A)
                        else:
                            ins = None
                        inc(ins, "V", 1)
                        mark(f"tern_d:{mat}:{ch}", "V")
                    tvw = [w("LS", "p0_th")]
                    if mat == 0 and ch == 0:
                        tvw.append(w("S", "p0_aexp"))
                    else:
                        pm, pc = (mat, ch - 1) if ch > 0 else (mat - 1, 7)
                        tvw.append(w("S", f"tern:{pm}:{pc}"))
                    step("ve", tvw, tern_v)

                    def tern_s(e, dst_i=dst_i, mat=mat, ch=ch):
                        dst = dst_i[:, ch*8192:ch*8192+2048]
                        inc(e.activation(dst, F_B, ACT.Sign, bias=0.0, scale=1.0) if e else None, "S")
                        mark(f"tern:{mat}:{ch}", "S")
                    step("sc", [w("V", f"tern_d:{mat}:{ch}")], tern_s)

                    def tern_st(e, dst_i=dst_i, cc_w=cc_w, mat=mat, ch=ch):
                        kpc = 2 if mat == 0 else 8
                        src = dst_i[:, ch*8192:ch*8192+2048].rearrange("p (k m) -> p k m", k=kpc)
                        dst = cc_w[kpc*ch:kpc*(ch+1)].rearrange("k p m -> p k m")
                        inc(e.dma_start(out=dst, in_=src) if e else None, "LD")
                        mark(f"ternst:{mat}:{ch}", "LD")
                    step("sy", [w("S", f"tern:{mat}:{ch}")], tern_st)

            def emit_agw1():
                def agw1(e):
                    ins = e.collective_compute("AllGather", ALU.bypass,
                                               replica_groups=[list(range(8))],
                                               ins=[cc_w1_in[:]], outs=[cc_w1_out[:]]) if e else None
                    inc(ins, "CC")
                    mark("ag_w1", "CC")
                step("gp", [w("LD", "ternst:0:7")], agw1)
            hooks["agw1"] = emit_agw1

            def emit_agw2():
                def agw2(e):
                    ins = e.collective_compute("AllGather", ALU.bypass,
                                               replica_groups=[list(range(8))],
                                               ins=[cc_w2_in[:]], outs=[cc_w2_out[:]]) if e else None
                    inc(ins, "CC")
                    mark("ag_w2", "CC")
                step("gp", [w("LD", "ternst:1:7")], agw2)
                if dbg:
                    def dmp_w1(e):
                        inc(e.dma_start(out=o_dw1[:], in_=cc_w1_out[:]) if e else None, "SO")
                    step("sy", [w("CC", "ag_w1")], dmp_w1)
                    def dmp_th(e):
                        inc(e.dma_start(out=o_dth[:], in_=c_TH) if e else None, "SO")
                    step("sy", [w("LS", "p0_th")], dmp_th)
            hooks["agw2"] = emit_agw2

        def wave(gw):
            beta, ww = gw % 2, gw // 2

            def xsc_ld(e):
                inc(e.dma_start(out=ARA[:, 0:2054].rearrange("p (a t) -> p a t", a=2),
                                in_=i_xsc[gw]) if e else None, "LD")
                mark(f"xsc:{gw}", "LD")
            if gw == 0:
                waits = [w("LD", "ternst:1:7")]
            else:
                waits = [w("V", f"a_q2:{gw-1}"), w("LS", f"xqt:{gw-1}")]
                if gw == 4 and not noffn:
                    waits.append(w("V", "f_oc:0:3:3"))
                    waits.append(w("SO", "f_ost:0:3:3"))
            step("sy", waits, xsc_ld)

            def xsq(e):
                for dt in range(NDT):
                    inc(e.activation(XSQd(dt), ARA[:, dt*1027+3:dt*1027+1027], ACT.Square,
                                     bias=0.0, scale=1.0) if e else None, "S")
                mark(f"xsq:{gw}", "S")
            sw = [w("LD", f"xsc:{gw}")]
            if gw == 0:
                sw.append(w("S", "tern:1:7"))
            step("sc", sw, xsq)

            def smm(e):
                for hf in range(2):
                    for dt in range(NDT):
                        ins = e.matmul(PSA[0:1, hf*512:(hf+1)*512], c_ones,
                                       ARA[:, dt*1027+3+hf*512:dt*1027+3+(hf+1)*512],
                                       start=(dt == 0), stop=(dt == 1)) if e else None
                        inc(ins, "T")
                for hf in range(2):
                    for dt in range(NDT):
                        ins = e.matmul(PSB[0:1, hf*512:(hf+1)*512], c_ones,
                                       FSC[:, dt*1024+hf*512:dt*1024+(hf+1)*512],
                                       start=(dt == 0), stop=(dt == 1)) if e else None
                        inc(ins, "T")
                mark(f"stat_mm:{gw}", "T")
            tw_ = [w("S", f"xsq:{gw}"), w("LD", f"xsc:{gw}")]
            if gw == 0:
                tw_.append(w("S", "p0_gpart"))
            else:
                tw_.append(w("S", f"pj_evac:{gw-1}"))
            step("te", tw_, smm)

            def sevac(e):
                inc(e.copy(stb(0), PSA[0:1, 0:1024]) if e else None, "S")
                inc(e.copy(stb(1), PSB[0:1, 0:1024]) if e else None, "S")
                mark(f"stat_evac:{gw}", "S")
            sw2 = [w("T", f"stat_mm:{gw}")]
            if gw == 0:
                sw2.append(w("LD", "ternst:1:7"))
            step("sc", sw2, sevac)

            def apack(e):
                inc(e.dma_start(out=cc_a_in.rearrange("a t -> (a t)").unsqueeze(0),
                                in_=STT2[0:1, 0:2048]) if e else None, "LD")
                mark(f"apack:{gw}", "LD")
            step("sy", [w("S", f"stat_evac:{gw}")], apack)

            def ar_a(e):
                ins = e.collective_compute("AllReduce", ALU.add,
                                           replica_groups=[list(range(8))],
                                           ins=[cc_a_in[:]], outs=[cc_a_out[:]]) if e else None
                inc(ins, "CC")
                mark(f"ar_a:{gw}", "CC")
            step("gp", [w("LD", f"apack:{gw}")], ar_a)

            def aload(e):
                inc(e.dma_start(out=STT2[0:1, 0:2048],
                                in_=cc_a_out.rearrange("a t -> (a t)").unsqueeze(0)) if e else None, "LD")
                mark(f"aload:{gw}", "LD")
            step("sy", [w("CC", f"ar_a:{gw}")], aload)

            def statv(e):
                if e:
                    e.tensor_scalar_mul(stb(2), stb(0), 1.0 / 2048.0)
                    e.tensor_scalar_mul(stb(4), stb(1), 1.0 / 2048.0)
                    e.tensor_mul(stb(5), stb(2), stb(2))
                    ins = e.tensor_sub(stb(1), stb(4), stb(5))
                else:
                    ins = None
                inc(ins, "V", 1)
                mark(f"stat_v:{gw}", "V")
            step("ve", [w("LD", f"aload:{gw}")], statv)

            def rstd(e):
                if e:
                    e.activation(stb(5), stb(1), ACT.Ln, bias=EPS1, scale=1.0)
                    ins = e.activation(stb(3), stb(5), ACT.Exp, bias=0.0, scale=-0.5)
                else:
                    ins = None
                inc(ins, "S", 1)
                mark(f"rstd:{gw}", "S")
            step("sc", [w("V", f"stat_v:{gw}")], rstd)

            if dbg:
                def dmp_st(e):
                    inc(e.dma_start(out=o_dst[gw], in_=STT2[0:1, 2048:4096]) if e else None, "SO")
                step("sy", [w("S", f"rstd:{gw}")], dmp_st)

            def stscr_w(e):
                inc(e.dma_start(out=st_scr[gw].rearrange("a t -> (a t)").unsqueeze(0),
                                in_=STT2[0:1, 2048:4096]) if e else None, "LD")
                mark(f"stscr:{gw}", "LD")
            step("sy", [w("S", f"rstd:{gw}")], stscr_w)

            def bcast_mr(e):
                inc(e.dma_start(out=MB[:, 3:1027],
                                in_=st_scr[gw][0:1, :].squeeze(0).partition_broadcast(P))
                    if e else None, "LS")
                inc(e.dma_start(out=RB[:, 3:1027],
                                in_=st_scr[gw][1:2, :].squeeze(0).partition_broadcast(P))
                    if e else None, "LS")
                if ww > 0:
                    inc(e.dma_start(out=MB[:, 0:3],
                                    in_=st_scr[gw-2][0:1, 1021:1024].squeeze(0).partition_broadcast(P))
                        if e else None, "LS")
                    inc(e.dma_start(out=RB[:, 0:3],
                                    in_=st_scr[gw-2][1:2, 1021:1024].squeeze(0).partition_broadcast(P))
                        if e else None, "LS")
                mark(f"bcast_mr:{gw}", "LS")
            step("sc", [w("LD", f"stscr:{gw}")], bcast_mr)

            if ww == 0:
                def memh(e):
                    if e:
                        e.memset(MB[:, 0:3], 0.0)
                        e.memset(RB[:, 0:3], 0.0)
                step("ve", [], memh)

            def xhat(e):
                for dt in range(NDT):
                    if e:
                        e.tensor_sub(XHsd(dt), XHATd(dt), MB)
                        ins = e.tensor_mul(XHATd(dt), XHsd(dt), RB)
                    else:
                        ins = None
                    inc(ins, "V", 1)
                mark(f"xhat:{gw}", "V")
            step("ve", [w("LS", f"bcast_mr:{gw}"), w("LD", f"xsc:{gw}")], xhat)

            for dt in range(NDT):
                def conv_dt(e, dt=dt):
                    xh = XHATd(dt)
                    if e:
                        e.tensor_scalar_mul(CV1, xh[:, 0:1024], c_cw[:, dt*4:dt*4+1])
                        e.scalar_tensor_tensor(CV2, xh[:, 1:1025], c_cw[:, dt*4+1:dt*4+2], CV1, ALU.mult, ALU.add)
                        e.scalar_tensor_tensor(CV1, xh[:, 2:1026], c_cw[:, dt*4+2:dt*4+3], CV2, ALU.mult, ALU.add)
                        ins = e.scalar_tensor_tensor(CV2, xh[:, 3:1027], c_cw[:, dt*4+3:dt*4+4], CV1, ALU.mult, ALU.add)
                    else:
                        ins = None
                    inc(ins, "V", 1)
                    mark(f"conv:{gw}:{dt}", "V")
                wv = [w("V", f"xhat:{gw}")]
                if dt > 0:
                    wv.append(w("S", f"silu:{gw}:{dt-1}"))
                step("ve", wv, conv_dt)

                def silu_dt(e, dt=dt):
                    inc(e.activation(XCd(dt), CV2, ACT.Silu, bias=c_cb[:, dt:dt+1], scale=1.0)
                        if e else None, "S")
                    mark(f"silu:{gw}:{dt}", "S")
                step("sc", [w("V", f"conv:{gw}:{dt}")], silu_dt)

            def xcb_c(e):
                inc(e.copy(XCB, XC) if e else None, "S")
                mark(f"xcb:{gw}", "S")
            step("sc", [w("S", f"silu:{gw}:1")], xcb_c)

            if dbg:
                def dmp_xh(e):
                    inc(e.dma_start(out=o_dxh[gw], in_=XHAT) if e else None, "SO")
                step("sy", [w("V", f"xhat:{gw}")], dmp_xh)
                def dmp_xc(e):
                    inc(e.dma_start(out=o_dxc[gw], in_=XC) if e else None, "SO")
                step("sy", [w("S", f"silu:{gw}:1")], dmp_xc)
                def dmp_pj(e):
                    inc(e.dma_start(out=o_dpj[gw], in_=PJ1) if e else None, "SO")
                step("sy", [w("S", f"pj_evac:{gw}")], dmp_pj)

            def projmm(e):
                XCBv = XCB.rearrange("p (a t) -> p a t", a=2)
                for hf in range(2):
                    for dt in range(NDT):
                        ins = e.matmul(PSA[:, hf*512:(hf+1)*512], cb_xpw[:, dt*160:dt*160+128],
                                       XCBv[:, dt, hf*512:(hf+1)*512],
                                       start=(dt == 0), stop=(dt == 1)) if e else None
                        inc(ins, "T")
                for hf in range(2):
                    for dt in range(NDT):
                        ins = e.matmul(PSB[0:32, hf*512:(hf+1)*512], cb_xpw[:, dt*160+128:dt*160+160],
                                       XCBv[:, dt, hf*512:(hf+1)*512],
                                       start=(dt == 0), stop=(dt == 1)) if e else None
                        inc(ins, "T")
                mark(f"proj_mm:{gw}", "T")
            step("te", [w("S", f"xcb:{gw}"), w("S", f"stat_evac:{gw}")], projmm)

            def pjevac(e):
                inc(e.copy(PJ1, PSA[:, 0:1024]) if e else None, "S")
                inc(e.copy(PJ2, PSB[0:32, 0:1024]) if e else None, "S")
                mark(f"pj_evac:{gw}", "S")
            step("sc", [w("T", f"proj_mm:{gw}")], pjevac)

            def bpack(e):
                inc(e.dma_start(out=cc_b_in[0:128, :], in_=PJ1) if e else None, "LD")
                inc(e.dma_start(out=cc_b_in[128:160, :], in_=PJ2) if e else None, "LD")
                mark(f"bpack:{gw}", "LD")
            step("sy", [w("S", f"pj_evac:{gw}")], bpack)

            def ar_b(e):
                ins = e.collective_compute("AllReduce", ALU.add,
                                           replica_groups=[list(range(8))],
                                           ins=[cc_b_in[:]], outs=[cc_b_out[:]]) if e else None
                inc(ins, "CC")
                mark(f"ar_b:{gw}", "CC")
            arbw = [w("LD", f"bpack:{gw}")]
            if gw > 0:
                arbw.append(w("LS", f"bc:{gw-1}:15"))
            step("gp", arbw, ar_b)
            if gw == 0:
                hooks["agw1"]()

            if dbg:
                def dmp_bo(e):
                    inc(e.dma_start(out=o_dbo[gw], in_=cc_b_out[:]) if e else None, "SO")
                step("sy", [w("CC", f"ar_b:{gw}")], dmp_bo)

            def dtild(e):
                inc(e.dma_start(out=DTI.rearrange("p (a t) -> p a t", a=2),
                                in_=cc_b_out[0:128, :].rearrange("p (a t) -> p a t", a=2))
                    if e else None, "LG")
                mark(f"dti:{gw}", "LG")
            step("gp", [w("CC", f"ar_b:{gw}")], dtild)

            for dt in range(NDT):
                def dtpmm(e, dt=dt):
                    DTIv = DTI.rearrange("p (a t) -> p a t", a=2)
                    for hf in range(2):
                        ins = e.matmul(PSC[:, hf*512:(hf+1)*512], cb_dtw[:, dt*128:(dt+1)*128],
                                       DTIv[:, hf, :], start=True, stop=True) if e else None
                        inc(ins, "T")
                    mark(f"dtp_mm:{gw}:{dt}", "T")
                tw2 = [w("LG", f"dti:{gw}")]
                if dt == 0:
                    if gw > 0:
                        tw2.append(w("S", f"a_trev:{gw-1}:15"))
                else:
                    tw2.append(w("V", f"delta:{gw}:0"))
                step("te", tw2, dtpmm)

                def sp_s(e, dt=dt):
                    if e:
                        e.activation(SP1, PSC[:, 0:1024], ACT.Abs, bias=c_dtb[:, dt:dt+1], scale=1.0)
                        e.activation(SP2, SP1, ACT.Exp, bias=0.0, scale=-1.0)
                        ins = e.activation(SP1, SP2, ACT.Ln, bias=1.0, scale=1.0)
                    else:
                        ins = None
                    inc(ins, "S", 1)
                    mark(f"sp_ln:{gw}:{dt}", "S")
                step("sc", [w("T", f"dtp_mm:{gw}:{dt}")], sp_s)

                def sp_v(e, dt=dt):
                    if e:
                        e.tensor_scalar(SP2, PSC[:, 0:1024], c_dtb[:, dt:dt+1], 0.0, ALU.add, ALU.max)
                        ins = e.tensor_add(DLTd(dt), SP1, SP2)
                    else:
                        ins = None
                    inc(ins, "V", 1)
                    mark(f"delta:{gw}:{dt}", "V")
                step("ve", [w("S", f"sp_ln:{gw}:{dt}")], sp_v)

                def u_v(e, dt=dt):
                    inc(e.tensor_mul(UUd(dt), DLTd(dt), XCd(dt)) if e else None, "V")
                    mark(f"u:{gw}:{dt}", "V")
                step("ve", [w("V", f"delta:{gw}:{dt}")], u_v)

            if dbg:
                def dmp_dl(e):
                    inc(e.dma_start(out=o_ddl[gw], in_=ARA[:, 6156:8204]) if e else None, "SO")
                step("sy", [w("V", f"delta:{gw}:1")], dmp_dl)

            if dbg:
                def dmp_pre(e):
                    inc(e.dma_start(out=o_ds3[gw], in_=STATE[:, 0:64]) if e else None, "SO")
                    mark(f"dmp_pre:{gw}", "SO")
                step("sy", [w("V", f"delta:{gw}:1")], dmp_pre)

            for n in range(NN):
                pg = n % 2

                def bc_n(e, n=n, pg=pg):
                    if e:
                        i1 = e.dma_start(out=BRO[pg],
                                         in_=cc_b_out[128+n:129+n, :].squeeze(0).partition_broadcast(P))
                        i2 = e.dma_start(out=CRO[pg],
                                         in_=cc_b_out[144+n:145+n, :].squeeze(0).partition_broadcast(P))
                        i1.then_inc(sems["LS"], 16)
                        i2.then_inc(sems["LS"], 16)
                    cnt["LS"] += 32
                    mark(f"bc:{gw}:{n}", "LS")
                wv = [w("CC", f"ar_b:{gw}")]
                if n >= 2:
                    wv.append(w("V", f"hc:{gw}:1:{n-2}"))
                step("sc", wv, bc_n)

                for dt in range(NDT):
                    col = (beta * 2 + dt) * 16 + n

                    def aexp(e, n=n, dt=dt, col=col):
                        inc(e.activation(AT[dt], DLTd(dt), ACT.Exp, bias=0.0,
                                         scale=c_A[:, dt*16+n:dt*16+n+1]) if e else None, "S")
                        mark(f"aexp:{gw}:{dt}:{n}", "S")
                    wv2 = [w("V", f"delta:{gw}:{dt}")]
                    if n >= 1:
                        wv2.append(w("V", f"scan:{gw}:{dt}:{n-1}"))
                    if dbg and n == 1:
                        wv2.append(w("SO", f"dmp_h0:{gw}"))
                    step("sc", wv2, aexp)

                    def scanv(e, n=n, dt=dt, pg=pg, col=col):
                        if e:
                            e.tensor_mul(BT[dt], UUd(dt), BRO[pg])
                            e.tensor_tensor_scan(HT[dt], AT[dt], BT[dt],
                                                 STATE[:, col:col+1], ALU.mult, ALU.add)
                            sc_ins = e.tensor_copy(STATE[:, col:col+1], HT[dt][:, 1023:1024])
                        else:
                            sc_ins = None
                        inc(sc_ins, "V", 1)
                        mark(f"scan:{gw}:{dt}:{n}", "V")
                        hc_ins = e.tensor_mul(HCT[dt], HT[dt], CRO[pg]) if e else None
                        inc(hc_ins, "V", 1)
                        mark(f"hc:{gw}:{dt}:{n}", "V")
                    wv3 = [w("S", f"aexp:{gw}:{dt}:{n}"), w("LS", f"bc:{gw}:{n}"),
                           w("V", f"u:{gw}:{dt}")]
                    if n >= 1:
                        wv3.append(w("T", f"yacc:{gw}:{dt}:{n-1}"))
                    if dbg and n == 0 and dt == 0:
                        wv3.append(w("SO", f"dmp_pre:{gw}"))
                    step("ve", wv3, scanv)

                    def yacc(e, n=n, dt=dt):
                        PSY = PSD if dt == 0 else PSC
                        for hf in range(2):
                            ins = e.matmul(PSY[:, hf*512:(hf+1)*512], c_idb,
                                           HCT[dt][:, hf*512:(hf+1)*512],
                                           start=(n == 0), stop=(n == NN-1)) if e else None
                            inc(ins, "T")
                        mark(f"yacc:{gw}:{dt}:{n}", "T")
                    wv4 = [w("V", f"hc:{gw}:{dt}:{n}")]
                    if n == 0:
                        if dt == 0:
                            if gw > 0:
                                wv4.append(w("V", f"yev:{gw-1}:0"))
                        else:
                            wv4.append(w("V", f"delta:{gw}:1"))
                    step("te", wv4, yacc)

            if dbg:
                def dmp_h0(e):
                    inc(e.dma_start(out=o_dh0[gw][:, 0:2048], in_=ARB[:, 4096:6144]) if e else None, "SO")
                    inc(e.dma_start(out=o_dh0[gw][:, 2048:4096], in_=ARB[:, 0:2048]) if e else None, "SO")
                    inc(e.dma_start(out=o_dh0[gw][:, 4096:6144], in_=ARB[:, 2048:4096]) if e else None, "SO")
                    mark(f"dmp_h0:{gw}", "SO")
                step("sy", [w("V", f"hc:{gw}:1:0")], dmp_h0)

                def dmp_ab(e):
                    inc(e.dma_start(out=o_dab[gw][:, 0:2048], in_=ARB[:, 0:2048]) if e else None, "SO")
                    inc(e.dma_start(out=o_dab[gw][:, 2048:4096], in_=ARB[:, 2048:4096]) if e else None, "SO")
                    inc(e.dma_start(out=o_dhc[gw][:, 0:2048], in_=ARB[:, 6144:8192]) if e else None, "SO")
                    inc(e.dma_start(out=o_dhc[gw][:, 2048:2112], in_=STATE[:, 0:64]) if e else None, "SO")
                step("sy", [w("T", f"yacc:{gw}:1:{NN-1}")], dmp_ab)

            for dt in range(NDT):
                def yev(e, dt=dt):
                    PSY = PSD if dt == 0 else PSC
                    ins = e.scalar_tensor_tensor(YYd(dt), XHATd(dt)[:, 3:1027],
                                                 c_dp[:, dt:dt+1], PSY[:, 0:1024],
                                                 ALU.mult, ALU.add) if e else None
                    inc(ins, "V", 1)
                    mark(f"yev:{gw}:{dt}", "V")
                step("ve", [w("T", f"yacc:{gw}:{dt}:{NN-1}")], yev)

            if dbg:
                def dmp_yy(e):
                    inc(e.dma_start(out=o_dyy[gw], in_=YY) if e else None, "SO")
                    mark(f"dmp_yy:{gw}", "SO")
                step("sy", [w("V", f"yev:{gw}:1")], dmp_yy)

            if dbg:
                def dmp_s1(e):
                    inc(e.dma_start(out=o_ds2[gw][0], in_=STATE[:, 0:64]) if e else None, "SO")
                step("sy", [w("V", f"yev:{gw}:1")], dmp_s1)

            def yyb_c(e):
                inc(e.tensor_copy(YYB, YY) if e else None, "V")
                mark(f"yyb:{gw}", "V")
            step("ve", [w("V", f"yev:{gw}:1")], yyb_c)

            def ypack(e):
                inc(e.dma_start(out=cc_y_in.rearrange("(a p) t -> p a t", p=P),
                                in_=YYB.rearrange("p (a t) -> p a t", a=2)) if e else None, "LD")
                mark(f"ypack:{gw}", "LD")
            step("sy", [w("V", f"yyb:{gw}")], ypack)

            def ag_y(e):
                ins = e.collective_compute("AllGather", ALU.bypass,
                                           replica_groups=[list(range(8))],
                                           ins=[cc_y_in[:]], outs=[cc_y_out[:]]) if e else None
                inc(ins, "CC")
                mark(f"ag_y:{gw}", "CC")
            step("gp", [w("LD", f"ypack:{gw}")], ag_y)
            if gw == 0:
                hooks["agw2"]()

            def yread(e):
                if e:
                    rank = e.partition_id()
                    src = cc_y_out[:, :, bass.ts(rank, P)]
                    src = src.rearrange("r (a p) t -> p r a t", p=P)
                    dst = YASB.rearrange("p (r a t) -> p r a t", r=8, a=2)
                    ins = e.dma_start(out=dst, in_=src)
                else:
                    ins = None
                inc(ins, "LD")
                mark(f"yread:{gw}", "LD")
                inc(e.dma_start(out=XPT.rearrange("p (k t) -> p k t", k=16),
                                in_=i_xpt[gw]) if e else None, "LD")
                mark(f"xptld:{gw}", "LD")
            step("sy", [w("CC", f"ag_y:{gw}")], yread)

            def a_x2dt(e):
                inc(e.tensor_add(A_X2DT, XPT, YASB) if e else None, "V")
                mark(f"a_x2dt:{gw}", "V")
            a2w = [w("LD", f"xptld:{gw}"), w("V", f"yyb:{gw}")]
            if dbg:
                a2w.append(w("SO", f"dmp_yy:{gw}"))
            step("ve", a2w, a_x2dt)

            for k in range(16):
                def a_tr(e, k=k):
                    pg = k % 2
                    ins = e.transpose(PSC[:, pg*512:pg*512+128],
                                      A_X2DT[:, k*128:(k+1)*128], c_id) if e else None
                    inc(ins, "T")
                    mark(f"a_tr:{gw}:{k}", "T")
                wv = [w("V", f"a_x2dt:{gw}")]
                if k == 0:
                    wv.append(w("V", f"yev:{gw}:1"))
                if k >= 2:
                    wv.append(w("S", f"a_trev:{gw}:{k-2}"))
                step("te", wv, a_tr)

                def a_trev(e, k=k):
                    pg = k % 2
                    inc(e.copy(ARA[:, k*128:(k+1)*128], PSC[:, pg*512:pg*512+128]) if e else None, "S")
                    mark(f"a_trev:{gw}:{k}", "S")
                step("sc", [w("T", f"a_tr:{gw}:{k}")], a_trev)

            def x2st(e):
                inc(e.dma_start(out=x2_scr[gw], in_=A_X2) if e else None, "LD")
                if dbg:
                    inc(e.dma_start(out=o_x2[gw], in_=A_X2) if e else None, "SO")
                mark(f"x2st:{gw}", "LD")
            step("sy", [w("S", f"a_trev:{gw}:15")], x2st)

            def a_sq(e):
                inc(e.activation(A_SQ, A_X2, ACT.Square, bias=0.0, scale=1.0,
                                 accum_out=SCL[:, 48:49]) if e else None, "S")
                mark(f"a_sq:{gw}", "S")
            step("sc", [w("S", f"a_trev:{gw}:15")], a_sq)

            def a_v(e):
                if e:
                    e.reduce_sum(SCL[:, 49:50], A_X2, axis=AX.X)
                    e.reduce_max(SCL[:, 50:51], A_X2, axis=AX.X)
                    ins = e.tensor_reduce(SCL[:, 51:52], A_X2, AX.X, ALU.min)
                else:
                    ins = None
                inc(ins, "V", 1)
                mark(f"a_v:{gw}", "V")
            avw = [w("S", f"a_sq:{gw}")]
            if dbg and gw > 0:
                avw.append(w("SO", f"dmp_scl:{gw-1}"))
            step("ve", avw, a_v)

            def a_b1w(e):
                for cc_ in range(4):
                    inc(e.dma_start(out=qs_a[cc_].unsqueeze(1), in_=SCL[:, 48+cc_:49+cc_]) if e else None, "LD")
                mark(f"a_b1w:{gw}", "LD")
            ab1w = [w("V", f"a_v:{gw}")]
            if dbg and gw > 0:
                ab1w.append(w("SO", f"dmp_aq:{gw-1}"))
            step("sy", ab1w, a_b1w)

            def a_b1(e):
                for cc_ in range(4):
                    inc(e.dma_start(out=aq(cc_), in_=qs_a[cc_].unsqueeze(0)) if e else None, "LD")
                mark(f"a_b1:{gw}", "LD")
            step("sy", [w("LD", f"a_b1w:{gw}")], a_b1)

            def a_rm1(e):
                if e:
                    e.tensor_scalar_mul(aq(4), aq(1), 1.0 / 2048.0)
                    e.tensor_scalar_mul(aq(5), aq(0), 1.0 / 2048.0)
                    e.tensor_mul(aq(6), aq(4), aq(4))
                    ins = e.tensor_sub(aq(5), aq(5), aq(6))
                else:
                    ins = None
                inc(ins, "V", 1)
                mark(f"a_rm1:{gw}", "V")
            step("ve", [w("LD", f"a_b1:{gw}")], a_rm1)

            def a_rs1(e):
                if e:
                    e.activation(aq(6), aq(5), ACT.Ln, bias=EPS1, scale=1.0)
                    ins = e.activation(aq(7), aq(6), ACT.Exp, bias=0.0, scale=-0.5)
                else:
                    ins = None
                inc(ins, "S", 1)
                mark(f"a_rs1:{gw}", "S")
            step("sc", [w("V", f"a_rm1:{gw}")], a_rs1)

            def a_rm2(e):
                if e:
                    e.tensor_mul(aq(8), aq(7), aq(7))
                    ins = e.tensor_mul(aq(9), aq(5), aq(8))
                else:
                    ins = None
                inc(ins, "V", 1)
                mark(f"a_rm2:{gw}", "V")
            step("ve", [w("S", f"a_rs1:{gw}")], a_rm2)

            def a_rs2(e):
                if e:
                    e.activation(aq(10), aq(9), ACT.Ln, bias=EPS2, scale=1.0)
                    ins = e.activation(aq(11), aq(10), ACT.Exp, bias=0.0, scale=-0.5)
                else:
                    ins = None
                inc(ins, "S", 1)
                mark(f"a_rs2:{gw}", "S")
            step("sc", [w("V", f"a_rm2:{gw}")], a_rs2)

            def a_rm3(e):
                if e:
                    e.tensor_mul(aq(12), aq(7), aq(11))
                    e.tensor_sub(aq(13), aq(2), aq(4))
                    e.tensor_sub(aq(14), aq(4), aq(3))
                    e.tensor_max(aq(15), aq(13), aq(14))
                    ins = e.tensor_mul(aq(13), aq(15), aq(12))
                else:
                    ins = None
                inc(ins, "V", 1)
                mark(f"a_rm3:{gw}", "V")
            step("ve", [w("S", f"a_rs2:{gw}")], a_rm3)

            def a_rs3(e):
                if e:
                    e.activation(aq(14), aq(13), ACT.Ln, bias=0.0, scale=1.0)
                    ins = e.activation(aq(15), aq(14), ACT.Exp, bias=0.0, scale=-1.0)
                else:
                    ins = None
                inc(ins, "S", 1)
                mark(f"a_rs3:{gw}", "S")
            step("sc", [w("V", f"a_rm3:{gw}")], a_rs3)

            def a_rm4(e):
                if e:
                    e.tensor_mul(aq(14), aq(12), aq(15))
                    e.tensor_scalar(aq(15), aq(13), SG[0:1, 4:5], None, ALU.mult)
                    e.tensor_scalar_mul(aq(13), aq(14), 128.0)
                    e.tensor_mul(aq(12), aq(4), aq(13))
                    ins = e.tensor_scalar_mul(aq(14), aq(12), -1.0)
                else:
                    ins = None
                inc(ins, "V", 1)
                mark(f"a_rm4:{gw}", "V")
            step("ve", [w("S", f"a_rs3:{gw}")], a_rm4)

            if dbg:
                def dmp_aq(e):
                    inc(e.dma_start(out=o_daq[gw], in_=STT2[0:1, 4096:6144]) if e else None, "SO")
                    mark(f"dmp_aq:{gw}", "SO")
                step("sy", [w("V", f"a_rm4:{gw}")], dmp_aq)

            def a_b2w(e):
                for i_ in range(3):
                    inc(e.dma_start(out=qs_b[i_].unsqueeze(0), in_=aq(13+i_)) if e else None, "LD")
                mark(f"a_b2w:{gw}", "LD")
            step("sy", [w("V", f"a_rm4:{gw}")], a_b2w)

            def a_b2(e):
                for i_, dst in ((0, scl(0, gw)), (1, scl(8, gw)), (2, scl(16, gw))):
                    inc(e.dma_start(out=dst, in_=qs_b[i_].unsqueeze(1)) if e else None, "LD")
                mark(f"a_scl:{gw}", "LD")
            step("sy", [w("LD", f"a_b2w:{gw}")], a_b2)

            def a_q2(e):
                if e:
                    e.tensor_scalar(A_XQF, A_X2, scl(0, gw), scl(8, gw), ALU.mult, ALU.add)
                    e.tensor_scalar(A_SQ, A_XQF, CLIP_HI, CLIP_LO, ALU.min, ALU.max)
                    ins = e.tensor_scalar(A_XQB, A_SQ, scl(16, gw), None, ALU.mult)
                else:
                    ins = None
                inc(ins, "V", 1)
                mark(f"a_q2:{gw}", "V")
            q2w = [w("LD", f"a_scl:{gw}"), w("LD", f"x2st:{gw}")]
            if dbg and gw > 0:
                q2w.append(w("SO", f"dmp_xq:{gw-1}"))
            step("ve", q2w, a_q2)

            if dbg:
                def dmp_s2(e):
                    inc(e.dma_start(out=o_ds2[gw][1], in_=GRD2[:, 0:64]) if e else None, "SO")
                step("sy", [w("V", f"a_q2:{gw}")], dmp_s2)

            def xqt(e):
                out_v = XQS[:, gw*2048:(gw+1)*2048].rearrange("p (k t) -> p k t", k=16)
                inc(e.dma_start_transpose(out_v, A_XQB) if e else None, "LS")
                mark(f"xqt:{gw}", "LS")
            step("sc", [w("V", f"a_q2:{gw}")], xqt)
            if dbg:
                def dmp_scl(e):
                    inc(e.dma_start(out=o_dscl[gw], in_=SCL[:, 0:64]) if e else None, "SO")
                    mark(f"dmp_scl:{gw}", "SO")
                step("sy", [w("LD", f"a_scl:{gw}")], dmp_scl)
                def dmp_xq(e):
                    inc(e.dma_start(out=o_dxq[gw], in_=A_XQB) if e else None, "SO")
                    mark(f"dmp_xq:{gw}", "SO")
                step("sy", [w("V", f"a_q2:{gw}")], dmp_xq)

        def ffn(H):
            for o in range(8):
                def w1ld(e, o=o):
                    src = cc_w1_out[o].rearrange("k p m -> p k m")
                    dst = WS[o % 2].rearrange("p (k m) -> p k m", k=16)
                    inc(e.dma_start(out=dst, in_=src) if e else None, "LG")
                    mark(f"f_w1:{H}:{o}", "LG")
                wv = [w("CC", "ag_w1")]
                if o < 2:
                    wv.append(w("V", f"yev:{4*H+3}:1"))
                    wv.append(w("V", f"a_q2:{4*H+3}"))
                else:
                    wv.append(w("S", f"f_sq:{H}:{o-2}:3"))
                step("gp", wv, w1ld)

                for s in range(4):
                    idx = o * 4 + s

                    def mm1(e, o=o, s=s, idx=idx):
                        PS = PSA if idx % 2 == 0 else PSB
                        wsv = WS[o % 2]
                        for k in range(16):
                            lhsT = XQS[:, ((4*H+s)*16+k)*128:((4*H+s)*16+k+1)*128]
                            for hf in range(2):
                                ins = e.matmul(PS[:, hf*512:(hf+1)*512], lhsT,
                                               wsv[:, k*1024+hf*512:k*1024+(hf+1)*512],
                                               start=(k == 0), stop=(k == 15)) if e else None
                                inc(ins, "T")
                        mark(f"f_mm1:{H}:{o}:{s}", "T")
                    wv = [w("LG", f"f_w1:{H}:{o}"), w("LS", f"xqt:{4*H+s}")]
                    if idx >= 2:
                        wv.append(w("S", f"f_sq:{H}:{(idx-2)//4}:{(idx-2)%4}"))
                    else:
                        wv.append(w("S", f"pj_evac:{4*H+3}"))
                    step("te", wv, mm1)

                    def gelu_s(e, o=o, s=s, idx=idx):
                        PS = PSA if idx % 2 == 0 else PSB
                        inc(e.activation(HBc(s, o), PS[:, 0:1024], ACT.Gelu, bias=0.0, scale=1.0,
                                         accum_out=GST[:, s*32+o:s*32+o+1]) if e else None, "S")
                        mark(f"f_gelu:{H}:{o}:{s}", "S")
                        inc(e.activation(F_A1, HBc(s, o), ACT.Square, bias=0.0, scale=1.0,
                                         accum_out=GST[:, s*32+8+o:s*32+8+o+1]) if e else None, "S")
                        mark(f"f_sq:{H}:{o}:{s}", "S")
                    step("sc", [w("T", f"f_mm1:{H}:{o}:{s}")], gelu_s)

                    def mmx(e, o=o, s=s):
                        if e:
                            e.reduce_max(GST[:, s*32+16+o:s*32+17+o], HBc(s, o), axis=AX.X)
                            ins = e.tensor_reduce(GST[:, s*32+24+o:s*32+25+o], HBc(s, o),
                                                  AX.X, ALU.min)
                        else:
                            ins = None
                        inc(ins, "V", 1)
                        mark(f"f_mmx:{H}:{o}:{s}", "V")
                    step("ve", [w("S", f"f_gelu:{H}:{o}:{s}")], mmx)

            for s in range(4):
                def f_hv(e, s=s):
                    if e:
                        e.reduce_sum(SCL[:, 48:49], GST[:, s*32:s*32+8], axis=AX.X)
                        e.reduce_sum(SCL[:, 49:50], GST[:, s*32+8:s*32+16], axis=AX.X)
                        e.reduce_max(SCL[:, 50:51], GST[:, s*32+16:s*32+24], axis=AX.X)
                        ins = e.tensor_reduce(SCL[:, 51:52], GST[:, s*32+24:s*32+32], AX.X, ALU.min)
                    else:
                        ins = None
                    inc(ins, "V", 1)
                    mark(f"f_hv:{H}:{s}", "V")
                step("ve", [w("V", f"f_mmx:{H}:7:{s}"), w("S", f"f_sq:{H}:7:{s}")], f_hv)

                if dbg:
                    def dmp_fh1(e, s=s):
                        inc(e.dma_start(out=o_fh1[4*H+s], in_=HBs(s)) if e else None, "SO")
                        mark(f"dmp_fh1:{H}:{s}", "SO")
                    step("sy", [w("S", f"f_sq:{H}:7:{s}"), w("V", f"f_mmx:{H}:7:{s}")], dmp_fh1)
                    def dmp_fgs(e, s=s):
                        inc(e.dma_start(out=o_fgs[4*H+s], in_=GST[:, s*32:(s+1)*32]) if e else None, "SO")
                    step("sy", [w("V", f"f_hv:{H}:{s}")], dmp_fgs)

                def f_hb1w(e, s=s):
                    for cc_ in range(4):
                        inc(e.dma_start(out=qs_a[cc_].unsqueeze(1), in_=SCL[:, 48+cc_:49+cc_]) if e else None, "LD")
                    mark(f"f_hb1w:{H}:{s}", "LD")
                step("sy", [w("V", f"f_hv:{H}:{s}")], f_hb1w)

                def f_hb1(e, s=s):
                    for cc_ in range(4):
                        inc(e.dma_start(out=aq(cc_), in_=qs_a[cc_].unsqueeze(0)) if e else None, "LD")
                    mark(f"f_hb1:{H}:{s}", "LD")
                step("sy", [w("LD", f"f_hb1w:{H}:{s}")], f_hb1)

                def f_hrm1(e, s=s):
                    if e:
                        e.tensor_scalar_mul(aq(4), aq(0), 1.0 / 8192.0)
                        e.tensor_scalar_mul(aq(5), aq(1), 1.0 / 8192.0)
                        e.tensor_mul(aq(6), aq(4), aq(4))
                        ins = e.tensor_sub(aq(5), aq(5), aq(6))
                    else:
                        ins = None
                    inc(ins, "V", 1)
                    mark(f"f_hrm1:{H}:{s}", "V")
                step("ve", [w("LD", f"f_hb1:{H}:{s}")], f_hrm1)

                def f_hrs1(e, s=s):
                    if e:
                        e.activation(aq(6), aq(5), ACT.Ln, bias=EPS2, scale=1.0)
                        ins = e.activation(aq(7), aq(6), ACT.Exp, bias=0.0, scale=-0.5)
                    else:
                        ins = None
                    inc(ins, "S", 1)
                    mark(f"f_hrs1:{H}:{s}", "S")
                step("sc", [w("V", f"f_hrm1:{H}:{s}")], f_hrs1)

                def f_hrm2(e, s=s):
                    if e:
                        e.tensor_sub(aq(8), aq(2), aq(4))
                        e.tensor_sub(aq(9), aq(4), aq(3))
                        e.tensor_max(aq(10), aq(8), aq(9))
                        ins = e.tensor_mul(aq(13), aq(10), aq(7))
                    else:
                        ins = None
                    inc(ins, "V", 1)
                    mark(f"f_hrm2:{H}:{s}", "V")
                step("ve", [w("S", f"f_hrs1:{H}:{s}")], f_hrm2)

                def f_hrs2(e, s=s):
                    if e:
                        e.activation(aq(11), aq(13), ACT.Ln, bias=0.0, scale=1.0)
                        ins = e.activation(aq(12), aq(11), ACT.Exp, bias=0.0, scale=-1.0)
                    else:
                        ins = None
                    inc(ins, "S", 1)
                    mark(f"f_hrs2:{H}:{s}", "S")
                step("sc", [w("V", f"f_hrm2:{H}:{s}")], f_hrs2)

                def f_hrm3(e, s=s):
                    if e:
                        e.tensor_mul(aq(14), aq(7), aq(12))
                        e.tensor_scalar(aq(15), aq(13), SG[0:1, 5:6], None, ALU.mult)
                        e.tensor_scalar_mul(aq(13), aq(14), 128.0)
                        e.tensor_mul(aq(12), aq(4), aq(13))
                        ins = e.tensor_scalar_mul(aq(14), aq(12), -1.0)
                    else:
                        ins = None
                    inc(ins, "V", 1)
                    mark(f"f_hrm3:{H}:{s}", "V")
                step("ve", [w("S", f"f_hrs2:{H}:{s}")], f_hrm3)

                def f_hb2w(e, s=s):
                    for i_ in range(3):
                        inc(e.dma_start(out=qs_b[i_].unsqueeze(0), in_=aq(13+i_)) if e else None, "LD")
                    mark(f"f_hb2w:{H}:{s}", "LD")
                step("sy", [w("V", f"f_hrm3:{H}:{s}")], f_hb2w)

                def f_hb2(e, s=s):
                    for i_, dst in ((0, scl(24, s)), (1, scl(32, s)), (2, scl(40, s))):
                        inc(e.dma_start(out=dst, in_=qs_b[i_].unsqueeze(1)) if e else None, "LD")
                    mark(f"f_hscl:{H}:{s}", "LD")
                step("sy", [w("LD", f"f_hb2w:{H}:{s}")], f_hb2)

                for chk in range(4):
                    def f_hq(e, s=s, chk=chk):
                        hv = ARAb[:, s*8192+chk*2048:s*8192+(chk+1)*2048]
                        if e:
                            e.tensor_scalar(F_A, hv, scl(24, s), scl(32, s), ALU.mult, ALU.add)
                            e.tensor_scalar(F_B, F_A, CLIP_HI, CLIP_LO, ALU.min, ALU.max)
                            ins = e.tensor_scalar(hv, F_B, scl(40, s), None, ALU.mult)
                        else:
                            ins = None
                        inc(ins, "V", 1)
                        mark(f"f_hq:{H}:{s}:{chk}", "V")
                    fhqw = [w("LD", f"f_hscl:{H}:{s}")]
                    if dbg and chk == 0:
                        fhqw.append(w("SO", f"dmp_fh1:{H}:{s}"))
                    step("ve", fhqw, f_hq)

            for s in range(4):
                if dbg:
                    def dmp_fh2(e, s=s):
                        inc(e.dma_start(out=o_fh2[4*H+s], in_=HBs(s)) if e else None, "SO")
                        mark(f"dmp_fh2:{H}:{s}", "SO")
                    step("sy", [w("V", f"f_hq:{H}:{s}:3")], dmp_fh2)

                def f_ht(e, s=s):
                    out_v = ARBb[:, s*8192:(s+1)*8192].rearrange("p (c t) -> p c t", c=64)
                    inc(e.dma_start_transpose(out_v, ARAb[:, s*8192:(s+1)*8192])
                        if e else None, "LS")
                    mark(f"f_ht:{H}:{s}", "LS")
                step("sc", [w("V", f"f_hq:{H}:{s}:3"), w("T", f"f_mm1:{H}:7:3")], f_ht)

            PSQ = [PSA[:, 0:512], PSA[:, 512:1024], PSB[:, 0:512], PSB[:, 512:1024]]
            for dc in range(4):
                for kh in range(2):
                    pg = (dc * 2 + kh) % 2

                    def w2ld(e, dc=dc, kh=kh, pg=pg):
                        for rr in range(2):
                            r = 2 * dc + rr
                            src = cc_w2_out[r, kh*32:(kh+1)*32].rearrange("kk p m -> p kk m")
                            dst = WS2[pg].rearrange("p (kk m) -> p kk m", kk=32)[:, :, rr*256:(rr+1)*256]
                            inc(e.dma_start(out=dst, in_=src) if e else None, "LG")
                        mark(f"f_w2:{H}:{dc}:{kh}", "LG")
                    wv = [w("CC", "ag_w2"), w("LS", f"f_ht:{H}:3")]
                    if dbg and dc == 0 and kh == 0:
                        wv.append(w("SO", f"dmp_fh2:{H}:3"))
                    if dc * 2 + kh >= 2:
                        prev = dc * 2 + kh - 2
                        wv.append(w("T", f"f_mm2:{H}:{prev//2}:{prev%2}:3"))
                    step("gp", wv, w2ld)

                    for s in range(4):
                        def mm2(e, dc=dc, kh=kh, s=s, pg=pg):
                            for kk in range(32):
                                lhsT = ARBb[:, s*8192+(kh*32+kk)*128:s*8192+(kh*32+kk+1)*128]
                                ins = e.matmul(PSQ[s], lhsT,
                                               WS2[pg][:, kk*512:(kk+1)*512],
                                               start=(kh == 0 and kk == 0),
                                               stop=(kh == 1 and kk == 31)) if e else None
                                inc(ins, "T")
                            mark(f"f_mm2:{H}:{dc}:{kh}:{s}", "T")
                        wv = [w("LG", f"f_w2:{H}:{dc}:{kh}")]
                        if dc == 0 and kh == 0:
                            wv.append(w("S", f"f_gelu:{H}:7:3"))
                        if dc > 0 and kh == 0:
                            wv.append(w("V", f"f_oc:{H}:{dc-1}:{s}"))
                        step("te", wv, mm2)

                for s in range(4):
                    pg2 = (dc * 4 + s) % 2

                    def x2c_ld(e, dc=dc, s=s, pg2=pg2):
                        inc(e.dma_start(out=X2C[:, pg2*512:(pg2+1)*512],
                                        in_=x2_scr[4*H+s][:, dc*512:(dc+1)*512]) if e else None, "LD")
                        mark(f"f_x2c:{H}:{dc}:{s}", "LD")
                    wv = [w("LD", f"x2st:{4*H+s}"), w("V", f"f_hq:{H}:3:3")]
                    if dc * 4 + s >= 2:
                        prev = dc * 4 + s - 2
                        wv.append(w("V", f"f_oc:{H}:{prev//4}:{prev%4}"))
                    step("sy", wv, x2c_ld)

                    def f_oc(e, dc=dc, s=s, pg2=pg2):
                        ins = e.tensor_add(OUTP[:, pg2*512:(pg2+1)*512], PSQ[s],
                                           X2C[:, pg2*512:(pg2+1)*512]) if e else None
                        inc(ins, "V", 1)
                        mark(f"f_oc:{H}:{dc}:{s}", "V")
                    ocw = [w("T", f"f_mm2:{H}:{dc}:1:{s}"), w("LD", f"f_x2c:{H}:{dc}:{s}")]
                    if dc * 4 + s >= 2:
                        prev = dc * 4 + s - 2
                        ocw.append(w("SO", f"f_ost:{H}:{prev//4}:{prev%4}"))
                    step("ve", ocw, f_oc)

                    def f_ost(e, dc=dc, s=s, pg2=pg2):
                        inc(e.dma_start(out=o_out[4*H+s][:, dc*512:(dc+1)*512],
                                        in_=OUTP[:, pg2*512:(pg2+1)*512]) if e else None, "SO")
                        mark(f"f_ost:{H}:{dc}:{s}", "SO")
                    step("sy", [w("V", f"f_oc:{H}:{dc}:{s}")], f_ost)

        p0()
        for gw in range(4):
            wave(gw)
        if not noffn:
            ffn(0)
        for gw in range(4, 8):
            wave(gw)
        if not noffn:
            ffn(1)

        def fin(e):
            mark("final_out", "SO")
        step("sy", [w("SO", "f_ost:1:3:3") if not noffn else w("LD", "x2st:7")], fin)

        for eng, waits, fn in steps:
            fn(None)
        for eng, waits, fn in steps:
            for (sem, key, d) in waits:
                assert key in M, f"wait on undefined mark {key}"
                assert M[key][0] == sem, f"wait sem mismatch: {key} marked on {M[key][0]}, waited on {sem}"
        mode["dry"] = False

        def run_engine(e, name):
            for k in cnt:
                pass
            for eng, waits, fn in steps:
                if eng != name:
                    continue
                for (sem, key, d) in waits:
                    e.wait_ge(sems[sem], M[key] + d)
                fn(e)

        emitted = {"done": False}

        def run_all(engines):
            for k in cnt:
                cnt[k] = 0
            for eng, waits, fn in steps:
                e = engines.get(eng)
                if e is not None:
                    for (sem, key, d) in waits:
                        msem, mval = M[key]
                        assert msem == sem, (key, msem, sem)
                        e.wait_ge(sems[sem], mval + d)
                fn(e)

        @blk.sync
        def _(eng):
            run_all({"sy": eng})

        @blk.scalar
        def _(eng):
            run_all({"sc": eng})

        @blk.vector
        def _(eng):
            run_all({"ve": eng})

        @blk.tensor
        def _(eng):
            run_all({"te": eng})

        @blk.gpsimd
        def _(eng):
            run_all({"gp": eng})

    return nc


def make_inmaps(x, conv_w, conv_b, xproj_w, dtproj_w, dtproj_b, A_log, D_param,
                w_fc, w_proj):
    f = np.float32
    x = np.asarray(x, f)
    xp3 = np.concatenate([np.zeros((2, 3, D), f), x], axis=1)
    xpwT = np.ascontiguousarray(np.asarray(xproj_w, f).T)
    w_fcT = np.ascontiguousarray(np.asarray(w_fc, f).T)
    w_projT = np.ascontiguousarray(np.asarray(w_proj, f).T)
    ims = []
    for c in range(8):
        d0 = 256 * c
        xsc = np.empty((NW, P, NDT, 1027), f)
        xpt = np.empty((NW, P, 16, P), f)
        for gw in range(NW):
            beta, ww = gw % 2, gw // 2
            blkx = xp3[beta, 1024*ww:1024*ww+1027, d0:d0+256]
            xsc[gw] = blkx.T.reshape(NDT, P, 1027).transpose(1, 0, 2)
            stri = x[beta, 1024*ww+128*c:1024*ww+128*c+128, :]
            xpt[gw] = stri.T.reshape(16, P, P).transpose(1, 0, 2)
        ws1 = w_fcT[:, 1024*c:1024*(c+1)].reshape(16, P, 1024).transpose(1, 0, 2)
        ws2 = w_projT[:, 256*c:256*(c+1)].reshape(64, P, 256).transpose(1, 0, 2)
        ims.append({
            "xsc": np.ascontiguousarray(xsc),
            "xpt": np.ascontiguousarray(xpt),
            "ws1": np.ascontiguousarray(ws1),
            "ws2": np.ascontiguousarray(ws2),
            "xpw": np.ascontiguousarray(xpwT[d0:d0+256].reshape(NDT, P, 160).transpose(1, 0, 2)),
            "dtw": np.ascontiguousarray(
                np.asarray(dtproj_w, f)[d0:d0+256, :].T.reshape(P, NDT, P)),
            "dtb": np.ascontiguousarray(np.asarray(dtproj_b, f)[d0:d0+256].reshape(NDT, P).T),
            "alog": np.ascontiguousarray(np.asarray(A_log, f)[d0:d0+256].reshape(NDT, P, NN).transpose(1, 0, 2)),
            "cwv": np.ascontiguousarray(np.asarray(conv_w, f)[d0:d0+256, 0, :].reshape(NDT, P, 4).transpose(1, 0, 2)),
            "cbv": np.ascontiguousarray(np.asarray(conv_b, f)[d0:d0+256].reshape(NDT, P).T),
            "dpv": np.ascontiguousarray(np.asarray(D_param, f)[d0:d0+256].reshape(NDT, P).T),
            "onesf": np.ones((P, 1), f),
            "cvals": np.broadcast_to(np.array([1e-5, 1e-6, -1.0, -0.5, 0, 0, 0, 0], f), (P, 8)).copy(),
            "identf": np.eye(P, dtype=f),
        })
    return ims


def kernel(L0_attn_x, L0_attn_conv_w, L0_attn_conv_b, L0_attn_xproj_w,
           L0_attn_dtproj_w, L0_attn_dtproj_b, L0_attn_A_log, L0_attn_D_param,
           L0_ffn_w_fc, L0_ffn_w_proj, dbg=False, noffn=False):
    ims = make_inmaps(L0_attn_x, L0_attn_conv_w, L0_attn_conv_b, L0_attn_xproj_w,
                      L0_attn_dtproj_w, L0_attn_dtproj_b, L0_attn_A_log,
                      L0_attn_D_param, L0_ffn_w_fc, L0_ffn_w_proj)
    nc = build(dbg=dbg, noffn=noffn)
    res = run_bass_kernel_spmd(nc, ims, core_ids=list(range(8)))
    out = np.empty((2, 4096, D), np.float32)
    for c in range(8):
        r = res.results[c]
        for gw in range(NW):
            beta, ww = gw % 2, gw // 2
            out[beta, 1024*ww+128*c:1024*ww+128*c+128, :] = r["out"][gw]
    return out


# version 04
# speedup vs baseline: 1.0647x; 1.0647x over previous
import numpy as np
import ml_dtypes
import concourse.bass as bass
import concourse.mybir as mybir
from concourse.bass_utils import run_bass_kernel_spmd

F32 = mybir.dt.float32
BF16 = mybir.dt.bfloat16
I8 = mybir.dt.int8
ALU = mybir.AluOpType
ACT = mybir.ActivationFunctionType
AX = mybir.AxisListType

P = 128
NW = 8
TW = 1024
NDT = 2
NN = 16
D = 2048
M1 = 8192
Q_B = 128.0
CLIP_HI = float(np.float32(np.float32(128.0) - np.float32(1e-6)))
CLIP_LO = -CLIP_HI
EPS1 = float(np.float32(1e-5))
EPS2 = float(np.float32(1e-6))
GEPS = float(np.float32(1e-6))
CONST_VALS = [EPS1, EPS2, -1.0, -0.5]


def build(dbg=False, noffn=False):
    nc = bass.Bass("TRN2", target_bir_lowering=False)

    i_xsc = nc.dram_tensor("xsc", [NW, P, NDT, 1027], F32, kind="ExternalInput").ap()
    i_xpt = nc.dram_tensor("xpt", [NW, P, 16, P], F32, kind="ExternalInput").ap()
    i_ws1 = nc.dram_tensor("ws1", [P, 16, 1024], F32, kind="ExternalInput").ap()
    i_ws2 = nc.dram_tensor("ws2", [P, 64, 256], F32, kind="ExternalInput").ap()
    i_xpw = nc.dram_tensor("xpw", [P, NDT, 160], F32, kind="ExternalInput").ap()
    i_dtw = nc.dram_tensor("dtw", [P, NDT, P], F32, kind="ExternalInput").ap()
    i_dtb = nc.dram_tensor("dtb", [P, NDT], F32, kind="ExternalInput").ap()
    i_alog = nc.dram_tensor("alog", [P, NDT, NN], F32, kind="ExternalInput").ap()
    i_cw = nc.dram_tensor("cwv", [P, NDT, 4], F32, kind="ExternalInput").ap()
    i_cb = nc.dram_tensor("cbv", [P, NDT], F32, kind="ExternalInput").ap()
    i_dp = nc.dram_tensor("dpv", [P, NDT], F32, kind="ExternalInput").ap()
    i_ones = nc.dram_tensor("onesf", [P, 1], F32, kind="ExternalInput").ap()
    i_cvals = nc.dram_tensor("cvals", [P, 8], F32, kind="ExternalInput").ap()
    i_id = nc.dram_tensor("identf", [P, P], F32, kind="ExternalInput").ap()

    o_out = nc.dram_tensor("out", [NW, P, D], F32, kind="ExternalOutput").ap()
    o_x2 = nc.dram_tensor("dbg_x2", [NW, P, D], F32, kind="ExternalOutput").ap() if dbg else None
    o_dst = nc.dram_tensor("dbg_st", [NW, 1, 2048], F32, kind="ExternalOutput").ap() if dbg else None
    o_dxh = nc.dram_tensor("dbg_xh", [NW, P, 2054], F32, kind="ExternalOutput").ap() if dbg else None
    o_dxc = nc.dram_tensor("dbg_xc", [NW, P, 2048], F32, kind="ExternalOutput").ap() if dbg else None
    o_ddl = nc.dram_tensor("dbg_dl", [NW, P, 2048], F32, kind="ExternalOutput").ap() if dbg else None
    o_dyy = nc.dram_tensor("dbg_yy", [NW, P, 2048], F32, kind="ExternalOutput").ap() if dbg else None
    o_dpj = nc.dram_tensor("dbg_pj", [NW, P, 1024], F32, kind="ExternalOutput").ap() if dbg else None
    o_dbo = nc.dram_tensor("dbg_bo", [NW, 160, 1024], F32, kind="ExternalOutput").ap() if dbg else None
    o_dab = nc.dram_tensor("dbg_ab", [NW, P, 4096], F32, kind="ExternalOutput").ap() if dbg else None
    o_dhc = nc.dram_tensor("dbg_hc", [NW, P, 2048+64], F32, kind="ExternalOutput").ap() if dbg else None
    o_ds2 = nc.dram_tensor("dbg_s2", [NW, 2, P, 64], F32, kind="ExternalOutput").ap() if dbg else None
    o_dh0 = nc.dram_tensor("dbg_h0", [NW, P, 6144], F32, kind="ExternalOutput").ap() if dbg else None
    o_ds3 = nc.dram_tensor("dbg_s3", [NW, P, 64], F32, kind="ExternalOutput").ap() if dbg else None
    o_dxq = nc.dram_tensor("dbg_xq", [NW, P, 2048], BF16, kind="ExternalOutput").ap() if dbg else None
    o_fh1 = nc.dram_tensor("dbg_fh1", [NW, P, M1], BF16, kind="ExternalOutput").ap() if dbg else None
    o_fh2 = nc.dram_tensor("dbg_fh2", [NW, P, M1], BF16, kind="ExternalOutput").ap() if dbg else None
    o_fgs = nc.dram_tensor("dbg_fgs", [NW, P, 32], F32, kind="ExternalOutput").ap() if dbg else None
    o_dscl = nc.dram_tensor("dbg_scl", [NW, P, 64], F32, kind="ExternalOutput").ap() if dbg else None
    o_daq = nc.dram_tensor("dbg_aq", [NW, 1, 2048], F32, kind="ExternalOutput").ap() if dbg else None
    o_dw1 = nc.dram_tensor("dbg_w1", [8, 16, P, 1024], I8, kind="ExternalOutput").ap() if dbg else None
    o_dth = nc.dram_tensor("dbg_th", [P, 6], F32, kind="ExternalOutput").ap() if dbg else None

    cc_a_in = nc.dram_tensor("cc_a_in", [2, TW], F32, kind="Internal").ap()
    cc_a_out = nc.dram_tensor("cc_a_out", [2, TW], F32, kind="Internal").ap()
    cc_b_in = nc.dram_tensor("cc_b_in", [160, TW], F32, kind="Internal").ap()
    cc_b_out = nc.dram_tensor("cc_b_out", [160, TW], F32, kind="Internal").ap()
    cc_y_in = nc.dram_tensor("cc_y_in", [256, TW], BF16, kind="Internal").ap()
    cc_y_out = nc.dram_tensor("cc_y_out", [8, 256, TW], BF16, kind="Internal").ap()
    cc_g_in = nc.dram_tensor("cc_g_in", [1, 2], F32, kind="Internal").ap()
    cc_g_out = nc.dram_tensor("cc_g_out", [8, 1, 2], F32, kind="Internal").ap()
    cc_w1_in = nc.dram_tensor("cc_w1_in", [16, P, 1024], I8, kind="Internal").ap()
    cc_w1_out = nc.dram_tensor("cc_w1_out", [8, 16, P, 1024], I8, kind="Internal").ap()
    cc_w2_in = nc.dram_tensor("cc_w2_in", [64, P, 256], I8, kind="Internal").ap()
    cc_w2_out = nc.dram_tensor("cc_w2_out", [8, 64, P, 256], I8, kind="Internal").ap()
    st_scr = nc.dram_tensor("st_scr", [NW, 2, TW], F32, kind="Internal").ap()
    g_scr = nc.dram_tensor("g_scr", [1, 8], F32, kind="Internal").ap()
    x2_scr = nc.dram_tensor("x2_scr", [NW, P, D], F32, kind="Internal").ap()
    qs_a = nc.dram_tensor("qs_a", [4, P], F32, kind="Internal").ap()
    qs_b = nc.dram_tensor("qs_b", [3, P], F32, kind="Internal").ap()

    import contextlib
    with contextlib.ExitStack() as _st:
        STATE_t = _st.enter_context(nc.sbuf_tensor("STATE", [P, 64], F32))
        ARA_t = _st.enter_context(nc.sbuf_tensor("ARA", [P, 16384], F32))
        ARB_t = _st.enter_context(nc.sbuf_tensor("ARB", [P, 16384], F32))
        XQS_t = _st.enter_context(nc.sbuf_tensor("XQS", [P, 16384], BF16))
        FSC_t = _st.enter_context(nc.sbuf_tensor("FSC", [P, 4096], F32))
        GST_t = _st.enter_context(nc.sbuf_tensor("GST", [P, 128], F32))
        SCL_t = _st.enter_context(nc.sbuf_tensor("SCL", [P, 64], F32))
        GRD1_t = _st.enter_context(nc.sbuf_tensor("GRD1", [P, 64], F32))
        GRD2_t = _st.enter_context(nc.sbuf_tensor("GRD2", [P, 64], F32))
        CON_t = _st.enter_context(nc.sbuf_tensor("CON", [P, 256], F32))
        CONB_t = _st.enter_context(nc.sbuf_tensor("CONB", [P, 640], BF16))
        SG_t = _st.enter_context(nc.sbuf_tensor("SG", [1, 32], F32))
        STT2_t = _st.enter_context(nc.sbuf_tensor("STT2", [1, 6144], F32))
        PSA_t = _st.enter_context(nc.psum_tensor("PSA", [P, 1024], F32))
        PSB_t = _st.enter_context(nc.psum_tensor("PSB", [P, 1024], F32))
        PSC_t = _st.enter_context(nc.psum_tensor("PSC", [P, 1024], F32))
        PSD_t = _st.enter_context(nc.psum_tensor("PSD", [P, 1024], F32))
        h_LD = _st.enter_context(nc.semaphore("sLD"))
        h_LS = _st.enter_context(nc.semaphore("sLS"))
        h_LG = _st.enter_context(nc.semaphore("sLG"))
        h_CC = _st.enter_context(nc.semaphore("sCC"))
        h_V = _st.enter_context(nc.semaphore("sV"))
        h_S = _st.enter_context(nc.semaphore("sS"))
        h_T = _st.enter_context(nc.semaphore("sT"))
        h_SO = _st.enter_context(nc.semaphore("sSO"))
        blk = _st.enter_context(nc.Block())

        ARA = ARA_t[:]
        ARB = ARB_t[:]
        ARAb = ARA.bitcast(BF16)
        ARBb = ARB.bitcast(BF16)
        ARAi = ARA.bitcast(I8)
        ARBi = ARB.bitcast(I8)
        XQS = XQS_t[:]
        FSC = FSC_t[:]
        GST = GST_t[:]
        SCL = SCL_t[:]
        STATE = STATE_t[:]
        CON = CON_t[:]
        CONB = CONB_t[:]
        SG = SG_t[:]
        PSA, PSB, PSC, PSD = PSA_t[:], PSB_t[:], PSC_t[:], PSD_t[:]
        STT2 = STT2_t[:]
        GRD1, GRD2 = GRD1_t[:], GRD2_t[:]
        c_idb = GRD1.bitcast(BF16)[:, 0:128]
        def stb(i):
            return STT2[0:1, i*1024:(i+1)*1024]
        def aq(i):
            return STT2[0:1, 4096+i*128:4096+(i+1)*128]
        YASB = ARBb[:, 2*8192:2*8192+2048]
        XPT = ARB[:, 9216:11264]
        OUTP = FSC[:, 0:1024]
        X2C = FSC[:, 1024:2048]

        sems = {"LD": h_LD, "LS": h_LS, "LG": h_LG, "CC": h_CC,
                "V": h_V, "S": h_S, "T": h_T, "SO": h_SO}

        c_ones = CON[:, 0:1]
        c_id = CON[:, 1:129]
        c_A = CON[:, 129:161]
        c_cw = CON[:, 161:169]
        c_cb = CON[:, 169:171]
        c_dp = CON[:, 171:173]
        c_dtb = CON[:, 173:175]
        c_TH = CON[:, 175:181]
        c_Aexp = CON[:, 181:213]
        c_cvals = CON[:, 213:221]
        for _i, _v in enumerate(CONST_VALS):
            nc.const_aps.register(F32, float(_v), c_cvals[:, _i:_i+1])
        cb_xpw = CONB[:, 0:320]
        cb_dtw = CONB[:, 320:576]

        XHAT = ARA[:, 0:2054]
        def XHATd(dt): return ARA[:, dt*1027:(dt+1)*1027]
        def XSQd(dt): return FSC[:, dt*1024:(dt+1)*1024]
        XHs = ARA[:, 2054:4108]
        def XHsd(dt): return ARA[:, 2054+dt*1027:2054+(dt+1)*1027]
        def XCd(dt): return ARA[:, 4108+dt*1024:4108+(dt+1)*1024]
        XC = ARA[:, 4108:6156]
        def DLTd(dt): return ARA[:, 6156+dt*1024:6156+(dt+1)*1024]
        def UUd(dt): return ARA[:, 8204+dt*1024:8204+(dt+1)*1024]
        def YYd(dt): return ARA[:, 10252+dt*1024:10252+(dt+1)*1024]
        YY = ARA[:, 10252:12300]
        CV1 = ARA[:, 12300:13324]
        CV2 = ARA[:, 13324:14348]
        SP1 = CV1
        SP2 = CV2
        XCB = ARAb[:, 2*14348:2*14348+2048]
        YYB = XCB
        DTI = ARAb[:, 2*15372:2*15372+1024]
        A_X2 = ARB[:, 2048:4096]
        A_SQ = ARB[:, 4096:6144]
        A_XQB = ARBb[:, 12288:14336]
        A_X2DT = YY

        AT = [ARB[:, 0:1024], ARB[:, 1024:2048]]
        BT = [ARB[:, 2048:3072], ARB[:, 3072:4096]]
        HT = [ARB[:, 4096:5120], ARB[:, 5120:6144]]
        HCT = [ARBb[:, 12288:13312], ARBb[:, 13312:14336]]
        BRO = [ARB[:, 8192:9216], ARB[:, 9216:10240]]
        CRO = [ARB[:, 10240:11264], ARB[:, 11264:12288]]
        MB = ARB[:, 12288:13315]
        RB = ARB[:, 13315:14342]
        SB0 = 14342
        SROW = lambda r: ARB[r:r+1, SB0:SB0+1024]
        SROW2 = ARB[2:4, SB0:SB0+1024]
        PJ1 = ARB[:, 0:1024]
        PJ2 = ARB[0:32, 1024:2048]

        def HBs(s): return ARAb[:, s*8192:(s+1)*8192]
        def HBc(s, o): return ARAb[:, s*8192+o*1024:s*8192+(o+1)*1024]
        WS = [ARBb[:, 0:16384], ARBb[:, 16384:32768]]
        def HQTt(s, c): return ARBb[:, s*8192+c*128:s*8192+(c+1)*128]
        WS2 = [ARAb[:, 0:16384], ARAb[:, 16384:32768]]
        F_A = FSC[:, 0:2048]
        F_B = FSC[:, 2048:4096]
        F_A1 = FSC[:, 0:1024]

        def scl(base, i): return SCL[:, base+i:base+i+1]

        steps = []
        M = {}
        cnt = {k: 0 for k in sems}
        mode = {"dry": True}

        class IncProxy:
            pass

        def inc(ins, sem, n=None):
            if n is None:
                n = 16 if sem in ("LD", "LS", "LG", "SO") else 1
            cnt[sem] += n
            if ins is not None:
                ins.then_inc(sems[sem], n)

        def mark(key, sem):
            if mode["dry"]:
                assert key not in M, f"dup mark {key}"
                M[key] = (sem, cnt[sem])
            else:
                assert M[key] == (sem, cnt[sem]), (key, M[key], sem, cnt[sem])

        def step(eng, waits, fn):
            if not mode["dry"]:
                pass
            steps.append((eng, list(waits), fn))

        def w(sem, key, delta=0):
            return (sem, key, delta)

        hooks = {}

        def p0():
            def ld(e):
                pairs = [
                    (ARA[:, 0:16384].rearrange("p (k m) -> p k m", k=16), i_ws1),
                    (c_ones, i_ones),
                    (c_cvals, i_cvals),
                    (c_id, i_id),
                    (FSC[:, 576:608].rearrange("p (a n) -> p a n", a=2), i_alog),
                    (c_cw.rearrange("p (a j) -> p a j", a=2), i_cw),
                    (c_cb, i_cb),
                    (c_dp, i_dp),
                    (c_dtb, i_dtb),
                    (FSC[:, 0:320].rearrange("p (a q) -> p a q", a=2), i_xpw),
                    (FSC[:, 320:576].rearrange("p (a f) -> p a f", a=2), i_dtw),
                ]
                for o_, i_ in pairs:
                    inc(e.dma_start(out=o_, in_=i_) if e else None, "LD")
                mark("p0_loads", "LD")
            step("sy", [], ld)

            def ldB(e):
                inc(e.dma_start(out=ARB[:, 0:16384].rearrange("p (k m) -> p k m", k=64),
                                in_=i_ws2) if e else None, "LS")
                mark("p0_loadB", "LS")
            step("sc", [], ldB)

            def vconsts(e):
                inc(e.tensor_copy(cb_xpw, FSC[:, 0:320]) if e else None, "V")
                inc(e.tensor_copy(cb_dtw, FSC[:, 320:576]) if e else None, "V")
                inc(e.memset(STATE[:, 0:64], 0.0) if e else None, "V")
                inc(e.tensor_copy(c_idb, c_id) if e else None, "V")
                inc(e.memset(GRD2[:, 0:64], 7.0) if e else None, "V")
                inc(e.reduce_sum(SCL[:, 60:61], ARA[:, 0:16384], axis=AX.X,
                                 apply_absolute_value=True) if e else None, "V")
                inc(e.reduce_sum(SCL[:, 61:62], ARB[:, 0:16384], axis=AX.X,
                                 apply_absolute_value=True) if e else None, "V")
                mark("p0_abs", "V")
            step("ve", [w("LD", "p0_loads"), w("LS", "p0_loadB")], vconsts)

            def sexp(e):
                inc(e.activation(c_Aexp, FSC[:, 576:608], ACT.Exp, bias=0.0, scale=1.0)
                    if e else None, "S")
                mark("p0_aexp", "S")
            step("sc", [w("LD", "p0_loads")], sexp)

            def vneg(e):
                inc(e.tensor_scalar_mul(c_A, c_Aexp, -1.0) if e else None, "V")
                mark("p0_A", "V")
            step("ve", [w("S", "p0_aexp")], vneg)

            def gmm(e):
                ins = e.matmul(PSA[0:1, 0:2], c_ones, SCL[:, 60:62], start=True, stop=True) if e else None
                inc(ins, "T")
                mark("p0_gmm", "T")
            step("te", [w("V", "p0_abs")], gmm)

            def gevac(e):
                inc(e.copy(SG[0:1, 0:2], PSA[0:1, 0:2]) if e else None, "S")
                mark("p0_gpart", "S")
            step("sc", [w("T", "p0_gmm")], gevac)

            def gin(e):
                inc(e.dma_start(out=cc_g_in[:], in_=SG[0:1, 0:2]) if e else None, "LD")
                mark("p0_gin", "LD")
            step("sy", [w("S", "p0_gpart")], gin)

            def gag(e):
                ins = e.collective_compute("AllGather", ALU.bypass,
                                           replica_groups=[list(range(8))],
                                           ins=[cc_g_in[:]], outs=[cc_g_out[:]]) if e else None
                inc(ins, "CC")
                mark("ag_g", "CC")
            step("gp", [w("LD", "p0_gin")], gag)

            def gall(e):
                inc(e.dma_start(out=SG[0:1, 16:32],
                                in_=cc_g_out.rearrange("r o c -> o (r c)")) if e else None, "LD")
                mark("p0_gall", "LD")
            step("sy", [w("CC", "ag_g")], gall)

            def gmath(e):
                if e:
                    e.reduce_sum(SG[0:1, 8:9], SG[0:1, 16:32].rearrange("o (r c) -> o r c", c=2)[:, :, 0], axis=AX.X)
                    e.reduce_sum(SG[0:1, 9:10], SG[0:1, 16:32].rearrange("o (r c) -> o r c", c=2)[:, :, 1], axis=AX.X)
                    e.tensor_scalar_mul(SG[0:1, 8:9], SG[0:1, 8:9], 1.0 / (8192.0 * 2048.0))
                    e.tensor_scalar_mul(SG[0:1, 9:10], SG[0:1, 9:10], 1.0 / (8192.0 * 2048.0))
                    e.tensor_scalar(SG[0:1, 0:1], SG[0:1, 8:9], GEPS, 0.5, ALU.add, ALU.mult)
                    e.tensor_scalar(SG[0:1, 2:3], SG[0:1, 9:10], GEPS, 0.5, ALU.add, ALU.mult)
                    e.tensor_scalar(SG[0:1, 1:2], SG[0:1, 8:9], GEPS, -0.5, ALU.add, ALU.mult)
                    e.tensor_scalar(SG[0:1, 3:4], SG[0:1, 9:10], GEPS, -0.5, ALU.add, ALU.mult)
                    e.tensor_scalar_mul(SG[0:1, 4:5], SG[0:1, 8:9], 1.0 / 128.0)
                    ins = e.tensor_scalar_mul(SG[0:1, 5:6], SG[0:1, 9:10], 1.0 / 128.0)
                else:
                    ins = None
                inc(ins, "V")
                mark("p0_gmath", "V")
            step("ve", [w("LD", "p0_gall")], gmath)

            def gscr_w(e):
                inc(e.dma_start(out=g_scr[:], in_=SG[0:1, 0:8]) if e else None, "LD")
                mark("p0_gscr", "LD")
            step("sy", [w("V", "p0_gmath")], gscr_w)

            def thb(e):
                inc(e.dma_start(out=c_TH, in_=g_scr[0:1, 0:6].squeeze(0).partition_broadcast(P))
                    if e else None, "LS")
                mark("p0_th", "LS")
            step("sc", [w("LD", "p0_gscr")], thb)

            for mat in range(2):
                src_f = ARA if mat == 0 else ARB
                dst_i = ARAi if mat == 0 else ARBi
                th_ap = c_TH[:, 2*mat:2*mat+1]
                nth_ap = c_TH[:, 2*mat+1:2*mat+2]
                cc_w = cc_w1_in if mat == 0 else cc_w2_in
                for ch in range(8):
                    def tern_v(e, src_f=src_f, th_ap=th_ap, nth_ap=nth_ap, mat=mat, ch=ch):
                        chunk = src_f[:, ch*2048:(ch+1)*2048]
                        if e:
                            e.tensor_scalar(F_A, chunk, th_ap, nth_ap, ALU.min, ALU.max)
                            ins = e.tensor_sub(F_B, chunk, F_A)
                        else:
                            ins = None
                        inc(ins, "V", 1)
                        mark(f"tern_d:{mat}:{ch}", "V")
                    tvw = [w("LS", "p0_th")]
                    if mat == 0 and ch == 0:
                        tvw.append(w("S", "p0_aexp"))
                    else:
                        pm, pc = (mat, ch - 1) if ch > 0 else (mat - 1, 7)
                        tvw.append(w("S", f"tern:{pm}:{pc}"))
                    step("ve", tvw, tern_v)

                    def tern_s(e, dst_i=dst_i, mat=mat, ch=ch):
                        dst = dst_i[:, ch*8192:ch*8192+2048]
                        inc(e.activation(dst, F_B, ACT.Sign, bias=0.0, scale=1.0) if e else None, "S")
                        mark(f"tern:{mat}:{ch}", "S")
                    step("sc", [w("V", f"tern_d:{mat}:{ch}")], tern_s)

                    def tern_st(e, dst_i=dst_i, cc_w=cc_w, mat=mat, ch=ch):
                        kpc = 2 if mat == 0 else 8
                        src = dst_i[:, ch*8192:ch*8192+2048].rearrange("p (k m) -> p k m", k=kpc)
                        dst = cc_w[kpc*ch:kpc*(ch+1)].rearrange("k p m -> p k m")
                        inc(e.dma_start(out=dst, in_=src) if e else None, "LD")
                        mark(f"ternst:{mat}:{ch}", "LD")
                    step("sy", [w("S", f"tern:{mat}:{ch}")], tern_st)

            def emit_agw1():
                def agw1(e):
                    ins = e.collective_compute("AllGather", ALU.bypass,
                                               replica_groups=[list(range(8))],
                                               ins=[cc_w1_in[:]], outs=[cc_w1_out[:]]) if e else None
                    inc(ins, "CC")
                    mark("ag_w1", "CC")
                step("gp", [w("LD", "ternst:0:7")], agw1)
            hooks["agw1"] = emit_agw1

            def emit_agw2():
                def agw2(e):
                    ins = e.collective_compute("AllGather", ALU.bypass,
                                               replica_groups=[list(range(8))],
                                               ins=[cc_w2_in[:]], outs=[cc_w2_out[:]]) if e else None
                    inc(ins, "CC")
                    mark("ag_w2", "CC")
                step("gp", [w("LD", "ternst:1:7")], agw2)
                if dbg:
                    def dmp_w1(e):
                        inc(e.dma_start(out=o_dw1[:], in_=cc_w1_out[:]) if e else None, "SO")
                    step("sy", [w("CC", "ag_w1")], dmp_w1)
                    def dmp_th(e):
                        inc(e.dma_start(out=o_dth[:], in_=c_TH) if e else None, "SO")
                    step("sy", [w("LS", "p0_th")], dmp_th)
            hooks["agw2"] = emit_agw2

        def wave_f1(gw):
            beta, ww = gw % 2, gw // 2

            def xsc_ld(e):
                inc(e.dma_start(out=ARA[:, 0:2054].rearrange("p (a t) -> p a t", a=2),
                                in_=i_xsc[gw]) if e else None, "LD")
                mark(f"xsc:{gw}", "LD")
            if gw == 0:
                waits = [w("LD", "ternst:1:7")]
            elif gw % 4 == 0:
                waits = [w("V", f"a_q2:{gw-1}"), w("LS", f"xqt:{gw-1}")]
                if not noffn:
                    waits.append(w("V", "f_oc:0:3:3"))
                    waits.append(w("SO", "f_ost:0:3:3"))
            else:
                waits = [w("V", f"yev:{gw-1}:1")]
            step("sy", waits, xsc_ld)

            def xsq(e):
                for dt in range(NDT):
                    inc(e.activation(XSQd(dt), ARA[:, dt*1027+3:dt*1027+1027], ACT.Square,
                                     bias=0.0, scale=1.0) if e else None, "S")
                mark(f"xsq:{gw}", "S")
            sw = [w("LD", f"xsc:{gw}")]
            if gw == 0:
                sw.append(w("S", "tern:1:7"))
            step("sc", sw, xsq)

            def smm(e):
                for hf in range(2):
                    for dt in range(NDT):
                        ins = e.matmul(PSA[0:1, hf*512:(hf+1)*512], c_ones,
                                       ARA[:, dt*1027+3+hf*512:dt*1027+3+(hf+1)*512],
                                       start=(dt == 0), stop=(dt == 1)) if e else None
                        inc(ins, "T")
                for hf in range(2):
                    for dt in range(NDT):
                        ins = e.matmul(PSB[0:1, hf*512:(hf+1)*512], c_ones,
                                       FSC[:, dt*1024+hf*512:dt*1024+(hf+1)*512],
                                       start=(dt == 0), stop=(dt == 1)) if e else None
                        inc(ins, "T")
                mark(f"stat_mm:{gw}", "T")
            tw_ = [w("S", f"xsq:{gw}"), w("LD", f"xsc:{gw}")]
            if gw == 0:
                tw_.append(w("S", "p0_gpart"))
            else:
                tw_.append(w("S", f"pj_evac:{gw-1}"))
            step("te", tw_, smm)

            def sevac(e):
                inc(e.copy(stb(0), PSA[0:1, 0:1024]) if e else None, "S")
                inc(e.copy(stb(1), PSB[0:1, 0:1024]) if e else None, "S")
                mark(f"stat_evac:{gw}", "S")
            sw2 = [w("T", f"stat_mm:{gw}")]
            if gw == 0:
                sw2.append(w("LD", "ternst:1:7"))
            step("sc", sw2, sevac)

            def apack(e):
                inc(e.dma_start(out=cc_a_in.rearrange("a t -> (a t)").unsqueeze(0),
                                in_=STT2[0:1, 0:2048]) if e else None, "LD")
                mark(f"apack:{gw}", "LD")
            step("sy", [w("S", f"stat_evac:{gw}")], apack)

            def ar_a(e):
                ins = e.collective_compute("AllReduce", ALU.add,
                                           replica_groups=[list(range(8))],
                                           ins=[cc_a_in[:]], outs=[cc_a_out[:]]) if e else None
                inc(ins, "CC")
                mark(f"ar_a:{gw}", "CC")
            step("gp", [w("LD", f"apack:{gw}")], ar_a)

        def wave_f2(gw):
            beta, ww = gw % 2, gw // 2

            def aload(e):
                inc(e.dma_start(out=STT2[0:1, 0:2048],
                                in_=cc_a_out.rearrange("a t -> (a t)").unsqueeze(0)) if e else None, "LD")
                mark(f"aload:{gw}", "LD")
            step("sy", [w("CC", f"ar_a:{gw}")], aload)

            def statv(e):
                if e:
                    e.tensor_scalar_mul(stb(2), stb(0), 1.0 / 2048.0)
                    e.tensor_scalar_mul(stb(3), stb(1), 1.0 / 2048.0)
                    e.tensor_mul(stb(0), stb(2), stb(2))
                    ins = e.tensor_sub(stb(1), stb(3), stb(0))
                else:
                    ins = None
                inc(ins, "V", 1)
                mark(f"stat_v:{gw}", "V")
            step("ve", [w("LD", f"aload:{gw}")], statv)

            def rstd(e):
                if e:
                    e.activation(stb(0), stb(1), ACT.Ln, bias=EPS1, scale=1.0)
                    ins = e.activation(stb(3), stb(0), ACT.Exp, bias=0.0, scale=-0.5)
                else:
                    ins = None
                inc(ins, "S", 1)
                mark(f"rstd:{gw}", "S")
            step("sc", [w("V", f"stat_v:{gw}")], rstd)

            if dbg:
                def dmp_st(e):
                    inc(e.dma_start(out=o_dst[gw], in_=STT2[0:1, 2048:4096]) if e else None, "SO")
                step("sy", [w("S", f"rstd:{gw}")], dmp_st)

            def stscr_w(e):
                inc(e.dma_start(out=st_scr[gw].rearrange("a t -> (a t)").unsqueeze(0),
                                in_=STT2[0:1, 2048:4096]) if e else None, "LD")
                mark(f"stscr:{gw}", "LD")
            step("sy", [w("S", f"rstd:{gw}")], stscr_w)

            def bcast_mr(e):
                inc(e.dma_start(out=MB[:, 3:1027],
                                in_=st_scr[gw][0:1, :].squeeze(0).partition_broadcast(P))
                    if e else None, "LS")
                inc(e.dma_start(out=RB[:, 3:1027],
                                in_=st_scr[gw][1:2, :].squeeze(0).partition_broadcast(P))
                    if e else None, "LS")
                if ww > 0:
                    inc(e.dma_start(out=MB[:, 0:3],
                                    in_=st_scr[gw-2][0:1, 1021:1024].squeeze(0).partition_broadcast(P))
                        if e else None, "LS")
                    inc(e.dma_start(out=RB[:, 0:3],
                                    in_=st_scr[gw-2][1:2, 1021:1024].squeeze(0).partition_broadcast(P))
                        if e else None, "LS")
                mark(f"bcast_mr:{gw}", "LS")
            step("sc", [w("LD", f"stscr:{gw}")], bcast_mr)

            if ww == 0:
                def memh(e):
                    if e:
                        e.memset(MB[:, 0:3], 0.0)
                        e.memset(RB[:, 0:3], 0.0)
                step("ve", [], memh)

            def xhat(e):
                for dt in range(NDT):
                    if e:
                        e.tensor_sub(XHsd(dt), XHATd(dt), MB)
                        ins = e.tensor_mul(XHATd(dt), XHsd(dt), RB)
                    else:
                        ins = None
                    inc(ins, "V", 1)
                mark(f"xhat:{gw}", "V")
            step("ve", [w("LS", f"bcast_mr:{gw}"), w("LD", f"xsc:{gw}")], xhat)

            for dt in range(NDT):
                def conv_dt(e, dt=dt):
                    xh = XHATd(dt)
                    if e:
                        e.tensor_scalar_mul(CV1, xh[:, 0:1024], c_cw[:, dt*4:dt*4+1])
                        e.scalar_tensor_tensor(CV2, xh[:, 1:1025], c_cw[:, dt*4+1:dt*4+2], CV1, ALU.mult, ALU.add)
                        e.scalar_tensor_tensor(CV1, xh[:, 2:1026], c_cw[:, dt*4+2:dt*4+3], CV2, ALU.mult, ALU.add)
                        ins = e.scalar_tensor_tensor(CV2, xh[:, 3:1027], c_cw[:, dt*4+3:dt*4+4], CV1, ALU.mult, ALU.add)
                    else:
                        ins = None
                    inc(ins, "V", 1)
                    mark(f"conv:{gw}:{dt}", "V")
                wv = [w("V", f"xhat:{gw}")]
                if dt > 0:
                    wv.append(w("S", f"silu:{gw}:{dt-1}"))
                step("ve", wv, conv_dt)

                def silu_dt(e, dt=dt):
                    inc(e.activation(XCd(dt), CV2, ACT.Silu, bias=c_cb[:, dt:dt+1], scale=1.0)
                        if e else None, "S")
                    mark(f"silu:{gw}:{dt}", "S")
                step("sc", [w("V", f"conv:{gw}:{dt}")], silu_dt)

            def xcb_c(e):
                inc(e.copy(XCB, XC) if e else None, "S")
                mark(f"xcb:{gw}", "S")
            xcbw = [w("S", f"silu:{gw}:1")]
            if gw > 0:
                xcbw.append(w("LD", f"ypack:{gw-1}"))
            step("sc", xcbw, xcb_c)

            if dbg:
                def dmp_xh(e):
                    inc(e.dma_start(out=o_dxh[gw], in_=XHAT) if e else None, "SO")
                step("sy", [w("V", f"xhat:{gw}")], dmp_xh)
                def dmp_xc(e):
                    inc(e.dma_start(out=o_dxc[gw], in_=XC) if e else None, "SO")
                step("sy", [w("S", f"silu:{gw}:1")], dmp_xc)
                def dmp_pj(e):
                    inc(e.dma_start(out=o_dpj[gw], in_=PJ1) if e else None, "SO")
                step("sy", [w("S", f"pj_evac:{gw}")], dmp_pj)

            def projmm(e):
                XCBv = XCB.rearrange("p (a t) -> p a t", a=2)
                for hf in range(2):
                    for dt in range(NDT):
                        ins = e.matmul(PSA[:, hf*512:(hf+1)*512], cb_xpw[:, dt*160:dt*160+128],
                                       XCBv[:, dt, hf*512:(hf+1)*512],
                                       start=(dt == 0), stop=(dt == 1)) if e else None
                        inc(ins, "T")
                for hf in range(2):
                    for dt in range(NDT):
                        ins = e.matmul(PSB[0:32, hf*512:(hf+1)*512], cb_xpw[:, dt*160+128:dt*160+160],
                                       XCBv[:, dt, hf*512:(hf+1)*512],
                                       start=(dt == 0), stop=(dt == 1)) if e else None
                        inc(ins, "T")
                mark(f"proj_mm:{gw}", "T")
            step("te", [w("S", f"xcb:{gw}"), w("S", f"stat_evac:{gw}")], projmm)

            def pjevac(e):
                inc(e.copy(PJ1, PSA[:, 0:1024]) if e else None, "S")
                inc(e.copy(PJ2, PSB[0:32, 0:1024]) if e else None, "S")
                mark(f"pj_evac:{gw}", "S")
            step("sc", [w("T", f"proj_mm:{gw}")], pjevac)

            def bpack(e):
                inc(e.dma_start(out=cc_b_in[0:128, :], in_=PJ1) if e else None, "LD")
                inc(e.dma_start(out=cc_b_in[128:160, :], in_=PJ2) if e else None, "LD")
                mark(f"bpack:{gw}", "LD")
            step("sy", [w("S", f"pj_evac:{gw}")], bpack)

            def ar_b(e):
                ins = e.collective_compute("AllReduce", ALU.add,
                                           replica_groups=[list(range(8))],
                                           ins=[cc_b_in[:]], outs=[cc_b_out[:]]) if e else None
                inc(ins, "CC")
                mark(f"ar_b:{gw}", "CC")
            arbw = [w("LD", f"bpack:{gw}")]
            if gw > 0:
                arbw.append(w("LS", f"bc:{gw-1}:15"))
            step("gp", arbw, ar_b)
            if gw == 0:
                hooks["agw1"]()

            if dbg:
                def dmp_bo(e):
                    inc(e.dma_start(out=o_dbo[gw], in_=cc_b_out[:]) if e else None, "SO")
                step("sy", [w("CC", f"ar_b:{gw}")], dmp_bo)

        def wave(gw):
            beta, ww = gw % 2, gw // 2

            def dtild(e):
                inc(e.dma_start(out=DTI.rearrange("p (a t) -> p a t", a=2),
                                in_=cc_b_out[0:128, :].rearrange("p (a t) -> p a t", a=2))
                    if e else None, "LG")
                mark(f"dti:{gw}", "LG")
            step("gp", [w("CC", f"ar_b:{gw}")], dtild)

            for dt in range(NDT):
                def dtpmm(e, dt=dt):
                    DTIv = DTI.rearrange("p (a t) -> p a t", a=2)
                    for hf in range(2):
                        ins = e.matmul(PSC[:, hf*512:(hf+1)*512], cb_dtw[:, dt*128:(dt+1)*128],
                                       DTIv[:, hf, :], start=True, stop=True) if e else None
                        inc(ins, "T")
                    mark(f"dtp_mm:{gw}:{dt}", "T")
                tw2 = [w("LG", f"dti:{gw}")]
                if dt == 0:
                    if gw > 0:
                        tw2.append(w("S", f"a_trev:{gw-1}:15"))
                else:
                    tw2.append(w("V", f"delta:{gw}:0"))
                step("te", tw2, dtpmm)

                def sp_s(e, dt=dt):
                    if e:
                        e.activation(SP1, PSC[:, 0:1024], ACT.Abs, bias=c_dtb[:, dt:dt+1], scale=1.0)
                        e.activation(SP2, SP1, ACT.Exp, bias=0.0, scale=-1.0)
                        ins = e.activation(SP1, SP2, ACT.Ln, bias=1.0, scale=1.0)
                    else:
                        ins = None
                    inc(ins, "S", 1)
                    mark(f"sp_ln:{gw}:{dt}", "S")
                step("sc", [w("T", f"dtp_mm:{gw}:{dt}")], sp_s)

                def sp_v(e, dt=dt):
                    if e:
                        e.tensor_scalar(SP2, PSC[:, 0:1024], c_dtb[:, dt:dt+1], 0.0, ALU.add, ALU.max)
                        ins = e.tensor_add(DLTd(dt), SP1, SP2)
                    else:
                        ins = None
                    inc(ins, "V", 1)
                    mark(f"delta:{gw}:{dt}", "V")
                step("ve", [w("S", f"sp_ln:{gw}:{dt}")], sp_v)

                def u_v(e, dt=dt):
                    inc(e.tensor_mul(UUd(dt), DLTd(dt), XCd(dt)) if e else None, "V")
                    mark(f"u:{gw}:{dt}", "V")
                step("ve", [w("V", f"delta:{gw}:{dt}")], u_v)

            if dbg:
                def dmp_dl(e):
                    inc(e.dma_start(out=o_ddl[gw], in_=ARA[:, 6156:8204]) if e else None, "SO")
                step("sy", [w("V", f"delta:{gw}:1")], dmp_dl)

            if dbg:
                def dmp_pre(e):
                    inc(e.dma_start(out=o_ds3[gw], in_=STATE[:, 0:64]) if e else None, "SO")
                    mark(f"dmp_pre:{gw}", "SO")
                step("sy", [w("V", f"delta:{gw}:1")], dmp_pre)

            for n in range(NN):
                pg = n % 2

                def bc_n(e, n=n, pg=pg):
                    if e:
                        i1 = e.dma_start(out=BRO[pg],
                                         in_=cc_b_out[128+n:129+n, :].squeeze(0).partition_broadcast(P))
                        i2 = e.dma_start(out=CRO[pg],
                                         in_=cc_b_out[144+n:145+n, :].squeeze(0).partition_broadcast(P))
                        i1.then_inc(sems["LS"], 16)
                        i2.then_inc(sems["LS"], 16)
                    cnt["LS"] += 32
                    mark(f"bc:{gw}:{n}", "LS")
                wv = [w("CC", f"ar_b:{gw}")]
                if n < 2 and gw > 0:
                    wv.append(w("V", f"a_x2dt:{gw-1}"))
                if n >= 2:
                    wv.append(w("V", f"hc:{gw}:1:{n-2}"))
                step("sc", wv, bc_n)

                for dt in range(NDT):
                    col = (beta * 2 + dt) * 16 + n

                    def aexp(e, n=n, dt=dt, col=col):
                        inc(e.activation(AT[dt], DLTd(dt), ACT.Exp, bias=0.0,
                                         scale=c_A[:, dt*16+n:dt*16+n+1]) if e else None, "S")
                        mark(f"aexp:{gw}:{dt}:{n}", "S")
                    wv2 = [w("V", f"delta:{gw}:{dt}")]
                    if n >= 1:
                        wv2.append(w("V", f"scan:{gw}:{dt}:{n-1}"))
                    if dbg and n == 1:
                        wv2.append(w("SO", f"dmp_h0:{gw}"))
                    step("sc", wv2, aexp)

                    def scanv(e, n=n, dt=dt, pg=pg, col=col):
                        if e:
                            e.tensor_mul(BT[dt], UUd(dt), BRO[pg])
                            e.tensor_tensor_scan(HT[dt], AT[dt], BT[dt],
                                                 STATE[:, col:col+1], ALU.mult, ALU.add)
                            sc_ins = e.tensor_mul(HCT[dt], HT[dt], CRO[pg])
                        else:
                            sc_ins = None
                        inc(sc_ins, "V", 1)
                        mark(f"scan:{gw}:{dt}:{n}", "V")
                        cp_ins = e.tensor_copy(STATE[:, col:col+1], HT[dt][:, 1023:1024]) if e else None
                        inc(cp_ins, "V", 1)
                        mark(f"hc:{gw}:{dt}:{n}", "V")
                    wv3 = [w("S", f"aexp:{gw}:{dt}:{n}"), w("LS", f"bc:{gw}:{n}"),
                           w("V", f"u:{gw}:{dt}")]
                    if n == 0 and gw > 0:
                        wv3.append(w("V", f"a_q2:{gw-1}"))
                        wv3.append(w("LS", f"xqt:{gw-1}"))
                    if n >= 1:
                        wv3.append(w("T", f"yacc:{gw}:{dt}:{n-1}"))
                    if dbg and n == 0 and dt == 0:
                        wv3.append(w("SO", f"dmp_pre:{gw}"))
                    step("ve", wv3, scanv)

                    def yacc(e, n=n, dt=dt):
                        PSY = PSD if dt == 0 else PSC
                        for hf in range(2):
                            ins = e.matmul(PSY[:, hf*512:(hf+1)*512], c_idb,
                                           HCT[dt][:, hf*512:(hf+1)*512],
                                           start=(n == 0), stop=(n == NN-1)) if e else None
                            inc(ins, "T")
                        mark(f"yacc:{gw}:{dt}:{n}", "T")
                    wv4 = [w("V", f"hc:{gw}:{dt}:{n}")]
                    if n == 0:
                        if dt == 0:
                            if gw > 0:
                                wv4.append(w("V", f"yev:{gw-1}:0"))
                        else:
                            wv4.append(w("V", f"delta:{gw}:1"))
                    step("te", wv4, yacc)

            if dbg:
                def dmp_h0(e):
                    inc(e.dma_start(out=o_dh0[gw][:, 0:2048], in_=ARB[:, 4096:6144]) if e else None, "SO")
                    inc(e.dma_start(out=o_dh0[gw][:, 2048:4096], in_=ARB[:, 0:2048]) if e else None, "SO")
                    inc(e.dma_start(out=o_dh0[gw][:, 4096:6144], in_=ARB[:, 2048:4096]) if e else None, "SO")
                    mark(f"dmp_h0:{gw}", "SO")
                step("sy", [w("V", f"hc:{gw}:1:0")], dmp_h0)

                def dmp_ab(e):
                    inc(e.dma_start(out=o_dab[gw][:, 0:2048], in_=ARB[:, 0:2048]) if e else None, "SO")
                    inc(e.dma_start(out=o_dab[gw][:, 2048:4096], in_=ARB[:, 2048:4096]) if e else None, "SO")
                    inc(e.dma_start(out=o_dhc[gw][:, 0:2048], in_=ARB[:, 6144:8192]) if e else None, "SO")
                    inc(e.dma_start(out=o_dhc[gw][:, 2048:2112], in_=STATE[:, 0:64]) if e else None, "SO")
                step("sy", [w("T", f"yacc:{gw}:1:{NN-1}")], dmp_ab)

            for dt in range(NDT):
                def yev(e, dt=dt):
                    PSY = PSD if dt == 0 else PSC
                    ins = e.scalar_tensor_tensor(YYd(dt), XHATd(dt)[:, 3:1027],
                                                 c_dp[:, dt:dt+1], PSY[:, 0:1024],
                                                 ALU.mult, ALU.add) if e else None
                    inc(ins, "V", 1)
                    mark(f"yev:{gw}:{dt}", "V")
                step("ve", [w("T", f"yacc:{gw}:{dt}:{NN-1}")], yev)

            if dbg:
                def dmp_yy(e):
                    inc(e.dma_start(out=o_dyy[gw], in_=YY) if e else None, "SO")
                    mark(f"dmp_yy:{gw}", "SO")
                step("sy", [w("V", f"yev:{gw}:1")], dmp_yy)

            if dbg:
                def dmp_s1(e):
                    inc(e.dma_start(out=o_ds2[gw][0], in_=STATE[:, 0:64]) if e else None, "SO")
                step("sy", [w("V", f"yev:{gw}:1")], dmp_s1)

            def yyb_c(e):
                inc(e.tensor_copy(YYB, YY) if e else None, "V")
                mark(f"yyb:{gw}", "V")
            step("ve", [w("V", f"yev:{gw}:1")], yyb_c)

            def ypack(e):
                inc(e.dma_start(out=cc_y_in.rearrange("(a p) t -> p a t", p=P),
                                in_=YYB.rearrange("p (a t) -> p a t", a=2)) if e else None, "LD")
                mark(f"ypack:{gw}", "LD")
            step("sy", [w("V", f"yyb:{gw}")], ypack)

            def ag_y(e):
                ins = e.collective_compute("AllGather", ALU.bypass,
                                           replica_groups=[list(range(8))],
                                           ins=[cc_y_in[:]], outs=[cc_y_out[:]]) if e else None
                inc(ins, "CC")
                mark(f"ag_y:{gw}", "CC")
            step("gp", [w("LD", f"ypack:{gw}")], ag_y)
            if gw + 1 < NW and (gw + 1) % 4 != 0:
                wave_f1(gw + 1)
            if gw == 0:
                hooks["agw2"]()

            def yread(e):
                if e:
                    rank = e.partition_id()
                    src = cc_y_out[:, :, bass.ts(rank, P)]
                    src = src.rearrange("r (a p) t -> p r a t", p=P)
                    dst = YASB.rearrange("p (r a t) -> p r a t", r=8, a=2)
                    ins = e.dma_start(out=dst, in_=src)
                else:
                    ins = None
                inc(ins, "LD")
                mark(f"yread:{gw}", "LD")
                inc(e.dma_start(out=XPT.rearrange("p (k t) -> p k t", k=16),
                                in_=i_xpt[gw]) if e else None, "LD")
                mark(f"xptld:{gw}", "LD")
            step("sy", [w("CC", f"ag_y:{gw}")], yread)

            def a_x2dt(e):
                inc(e.tensor_add(A_X2DT, XPT, YASB) if e else None, "V")
                mark(f"a_x2dt:{gw}", "V")
            a2w = [w("LD", f"xptld:{gw}"), w("V", f"yyb:{gw}")]
            if dbg:
                a2w.append(w("SO", f"dmp_yy:{gw}"))
            step("ve", a2w, a_x2dt)

            for k in range(16):
                def a_tr(e, k=k):
                    pg = k % 2
                    ins = e.transpose(PSC[:, pg*512:pg*512+128],
                                      A_X2DT[:, k*128:(k+1)*128], c_id) if e else None
                    inc(ins, "T")
                    mark(f"a_tr:{gw}:{k}", "T")
                wv = [w("V", f"a_x2dt:{gw}")]
                if k == 0:
                    wv.append(w("V", f"yev:{gw}:1"))
                if k >= 2:
                    wv.append(w("S", f"a_trev:{gw}:{k-2}"))
                step("te", wv, a_tr)

                def a_trev(e, k=k):
                    pg = k % 2
                    inc(e.copy(A_X2[:, k*128:(k+1)*128], PSC[:, pg*512:pg*512+128]) if e else None, "S")
                    mark(f"a_trev:{gw}:{k}", "S")
                step("sc", [w("T", f"a_tr:{gw}:{k}")], a_trev)

            if gw + 1 < NW and (gw + 1) % 4 != 0:
                wave_f2(gw + 1)

            def x2st(e):
                inc(e.dma_start(out=x2_scr[gw], in_=A_X2) if e else None, "LD")
                if dbg:
                    inc(e.dma_start(out=o_x2[gw], in_=A_X2) if e else None, "SO")
                mark(f"x2st:{gw}", "LD")
            step("sy", [w("S", f"a_trev:{gw}:15")], x2st)

            def a_sq(e):
                inc(e.activation(A_SQ, A_X2, ACT.Square, bias=0.0, scale=1.0,
                                 accum_out=SCL[:, 48:49]) if e else None, "S")
                mark(f"a_sq:{gw}", "S")
            step("sc", [w("S", f"a_trev:{gw}:15")], a_sq)

            def a_v(e):
                if e:
                    e.reduce_sum(SCL[:, 49:50], A_X2, axis=AX.X)
                    e.reduce_max(SCL[:, 50:51], A_X2, axis=AX.X)
                    ins = e.tensor_reduce(SCL[:, 51:52], A_X2, AX.X, ALU.min)
                else:
                    ins = None
                inc(ins, "V", 1)
                mark(f"a_v:{gw}", "V")
            avw = [w("S", f"a_sq:{gw}")]
            if dbg and gw > 0:
                avw.append(w("SO", f"dmp_scl:{gw-1}"))
            step("ve", avw, a_v)

            def a_b1w(e):
                for cc_ in range(4):
                    inc(e.dma_start(out=qs_a[cc_].unsqueeze(1), in_=SCL[:, 48+cc_:49+cc_]) if e else None, "LD")
                mark(f"a_b1w:{gw}", "LD")
            ab1w = [w("V", f"a_v:{gw}")]
            if dbg and gw > 0:
                ab1w.append(w("SO", f"dmp_aq:{gw-1}"))
            step("sy", ab1w, a_b1w)

            def a_b1(e):
                for cc_ in range(4):
                    inc(e.dma_start(out=aq(cc_), in_=qs_a[cc_].unsqueeze(0)) if e else None, "LD")
                mark(f"a_b1:{gw}", "LD")
            step("sy", [w("LD", f"a_b1w:{gw}")], a_b1)

            def a_rm1(e):
                if e:
                    e.tensor_scalar_mul(aq(4), aq(1), 1.0 / 2048.0)
                    e.tensor_scalar_mul(aq(5), aq(0), 1.0 / 2048.0)
                    e.tensor_mul(aq(6), aq(4), aq(4))
                    ins = e.tensor_sub(aq(5), aq(5), aq(6))
                else:
                    ins = None
                inc(ins, "V", 1)
                mark(f"a_rm1:{gw}", "V")
            step("ve", [w("LD", f"a_b1:{gw}")], a_rm1)

            def a_rs1(e):
                if e:
                    e.activation(aq(6), aq(5), ACT.Ln, bias=EPS1, scale=1.0)
                    ins = e.activation(aq(7), aq(6), ACT.Exp, bias=0.0, scale=-0.5)
                else:
                    ins = None
                inc(ins, "S", 1)
                mark(f"a_rs1:{gw}", "S")
            step("sc", [w("V", f"a_rm1:{gw}")], a_rs1)

            def a_rm2(e):
                if e:
                    e.tensor_mul(aq(8), aq(7), aq(7))
                    ins = e.tensor_mul(aq(9), aq(5), aq(8))
                else:
                    ins = None
                inc(ins, "V", 1)
                mark(f"a_rm2:{gw}", "V")
            step("ve", [w("S", f"a_rs1:{gw}")], a_rm2)

            def a_rs2(e):
                if e:
                    e.activation(aq(10), aq(9), ACT.Ln, bias=EPS2, scale=1.0)
                    ins = e.activation(aq(11), aq(10), ACT.Exp, bias=0.0, scale=-0.5)
                else:
                    ins = None
                inc(ins, "S", 1)
                mark(f"a_rs2:{gw}", "S")
            step("sc", [w("V", f"a_rm2:{gw}")], a_rs2)

            def a_rm3(e):
                if e:
                    e.tensor_mul(aq(12), aq(7), aq(11))
                    e.tensor_sub(aq(13), aq(2), aq(4))
                    e.tensor_sub(aq(14), aq(4), aq(3))
                    e.tensor_max(aq(15), aq(13), aq(14))
                    ins = e.tensor_mul(aq(13), aq(15), aq(12))
                else:
                    ins = None
                inc(ins, "V", 1)
                mark(f"a_rm3:{gw}", "V")
            step("ve", [w("S", f"a_rs2:{gw}")], a_rm3)

            def a_rs3(e):
                if e:
                    e.activation(aq(14), aq(13), ACT.Ln, bias=0.0, scale=1.0)
                    ins = e.activation(aq(15), aq(14), ACT.Exp, bias=0.0, scale=-1.0)
                else:
                    ins = None
                inc(ins, "S", 1)
                mark(f"a_rs3:{gw}", "S")
            step("sc", [w("V", f"a_rm3:{gw}")], a_rs3)

            def a_rm4(e):
                if e:
                    e.tensor_mul(aq(14), aq(12), aq(15))
                    e.tensor_scalar(aq(15), aq(13), SG[0:1, 4:5], None, ALU.mult)
                    e.tensor_scalar_mul(aq(13), aq(14), 128.0)
                    e.tensor_mul(aq(12), aq(4), aq(13))
                    ins = e.tensor_scalar_mul(aq(14), aq(12), -1.0)
                else:
                    ins = None
                inc(ins, "V", 1)
                mark(f"a_rm4:{gw}", "V")
            step("ve", [w("S", f"a_rs3:{gw}")], a_rm4)

            if dbg:
                def dmp_aq(e):
                    inc(e.dma_start(out=o_daq[gw], in_=STT2[0:1, 4096:6144]) if e else None, "SO")
                    mark(f"dmp_aq:{gw}", "SO")
                step("sy", [w("V", f"a_rm4:{gw}")], dmp_aq)

            def a_b2w(e):
                for i_ in range(3):
                    inc(e.dma_start(out=qs_b[i_].unsqueeze(0), in_=aq(13+i_)) if e else None, "LD")
                mark(f"a_b2w:{gw}", "LD")
            step("sy", [w("V", f"a_rm4:{gw}")], a_b2w)

            def a_b2(e):
                for i_, dst in ((0, scl(0, gw)), (1, scl(8, gw)), (2, scl(16, gw))):
                    inc(e.dma_start(out=dst, in_=qs_b[i_].unsqueeze(1)) if e else None, "LD")
                mark(f"a_scl:{gw}", "LD")
            step("sy", [w("LD", f"a_b2w:{gw}")], a_b2)

            def a_q2(e):
                if e:
                    e.tensor_scalar(A_SQ, A_X2, scl(0, gw), scl(8, gw), ALU.mult, ALU.add)
                    e.tensor_scalar(A_X2, A_SQ, CLIP_HI, CLIP_LO, ALU.min, ALU.max)
                    ins = e.tensor_scalar(A_XQB, A_X2, scl(16, gw), None, ALU.mult)
                else:
                    ins = None
                inc(ins, "V", 1)
                mark(f"a_q2:{gw}", "V")
            q2w = [w("LD", f"a_scl:{gw}"), w("LD", f"x2st:{gw}")]
            if dbg and gw > 0:
                q2w.append(w("SO", f"dmp_xq:{gw-1}"))
            step("ve", q2w, a_q2)

            if dbg:
                def dmp_s2(e):
                    inc(e.dma_start(out=o_ds2[gw][1], in_=GRD2[:, 0:64]) if e else None, "SO")
                step("sy", [w("V", f"a_q2:{gw}")], dmp_s2)

            def xqt(e):
                out_v = XQS[:, gw*2048:(gw+1)*2048].rearrange("p (k t) -> p k t", k=16)
                inc(e.dma_start_transpose(out_v, A_XQB) if e else None, "LS")
                mark(f"xqt:{gw}", "LS")
            step("sc", [w("V", f"a_q2:{gw}")], xqt)
            if dbg:
                def dmp_scl(e):
                    inc(e.dma_start(out=o_dscl[gw], in_=SCL[:, 0:64]) if e else None, "SO")
                    mark(f"dmp_scl:{gw}", "SO")
                step("sy", [w("LD", f"a_scl:{gw}")], dmp_scl)
                def dmp_xq(e):
                    inc(e.dma_start(out=o_dxq[gw], in_=A_XQB) if e else None, "SO")
                    mark(f"dmp_xq:{gw}", "SO")
                step("sy", [w("V", f"a_q2:{gw}")], dmp_xq)

        def ffn(H):
            for o in range(8):
                def w1ld(e, o=o):
                    src = cc_w1_out[o].rearrange("k p m -> p k m")
                    dst = WS[o % 2].rearrange("p (k m) -> p k m", k=16)
                    inc(e.dma_start(out=dst, in_=src) if e else None, "LG")
                    mark(f"f_w1:{H}:{o}", "LG")
                wv = [w("CC", "ag_w1")]
                if o < 2:
                    wv.append(w("V", f"yev:{4*H+3}:1"))
                    wv.append(w("V", f"a_q2:{4*H+3}"))
                    wv.append(w("LS", f"xqt:{4*H+3}"))
                else:
                    wv.append(w("S", f"f_sq:{H}:{o-2}:3"))
                step("gp", wv, w1ld)

                for s in range(4):
                    idx = o * 4 + s

                    def mm1(e, o=o, s=s, idx=idx):
                        PS = PSA if idx % 2 == 0 else PSB
                        wsv = WS[o % 2]
                        for k in range(16):
                            lhsT = XQS[:, ((4*H+s)*16+k)*128:((4*H+s)*16+k+1)*128]
                            for hf in range(2):
                                ins = e.matmul(PS[:, hf*512:(hf+1)*512], lhsT,
                                               wsv[:, k*1024+hf*512:k*1024+(hf+1)*512],
                                               start=(k == 0), stop=(k == 15)) if e else None
                                inc(ins, "T")
                        mark(f"f_mm1:{H}:{o}:{s}", "T")
                    wv = [w("LG", f"f_w1:{H}:{o}"), w("LS", f"xqt:{4*H+s}")]
                    if idx >= 2:
                        wv.append(w("S", f"f_sq:{H}:{(idx-2)//4}:{(idx-2)%4}"))
                    else:
                        wv.append(w("S", f"pj_evac:{4*H+3}"))
                    step("te", wv, mm1)

                    def gelu_s(e, o=o, s=s, idx=idx):
                        PS = PSA if idx % 2 == 0 else PSB
                        inc(e.activation(HBc(s, o), PS[:, 0:1024], ACT.Gelu, bias=0.0, scale=1.0,
                                         accum_out=GST[:, s*32+o:s*32+o+1]) if e else None, "S")
                        mark(f"f_gelu:{H}:{o}:{s}", "S")
                        inc(e.activation(F_A1, HBc(s, o), ACT.Square, bias=0.0, scale=1.0,
                                         accum_out=GST[:, s*32+8+o:s*32+8+o+1]) if e else None, "S")
                        mark(f"f_sq:{H}:{o}:{s}", "S")
                    step("sc", [w("T", f"f_mm1:{H}:{o}:{s}")], gelu_s)

                    def mmx(e, o=o, s=s):
                        if e:
                            e.reduce_max(GST[:, s*32+16+o:s*32+17+o], HBc(s, o), axis=AX.X)
                            ins = e.tensor_reduce(GST[:, s*32+24+o:s*32+25+o], HBc(s, o),
                                                  AX.X, ALU.min)
                        else:
                            ins = None
                        inc(ins, "V", 1)
                        mark(f"f_mmx:{H}:{o}:{s}", "V")
                    step("ve", [w("S", f"f_gelu:{H}:{o}:{s}")], mmx)

            for s in range(4):
                def f_hv(e, s=s):
                    if e:
                        e.reduce_sum(SCL[:, 48:49], GST[:, s*32:s*32+8], axis=AX.X)
                        e.reduce_sum(SCL[:, 49:50], GST[:, s*32+8:s*32+16], axis=AX.X)
                        e.reduce_max(SCL[:, 50:51], GST[:, s*32+16:s*32+24], axis=AX.X)
                        ins = e.tensor_reduce(SCL[:, 51:52], GST[:, s*32+24:s*32+32], AX.X, ALU.min)
                    else:
                        ins = None
                    inc(ins, "V", 1)
                    mark(f"f_hv:{H}:{s}", "V")
                step("ve", [w("V", f"f_mmx:{H}:7:{s}"), w("S", f"f_sq:{H}:7:{s}")], f_hv)

                if dbg:
                    def dmp_fh1(e, s=s):
                        inc(e.dma_start(out=o_fh1[4*H+s], in_=HBs(s)) if e else None, "SO")
                        mark(f"dmp_fh1:{H}:{s}", "SO")
                    step("sy", [w("S", f"f_sq:{H}:7:{s}"), w("V", f"f_mmx:{H}:7:{s}")], dmp_fh1)
                    def dmp_fgs(e, s=s):
                        inc(e.dma_start(out=o_fgs[4*H+s], in_=GST[:, s*32:(s+1)*32]) if e else None, "SO")
                    step("sy", [w("V", f"f_hv:{H}:{s}")], dmp_fgs)

                def f_hb1w(e, s=s):
                    for cc_ in range(4):
                        inc(e.dma_start(out=qs_a[cc_].unsqueeze(1), in_=SCL[:, 48+cc_:49+cc_]) if e else None, "LD")
                    mark(f"f_hb1w:{H}:{s}", "LD")
                step("sy", [w("V", f"f_hv:{H}:{s}")], f_hb1w)

                def f_hb1(e, s=s):
                    for cc_ in range(4):
                        inc(e.dma_start(out=aq(cc_), in_=qs_a[cc_].unsqueeze(0)) if e else None, "LD")
                    mark(f"f_hb1:{H}:{s}", "LD")
                step("sy", [w("LD", f"f_hb1w:{H}:{s}")], f_hb1)

                def f_hrm1(e, s=s):
                    if e:
                        e.tensor_scalar_mul(aq(4), aq(0), 1.0 / 8192.0)
                        e.tensor_scalar_mul(aq(5), aq(1), 1.0 / 8192.0)
                        e.tensor_mul(aq(6), aq(4), aq(4))
                        ins = e.tensor_sub(aq(5), aq(5), aq(6))
                    else:
                        ins = None
                    inc(ins, "V", 1)
                    mark(f"f_hrm1:{H}:{s}", "V")
                step("ve", [w("LD", f"f_hb1:{H}:{s}")], f_hrm1)

                def f_hrs1(e, s=s):
                    if e:
                        e.activation(aq(6), aq(5), ACT.Ln, bias=EPS2, scale=1.0)
                        ins = e.activation(aq(7), aq(6), ACT.Exp, bias=0.0, scale=-0.5)
                    else:
                        ins = None
                    inc(ins, "S", 1)
                    mark(f"f_hrs1:{H}:{s}", "S")
                step("sc", [w("V", f"f_hrm1:{H}:{s}")], f_hrs1)

                def f_hrm2(e, s=s):
                    if e:
                        e.tensor_sub(aq(8), aq(2), aq(4))
                        e.tensor_sub(aq(9), aq(4), aq(3))
                        e.tensor_max(aq(10), aq(8), aq(9))
                        ins = e.tensor_mul(aq(13), aq(10), aq(7))
                    else:
                        ins = None
                    inc(ins, "V", 1)
                    mark(f"f_hrm2:{H}:{s}", "V")
                step("ve", [w("S", f"f_hrs1:{H}:{s}")], f_hrm2)

                def f_hrs2(e, s=s):
                    if e:
                        e.activation(aq(11), aq(13), ACT.Ln, bias=0.0, scale=1.0)
                        ins = e.activation(aq(12), aq(11), ACT.Exp, bias=0.0, scale=-1.0)
                    else:
                        ins = None
                    inc(ins, "S", 1)
                    mark(f"f_hrs2:{H}:{s}", "S")
                step("sc", [w("V", f"f_hrm2:{H}:{s}")], f_hrs2)

                def f_hrm3(e, s=s):
                    if e:
                        e.tensor_mul(aq(14), aq(7), aq(12))
                        e.tensor_scalar(aq(15), aq(13), SG[0:1, 5:6], None, ALU.mult)
                        e.tensor_scalar_mul(aq(13), aq(14), 128.0)
                        e.tensor_mul(aq(12), aq(4), aq(13))
                        ins = e.tensor_scalar_mul(aq(14), aq(12), -1.0)
                    else:
                        ins = None
                    inc(ins, "V", 1)
                    mark(f"f_hrm3:{H}:{s}", "V")
                step("ve", [w("S", f"f_hrs2:{H}:{s}")], f_hrm3)

                def f_hb2w(e, s=s):
                    for i_ in range(3):
                        inc(e.dma_start(out=qs_b[i_].unsqueeze(0), in_=aq(13+i_)) if e else None, "LD")
                    mark(f"f_hb2w:{H}:{s}", "LD")
                step("sy", [w("V", f"f_hrm3:{H}:{s}")], f_hb2w)

                def f_hb2(e, s=s):
                    for i_, dst in ((0, scl(24, s)), (1, scl(32, s)), (2, scl(40, s))):
                        inc(e.dma_start(out=dst, in_=qs_b[i_].unsqueeze(1)) if e else None, "LD")
                    mark(f"f_hscl:{H}:{s}", "LD")
                step("sy", [w("LD", f"f_hb2w:{H}:{s}")], f_hb2)

                for chk in range(4):
                    def f_hq(e, s=s, chk=chk):
                        hv = ARAb[:, s*8192+chk*2048:s*8192+(chk+1)*2048]
                        if e:
                            e.tensor_scalar(F_A, hv, scl(24, s), scl(32, s), ALU.mult, ALU.add)
                            e.tensor_scalar(F_B, F_A, CLIP_HI, CLIP_LO, ALU.min, ALU.max)
                            ins = e.tensor_scalar(hv, F_B, scl(40, s), None, ALU.mult)
                        else:
                            ins = None
                        inc(ins, "V", 1)
                        mark(f"f_hq:{H}:{s}:{chk}", "V")
                    fhqw = [w("LD", f"f_hscl:{H}:{s}")]
                    if dbg and chk == 0:
                        fhqw.append(w("SO", f"dmp_fh1:{H}:{s}"))
                    step("ve", fhqw, f_hq)

            for s in range(4):
                if dbg:
                    def dmp_fh2(e, s=s):
                        inc(e.dma_start(out=o_fh2[4*H+s], in_=HBs(s)) if e else None, "SO")
                        mark(f"dmp_fh2:{H}:{s}", "SO")
                    step("sy", [w("V", f"f_hq:{H}:{s}:3")], dmp_fh2)

                def f_ht(e, s=s):
                    out_v = ARBb[:, s*8192:(s+1)*8192].rearrange("p (c t) -> p c t", c=64)
                    inc(e.dma_start_transpose(out_v, ARAb[:, s*8192:(s+1)*8192])
                        if e else None, "LS")
                    mark(f"f_ht:{H}:{s}", "LS")
                step("sc", [w("V", f"f_hq:{H}:{s}:3"), w("T", f"f_mm1:{H}:7:3")], f_ht)

            PSQ = [PSA[:, 0:512], PSA[:, 512:1024], PSB[:, 0:512], PSB[:, 512:1024]]
            for dc in range(4):
                for kh in range(2):
                    pg = (dc * 2 + kh) % 2

                    def w2ld(e, dc=dc, kh=kh, pg=pg):
                        for rr in range(2):
                            r = 2 * dc + rr
                            src = cc_w2_out[r, kh*32:(kh+1)*32].rearrange("kk p m -> p kk m")
                            dst = WS2[pg].rearrange("p (kk m) -> p kk m", kk=32)[:, :, rr*256:(rr+1)*256]
                            inc(e.dma_start(out=dst, in_=src) if e else None, "LG")
                        mark(f"f_w2:{H}:{dc}:{kh}", "LG")
                    wv = [w("CC", "ag_w2"), w("LS", f"f_ht:{H}:3")]
                    if dbg and dc == 0 and kh == 0:
                        wv.append(w("SO", f"dmp_fh2:{H}:3"))
                    if dc * 2 + kh >= 2:
                        prev = dc * 2 + kh - 2
                        wv.append(w("T", f"f_mm2:{H}:{prev//2}:{prev%2}:3"))
                    step("gp", wv, w2ld)

                    for s in range(4):
                        def mm2(e, dc=dc, kh=kh, s=s, pg=pg):
                            for kk in range(32):
                                lhsT = ARBb[:, s*8192+(kh*32+kk)*128:s*8192+(kh*32+kk+1)*128]
                                ins = e.matmul(PSQ[s], lhsT,
                                               WS2[pg][:, kk*512:(kk+1)*512],
                                               start=(kh == 0 and kk == 0),
                                               stop=(kh == 1 and kk == 31)) if e else None
                                inc(ins, "T")
                            mark(f"f_mm2:{H}:{dc}:{kh}:{s}", "T")
                        wv = [w("LG", f"f_w2:{H}:{dc}:{kh}")]
                        if dc == 0 and kh == 0:
                            wv.append(w("S", f"f_gelu:{H}:7:3"))
                        if dc > 0 and kh == 0:
                            wv.append(w("V", f"f_oc:{H}:{dc-1}:{s}"))
                        step("te", wv, mm2)

                for s in range(4):
                    pg2 = (dc * 4 + s) % 2

                    def x2c_ld(e, dc=dc, s=s, pg2=pg2):
                        inc(e.dma_start(out=X2C[:, pg2*512:(pg2+1)*512],
                                        in_=x2_scr[4*H+s][:, dc*512:(dc+1)*512]) if e else None, "LD")
                        mark(f"f_x2c:{H}:{dc}:{s}", "LD")
                    wv = [w("LD", f"x2st:{4*H+s}"), w("V", f"f_hq:{H}:3:3")]
                    if dc * 4 + s >= 2:
                        prev = dc * 4 + s - 2
                        wv.append(w("V", f"f_oc:{H}:{prev//4}:{prev%4}"))
                    step("sy", wv, x2c_ld)

                    def f_oc(e, dc=dc, s=s, pg2=pg2):
                        ins = e.tensor_add(OUTP[:, pg2*512:(pg2+1)*512], PSQ[s],
                                           X2C[:, pg2*512:(pg2+1)*512]) if e else None
                        inc(ins, "V", 1)
                        mark(f"f_oc:{H}:{dc}:{s}", "V")
                    ocw = [w("T", f"f_mm2:{H}:{dc}:1:{s}"), w("LD", f"f_x2c:{H}:{dc}:{s}")]
                    if dc * 4 + s >= 2:
                        prev = dc * 4 + s - 2
                        ocw.append(w("SO", f"f_ost:{H}:{prev//4}:{prev%4}"))
                    step("ve", ocw, f_oc)

                    def f_ost(e, dc=dc, s=s, pg2=pg2):
                        inc(e.dma_start(out=o_out[4*H+s][:, dc*512:(dc+1)*512],
                                        in_=OUTP[:, pg2*512:(pg2+1)*512]) if e else None, "SO")
                        mark(f"f_ost:{H}:{dc}:{s}", "SO")
                    step("sy", [w("V", f"f_oc:{H}:{dc}:{s}")], f_ost)

        p0()
        wave_f1(0)
        wave_f2(0)
        for gw in range(4):
            wave(gw)
        if not noffn:
            ffn(0)
        wave_f1(4)
        wave_f2(4)
        for gw in range(4, 8):
            wave(gw)
        if not noffn:
            ffn(1)

        def fin(e):
            mark("final_out", "SO")
        step("sy", [w("SO", "f_ost:1:3:3") if not noffn else w("LD", "x2st:7")], fin)

        for eng, waits, fn in steps:
            fn(None)
        for eng, waits, fn in steps:
            for (sem, key, d) in waits:
                assert key in M, f"wait on undefined mark {key}"
                assert M[key][0] == sem, f"wait sem mismatch: {key} marked on {M[key][0]}, waited on {sem}"
        mode["dry"] = False

        def run_engine(e, name):
            for k in cnt:
                pass
            for eng, waits, fn in steps:
                if eng != name:
                    continue
                for (sem, key, d) in waits:
                    e.wait_ge(sems[sem], M[key] + d)
                fn(e)

        emitted = {"done": False}

        def run_all(engines):
            for k in cnt:
                cnt[k] = 0
            for eng, waits, fn in steps:
                e = engines.get(eng)
                if e is not None:
                    for (sem, key, d) in waits:
                        msem, mval = M[key]
                        assert msem == sem, (key, msem, sem)
                        e.wait_ge(sems[sem], mval + d)
                fn(e)

        @blk.sync
        def _(eng):
            run_all({"sy": eng})

        @blk.scalar
        def _(eng):
            run_all({"sc": eng})

        @blk.vector
        def _(eng):
            run_all({"ve": eng})

        @blk.tensor
        def _(eng):
            run_all({"te": eng})

        @blk.gpsimd
        def _(eng):
            run_all({"gp": eng})

    return nc


def make_inmaps(x, conv_w, conv_b, xproj_w, dtproj_w, dtproj_b, A_log, D_param,
                w_fc, w_proj):
    f = np.float32
    x = np.asarray(x, f)
    xp3 = np.concatenate([np.zeros((2, 3, D), f), x], axis=1)
    xpwT = np.ascontiguousarray(np.asarray(xproj_w, f).T)
    w_fcT = np.ascontiguousarray(np.asarray(w_fc, f).T)
    w_projT = np.ascontiguousarray(np.asarray(w_proj, f).T)
    ims = []
    for c in range(8):
        d0 = 256 * c
        xsc = np.empty((NW, P, NDT, 1027), f)
        xpt = np.empty((NW, P, 16, P), f)
        for gw in range(NW):
            beta, ww = gw % 2, gw // 2
            blkx = xp3[beta, 1024*ww:1024*ww+1027, d0:d0+256]
            xsc[gw] = blkx.T.reshape(NDT, P, 1027).transpose(1, 0, 2)
            stri = x[beta, 1024*ww+128*c:1024*ww+128*c+128, :]
            xpt[gw] = stri.T.reshape(16, P, P).transpose(1, 0, 2)
        ws1 = w_fcT[:, 1024*c:1024*(c+1)].reshape(16, P, 1024).transpose(1, 0, 2)
        ws2 = w_projT[:, 256*c:256*(c+1)].reshape(64, P, 256).transpose(1, 0, 2)
        ims.append({
            "xsc": np.ascontiguousarray(xsc),
            "xpt": np.ascontiguousarray(xpt),
            "ws1": np.ascontiguousarray(ws1),
            "ws2": np.ascontiguousarray(ws2),
            "xpw": np.ascontiguousarray(xpwT[d0:d0+256].reshape(NDT, P, 160).transpose(1, 0, 2)),
            "dtw": np.ascontiguousarray(
                np.asarray(dtproj_w, f)[d0:d0+256, :].T.reshape(P, NDT, P)),
            "dtb": np.ascontiguousarray(np.asarray(dtproj_b, f)[d0:d0+256].reshape(NDT, P).T),
            "alog": np.ascontiguousarray(np.asarray(A_log, f)[d0:d0+256].reshape(NDT, P, NN).transpose(1, 0, 2)),
            "cwv": np.ascontiguousarray(np.asarray(conv_w, f)[d0:d0+256, 0, :].reshape(NDT, P, 4).transpose(1, 0, 2)),
            "cbv": np.ascontiguousarray(np.asarray(conv_b, f)[d0:d0+256].reshape(NDT, P).T),
            "dpv": np.ascontiguousarray(np.asarray(D_param, f)[d0:d0+256].reshape(NDT, P).T),
            "onesf": np.ones((P, 1), f),
            "cvals": np.broadcast_to(np.array([1e-5, 1e-6, -1.0, -0.5, 0, 0, 0, 0], f), (P, 8)).copy(),
            "identf": np.eye(P, dtype=f),
        })
    return ims


def kernel(L0_attn_x, L0_attn_conv_w, L0_attn_conv_b, L0_attn_xproj_w,
           L0_attn_dtproj_w, L0_attn_dtproj_b, L0_attn_A_log, L0_attn_D_param,
           L0_ffn_w_fc, L0_ffn_w_proj, dbg=False, noffn=False):
    ims = make_inmaps(L0_attn_x, L0_attn_conv_w, L0_attn_conv_b, L0_attn_xproj_w,
                      L0_attn_dtproj_w, L0_attn_dtproj_b, L0_attn_A_log,
                      L0_attn_D_param, L0_ffn_w_fc, L0_ffn_w_proj)
    nc = build(dbg=dbg, noffn=noffn)
    res = run_bass_kernel_spmd(nc, ims, core_ids=list(range(8)))
    out = np.empty((2, 4096, D), np.float32)
    for c in range(8):
        r = res.results[c]
        for gw in range(NW):
            beta, ww = gw % 2, gw // 2
            out[beta, 1024*ww+128*c:1024*ww+128*c+128, :] = r["out"][gw]
    return out


# version 06
# speedup vs baseline: 1.1090x; 1.0415x over previous
import numpy as np
import ml_dtypes
import concourse.bass as bass
import concourse.mybir as mybir
from concourse.bass_utils import run_bass_kernel_spmd

F32 = mybir.dt.float32
BF16 = mybir.dt.bfloat16
I8 = mybir.dt.int8
ALU = mybir.AluOpType
ACT = mybir.ActivationFunctionType
AX = mybir.AxisListType

P = 128
NW = 8
TW = 1024
NDT = 2
NN = 16
D = 2048
M1 = 8192
Q_B = 128.0
CLIP_HI = float(np.float32(np.float32(128.0) - np.float32(1e-6)))
CLIP_LO = -CLIP_HI
EPS1 = float(np.float32(1e-5))
EPS2 = float(np.float32(1e-6))
GEPS = float(np.float32(1e-6))
CONST_VALS = [EPS1, EPS2, -1.0, -0.5]


def build(dbg=False, noffn=False):
    nc = bass.Bass("TRN2", target_bir_lowering=False)

    i_xsc = nc.dram_tensor("xsc", [NW, P, NDT, 1027], F32, kind="ExternalInput").ap()
    i_xpt = nc.dram_tensor("xpt", [NW, P, 16, P], F32, kind="ExternalInput").ap()
    i_ws1 = nc.dram_tensor("ws1", [P, 16, 1024], F32, kind="ExternalInput").ap()
    i_ws2 = nc.dram_tensor("ws2", [P, 64, 256], F32, kind="ExternalInput").ap()
    i_xpw = nc.dram_tensor("xpw", [P, NDT, 160], F32, kind="ExternalInput").ap()
    i_dtw = nc.dram_tensor("dtw", [P, NDT, P], F32, kind="ExternalInput").ap()
    i_dtb = nc.dram_tensor("dtb", [P, NDT], F32, kind="ExternalInput").ap()
    i_alog = nc.dram_tensor("alog", [P, NDT, NN], F32, kind="ExternalInput").ap()
    i_cw = nc.dram_tensor("cwv", [P, NDT, 4], F32, kind="ExternalInput").ap()
    i_cb = nc.dram_tensor("cbv", [P, NDT], F32, kind="ExternalInput").ap()
    i_dp = nc.dram_tensor("dpv", [P, NDT], F32, kind="ExternalInput").ap()
    i_ones = nc.dram_tensor("onesf", [P, 1], F32, kind="ExternalInput").ap()
    i_cvals = nc.dram_tensor("cvals", [P, 8], F32, kind="ExternalInput").ap()
    i_id = nc.dram_tensor("identf", [P, P], F32, kind="ExternalInput").ap()

    o_out = nc.dram_tensor("out", [NW, P, D], F32, kind="ExternalOutput").ap()
    o_x2 = nc.dram_tensor("dbg_x2", [NW, P, D], F32, kind="ExternalOutput").ap() if dbg else None
    o_dst = nc.dram_tensor("dbg_st", [NW, 1, 2048], F32, kind="ExternalOutput").ap() if dbg else None
    o_dxh = nc.dram_tensor("dbg_xh", [NW, P, 2054], F32, kind="ExternalOutput").ap() if dbg else None
    o_dxc = nc.dram_tensor("dbg_xc", [NW, P, 2048], F32, kind="ExternalOutput").ap() if dbg else None
    o_ddl = nc.dram_tensor("dbg_dl", [NW, P, 2048], F32, kind="ExternalOutput").ap() if dbg else None
    o_dyy = nc.dram_tensor("dbg_yy", [NW, P, 2048], F32, kind="ExternalOutput").ap() if dbg else None
    o_dpj = nc.dram_tensor("dbg_pj", [NW, P, 1024], F32, kind="ExternalOutput").ap() if dbg else None
    o_dbo = nc.dram_tensor("dbg_bo", [NW, 160, 1024], F32, kind="ExternalOutput").ap() if dbg else None
    o_dab = nc.dram_tensor("dbg_ab", [NW, P, 4096], F32, kind="ExternalOutput").ap() if dbg else None
    o_dhc = nc.dram_tensor("dbg_hc", [NW, P, 2048+64], F32, kind="ExternalOutput").ap() if dbg else None
    o_ds2 = nc.dram_tensor("dbg_s2", [NW, 2, P, 64], F32, kind="ExternalOutput").ap() if dbg else None
    o_dh0 = nc.dram_tensor("dbg_h0", [NW, P, 6144], F32, kind="ExternalOutput").ap() if dbg else None
    o_ds3 = nc.dram_tensor("dbg_s3", [NW, P, 64], F32, kind="ExternalOutput").ap() if dbg else None
    o_dxq = nc.dram_tensor("dbg_xq", [NW, P, 2048], BF16, kind="ExternalOutput").ap() if dbg else None
    o_fh1 = nc.dram_tensor("dbg_fh1", [NW, P, M1], BF16, kind="ExternalOutput").ap() if dbg else None
    o_fh2 = nc.dram_tensor("dbg_fh2", [NW, P, M1], BF16, kind="ExternalOutput").ap() if dbg else None
    o_fgs = nc.dram_tensor("dbg_fgs", [NW, P, 32], F32, kind="ExternalOutput").ap() if dbg else None
    o_dscl = nc.dram_tensor("dbg_scl", [NW, P, 64], F32, kind="ExternalOutput").ap() if dbg else None
    o_daq = nc.dram_tensor("dbg_aq", [NW, 1, 2048], F32, kind="ExternalOutput").ap() if dbg else None
    o_dw1 = nc.dram_tensor("dbg_w1", [8, 16, P, 1024], I8, kind="ExternalOutput").ap() if dbg else None
    o_dth = nc.dram_tensor("dbg_th", [P, 6], F32, kind="ExternalOutput").ap() if dbg else None

    cc_a_in = nc.dram_tensor("cc_a_in", [2, TW], F32, kind="Internal").ap()
    cc_a_out = nc.dram_tensor("cc_a_out", [2, TW], F32, kind="Internal").ap()
    cc_b_in = nc.dram_tensor("cc_b_in", [160, TW], F32, kind="Internal").ap()
    cc_b_out = nc.dram_tensor("cc_b_out", [160, TW], F32, kind="Internal").ap()
    cc_y_in = nc.dram_tensor("cc_y_in", [256, TW], BF16, kind="Internal").ap()
    cc_y_out = nc.dram_tensor("cc_y_out", [8, 256, TW], BF16, kind="Internal").ap()
    cc_g_in = nc.dram_tensor("cc_g_in", [1, 2], F32, kind="Internal").ap()
    cc_g_out = nc.dram_tensor("cc_g_out", [8, 1, 2], F32, kind="Internal").ap()
    cc_w1_in = nc.dram_tensor("cc_w1_in", [16, P, 1024], I8, kind="Internal").ap()
    cc_w1_out = nc.dram_tensor("cc_w1_out", [8, 16, P, 1024], I8, kind="Internal").ap()
    cc_w2_in = nc.dram_tensor("cc_w2_in", [64, P, 256], I8, kind="Internal").ap()
    cc_w2_out = nc.dram_tensor("cc_w2_out", [8, 64, P, 256], I8, kind="Internal").ap()
    st_scr = nc.dram_tensor("st_scr", [NW, 2, TW], F32, kind="Internal").ap()
    g_scr = nc.dram_tensor("g_scr", [1, 8], F32, kind="Internal").ap()
    x2_scr = nc.dram_tensor("x2_scr", [NW, P, D], F32, kind="Internal").ap()
    qs_a = nc.dram_tensor("qs_a", [4, P], F32, kind="Internal").ap()
    qs_b = nc.dram_tensor("qs_b", [3, P], F32, kind="Internal").ap()

    import contextlib
    with contextlib.ExitStack() as _st:
        STATE_t = _st.enter_context(nc.sbuf_tensor("STATE", [P, 64], F32))
        ARA_t = _st.enter_context(nc.sbuf_tensor("ARA", [P, 16384], F32))
        ARB_t = _st.enter_context(nc.sbuf_tensor("ARB", [P, 16384], F32))
        XQS_t = _st.enter_context(nc.sbuf_tensor("XQS", [P, 16384], BF16))
        FSC_t = _st.enter_context(nc.sbuf_tensor("FSC", [P, 4096], F32))
        GST_t = _st.enter_context(nc.sbuf_tensor("GST", [P, 128], F32))
        SCL_t = _st.enter_context(nc.sbuf_tensor("SCL", [P, 64], F32))
        GRD1_t = _st.enter_context(nc.sbuf_tensor("GRD1", [P, 64], F32))
        GRD2_t = _st.enter_context(nc.sbuf_tensor("GRD2", [P, 64], F32))
        CON_t = _st.enter_context(nc.sbuf_tensor("CON", [P, 256], F32))
        CONB_t = _st.enter_context(nc.sbuf_tensor("CONB", [P, 640], BF16))
        SG_t = _st.enter_context(nc.sbuf_tensor("SG", [1, 32], F32))
        STT2_t = _st.enter_context(nc.sbuf_tensor("STT2", [1, 6144], F32))
        PSA_t = _st.enter_context(nc.psum_tensor("PSA", [P, 1024], F32))
        PSB_t = _st.enter_context(nc.psum_tensor("PSB", [P, 1024], F32))
        PSC_t = _st.enter_context(nc.psum_tensor("PSC", [P, 1024], F32))
        PSD_t = _st.enter_context(nc.psum_tensor("PSD", [P, 1024], F32))
        h_LD = _st.enter_context(nc.semaphore("sLD"))
        h_LS = _st.enter_context(nc.semaphore("sLS"))
        h_LG = _st.enter_context(nc.semaphore("sLG"))
        h_CC = _st.enter_context(nc.semaphore("sCC"))
        h_V = _st.enter_context(nc.semaphore("sV"))
        h_S = _st.enter_context(nc.semaphore("sS"))
        h_T = _st.enter_context(nc.semaphore("sT"))
        h_SO = _st.enter_context(nc.semaphore("sSO"))
        blk = _st.enter_context(nc.Block())

        ARA = ARA_t[:]
        ARB = ARB_t[:]
        ARAb = ARA.bitcast(BF16)
        ARBb = ARB.bitcast(BF16)
        ARAi = ARA.bitcast(I8)
        ARBi = ARB.bitcast(I8)
        XQS = XQS_t[:]
        FSC = FSC_t[:]
        GST = GST_t[:]
        SCL = SCL_t[:]
        STATE = STATE_t[:]
        CON = CON_t[:]
        CONB = CONB_t[:]
        SG = SG_t[:]
        PSA, PSB, PSC, PSD = PSA_t[:], PSB_t[:], PSC_t[:], PSD_t[:]
        STT2 = STT2_t[:]
        GRD1, GRD2 = GRD1_t[:], GRD2_t[:]
        c_idb = GRD1.bitcast(BF16)[:, 0:128]
        def stb(i):
            return STT2[0:1, i*1024:(i+1)*1024]
        def aq(i):
            return STT2[0:1, 4096+i*128:4096+(i+1)*128]
        YASB = ARBb[:, 2*8192:2*8192+2048]
        XPT = ARB[:, 9216:11264]
        OUTP = FSC[:, 0:1024]
        X2C = FSC[:, 1024:2048]

        sems = {"LD": h_LD, "LS": h_LS, "LG": h_LG, "CC": h_CC,
                "V": h_V, "S": h_S, "T": h_T, "SO": h_SO}

        c_ones = CON[:, 0:1]
        c_id = CON[:, 1:129]
        c_A = CON[:, 129:161]
        c_cw = CON[:, 161:169]
        c_cb = CON[:, 169:171]
        c_dp = CON[:, 171:173]
        c_dtb = CON[:, 173:175]
        c_TH = CON[:, 175:181]
        c_Aexp = CON[:, 181:213]
        c_cvals = CON[:, 213:221]
        for _i, _v in enumerate(CONST_VALS):
            nc.const_aps.register(F32, float(_v), c_cvals[:, _i:_i+1])
        cb_xpw = CONB[:, 0:320]
        cb_dtw = CONB[:, 320:576]

        XHAT = ARA[:, 0:2054]
        def XHATd(dt): return ARA[:, dt*1027:(dt+1)*1027]
        def XSQd(dt): return FSC[:, dt*1024:(dt+1)*1024]
        XHs = ARA[:, 2054:4108]
        def XHsd(dt): return ARA[:, 2054+dt*1027:2054+(dt+1)*1027]
        def XCd(dt): return ARA[:, 4108+dt*1024:4108+(dt+1)*1024]
        XC = ARA[:, 4108:6156]
        def DLTd(dt): return ARA[:, 6156+dt*1024:6156+(dt+1)*1024]
        def UUd(dt): return ARA[:, 8204+dt*1024:8204+(dt+1)*1024]
        def YYd(dt): return ARA[:, 10252+dt*1024:10252+(dt+1)*1024]
        YY = ARA[:, 10252:12300]
        CV1 = ARA[:, 12300:13324]
        CV2 = ARA[:, 13324:14348]
        SP1 = CV1
        SP2 = CV2
        XCB = ARAb[:, 2*14348:2*14348+2048]
        YYB = XCB
        DTI = ARAb[:, 2*15372:2*15372+1024]
        A_X2 = ARB[:, 2048:4096]
        A_SQ = ARB[:, 4096:6144]
        A_XQB = ARBb[:, 12288:14336]
        A_X2DT = YY

        AT = [ARB[:, 0:1024], ARB[:, 1024:2048]]
        BT = [ARB[:, 2048:3072], ARB[:, 3072:4096]]
        HT = [ARB[:, 4096:5120], ARB[:, 5120:6144]]
        HCT = [ARBb[:, 12288:13312], ARBb[:, 13312:14336]]
        BRO = [ARB[:, 8192:9216], ARB[:, 9216:10240]]
        CRO = [ARB[:, 10240:11264], ARB[:, 11264:12288]]
        MB = ARB[:, 12288:13315]
        RB = ARB[:, 13315:14342]
        SB0 = 14342
        SROW = lambda r: ARB[r:r+1, SB0:SB0+1024]
        SROW2 = ARB[2:4, SB0:SB0+1024]
        PJ1 = ARB[:, 0:1024]
        PJ2 = ARB[0:32, 1024:2048]

        def HBs(s): return ARAb[:, s*8192:(s+1)*8192]
        def HBc(s, o): return ARAb[:, s*8192+o*1024:s*8192+(o+1)*1024]
        WS = [ARBb[:, 0:16384], ARBb[:, 16384:32768]]
        def HQTt(s, c): return ARBb[:, s*8192+c*128:s*8192+(c+1)*128]
        WS2 = [ARAb[:, 0:16384], ARAb[:, 16384:32768]]
        F_A = FSC[:, 0:2048]
        F_B = FSC[:, 2048:4096]
        F_A1 = FSC[:, 0:1024]

        def scl(base, i): return SCL[:, base+i:base+i+1]

        steps = []
        M = {}
        cnt = {k: 0 for k in sems}
        mode = {"dry": True}

        class IncProxy:
            pass

        def inc(ins, sem, n=None):
            if n is None:
                n = 16 if sem in ("LD", "LS", "LG", "SO") else 1
            cnt[sem] += n
            if ins is not None:
                ins.then_inc(sems[sem], n)

        def mark(key, sem):
            if mode["dry"]:
                assert key not in M, f"dup mark {key}"
                M[key] = (sem, cnt[sem])
            else:
                assert M[key] == (sem, cnt[sem]), (key, M[key], sem, cnt[sem])

        def step(eng, waits, fn):
            if not mode["dry"]:
                pass
            steps.append((eng, list(waits), fn))

        def w(sem, key, delta=0):
            return (sem, key, delta)

        hooks = {}

        def p0():
            def ld(e):
                pairs = [
                    (ARA[:, 0:16384].rearrange("p (k m) -> p k m", k=16), i_ws1),
                    (c_ones, i_ones),
                    (c_cvals, i_cvals),
                    (c_id, i_id),
                    (FSC[:, 576:608].rearrange("p (a n) -> p a n", a=2), i_alog),
                    (c_cw.rearrange("p (a j) -> p a j", a=2), i_cw),
                    (c_cb, i_cb),
                    (c_dp, i_dp),
                    (c_dtb, i_dtb),
                    (FSC[:, 0:320].rearrange("p (a q) -> p a q", a=2), i_xpw),
                    (FSC[:, 320:576].rearrange("p (a f) -> p a f", a=2), i_dtw),
                ]
                for o_, i_ in pairs:
                    inc(e.dma_start(out=o_, in_=i_) if e else None, "LD")
                mark("p0_loads", "LD")
            step("sy", [], ld)

            def ldB(e):
                inc(e.dma_start(out=ARB[:, 0:16384].rearrange("p (k m) -> p k m", k=64),
                                in_=i_ws2) if e else None, "LS")
                mark("p0_loadB", "LS")
            step("sc", [], ldB)

            def vconsts(e):
                inc(e.tensor_copy(cb_xpw, FSC[:, 0:320]) if e else None, "V")
                inc(e.tensor_copy(cb_dtw, FSC[:, 320:576]) if e else None, "V")
                inc(e.memset(STATE[:, 0:64], 0.0) if e else None, "V")
                inc(e.tensor_copy(c_idb, c_id) if e else None, "V")
                inc(e.memset(GRD2[:, 0:64], 7.0) if e else None, "V")
                inc(e.reduce_sum(SCL[:, 60:61], ARA[:, 0:16384], axis=AX.X,
                                 apply_absolute_value=True) if e else None, "V")
                inc(e.reduce_sum(SCL[:, 61:62], ARB[:, 0:16384], axis=AX.X,
                                 apply_absolute_value=True) if e else None, "V")
                mark("p0_abs", "V")
            step("ve", [w("LD", "p0_loads"), w("LS", "p0_loadB")], vconsts)

            def sexp(e):
                inc(e.activation(c_Aexp, FSC[:, 576:608], ACT.Exp, bias=0.0, scale=1.0)
                    if e else None, "S")
                mark("p0_aexp", "S")
            step("sc", [w("LD", "p0_loads")], sexp)

            def vneg(e):
                inc(e.tensor_scalar_mul(c_A, c_Aexp, -1.0) if e else None, "V")
                mark("p0_A", "V")
            step("ve", [w("S", "p0_aexp")], vneg)

            def gmm(e):
                ins = e.matmul(PSA[0:1, 0:2], c_ones, SCL[:, 60:62], start=True, stop=True) if e else None
                inc(ins, "T")
                mark("p0_gmm", "T")
            step("te", [w("V", "p0_abs")], gmm)

            def gevac(e):
                inc(e.copy(SG[0:1, 0:2], PSA[0:1, 0:2]) if e else None, "S")
                mark("p0_gpart", "S")
            step("sc", [w("T", "p0_gmm")], gevac)

            def gin(e):
                inc(e.dma_start(out=cc_g_in[:], in_=SG[0:1, 0:2]) if e else None, "LD")
                mark("p0_gin", "LD")
            step("sy", [w("S", "p0_gpart")], gin)

            def gag(e):
                ins = e.collective_compute("AllGather", ALU.bypass,
                                           replica_groups=[list(range(8))],
                                           ins=[cc_g_in[:]], outs=[cc_g_out[:]]) if e else None
                inc(ins, "CC")
                mark("ag_g", "CC")
            step("gp", [w("LD", "p0_gin")], gag)

            def gall(e):
                inc(e.dma_start(out=SG[0:1, 16:32],
                                in_=cc_g_out.rearrange("r o c -> o (r c)")) if e else None, "LD")
                mark("p0_gall", "LD")
            step("sy", [w("CC", "ag_g")], gall)

            def gmath(e):
                if e:
                    e.reduce_sum(SG[0:1, 8:9], SG[0:1, 16:32].rearrange("o (r c) -> o r c", c=2)[:, :, 0], axis=AX.X)
                    e.reduce_sum(SG[0:1, 9:10], SG[0:1, 16:32].rearrange("o (r c) -> o r c", c=2)[:, :, 1], axis=AX.X)
                    e.tensor_scalar_mul(SG[0:1, 8:9], SG[0:1, 8:9], 1.0 / (8192.0 * 2048.0))
                    e.tensor_scalar_mul(SG[0:1, 9:10], SG[0:1, 9:10], 1.0 / (8192.0 * 2048.0))
                    e.tensor_scalar(SG[0:1, 0:1], SG[0:1, 8:9], GEPS, 0.5, ALU.add, ALU.mult)
                    e.tensor_scalar(SG[0:1, 2:3], SG[0:1, 9:10], GEPS, 0.5, ALU.add, ALU.mult)
                    e.tensor_scalar(SG[0:1, 1:2], SG[0:1, 8:9], GEPS, -0.5, ALU.add, ALU.mult)
                    e.tensor_scalar(SG[0:1, 3:4], SG[0:1, 9:10], GEPS, -0.5, ALU.add, ALU.mult)
                    e.tensor_scalar_mul(SG[0:1, 4:5], SG[0:1, 8:9], 1.0 / 128.0)
                    ins = e.tensor_scalar_mul(SG[0:1, 5:6], SG[0:1, 9:10], 1.0 / 128.0)
                else:
                    ins = None
                inc(ins, "V")
                mark("p0_gmath", "V")
            step("ve", [w("LD", "p0_gall")], gmath)

            def gscr_w(e):
                inc(e.dma_start(out=g_scr[:], in_=SG[0:1, 0:8]) if e else None, "LD")
                mark("p0_gscr", "LD")
            step("sy", [w("V", "p0_gmath")], gscr_w)

            def thb(e):
                inc(e.dma_start(out=c_TH, in_=g_scr[0:1, 0:6].squeeze(0).partition_broadcast(P))
                    if e else None, "LS")
                mark("p0_th", "LS")
            step("sc", [w("LD", "p0_gscr")], thb)

            for mat in range(2):
                src_f = ARA if mat == 0 else ARB
                dst_i = ARAi if mat == 0 else ARBi
                th_ap = c_TH[:, 2*mat:2*mat+1]
                nth_ap = c_TH[:, 2*mat+1:2*mat+2]
                cc_w = cc_w1_in if mat == 0 else cc_w2_in
                for ch in range(8):
                    def tern_v(e, src_f=src_f, th_ap=th_ap, nth_ap=nth_ap, mat=mat, ch=ch):
                        chunk = src_f[:, ch*2048:(ch+1)*2048]
                        if e:
                            e.tensor_scalar(F_A, chunk, th_ap, nth_ap, ALU.min, ALU.max)
                            ins = e.tensor_sub(F_B, chunk, F_A)
                        else:
                            ins = None
                        inc(ins, "V", 1)
                        mark(f"tern_d:{mat}:{ch}", "V")
                    tvw = [w("LS", "p0_th")]
                    if mat == 0 and ch == 0:
                        tvw.append(w("S", "p0_aexp"))
                    else:
                        pm, pc = (mat, ch - 1) if ch > 0 else (mat - 1, 7)
                        tvw.append(w("S", f"tern:{pm}:{pc}"))
                    step("ve", tvw, tern_v)

                    def tern_s(e, dst_i=dst_i, mat=mat, ch=ch):
                        dst = dst_i[:, ch*8192:ch*8192+2048]
                        inc(e.activation(dst, F_B, ACT.Sign, bias=0.0, scale=1.0) if e else None, "S")
                        mark(f"tern:{mat}:{ch}", "S")
                    step("sc", [w("V", f"tern_d:{mat}:{ch}")], tern_s)

                    def tern_st(e, dst_i=dst_i, cc_w=cc_w, mat=mat, ch=ch):
                        kpc = 2 if mat == 0 else 8
                        src = dst_i[:, ch*8192:ch*8192+2048].rearrange("p (k m) -> p k m", k=kpc)
                        dst = cc_w[kpc*ch:kpc*(ch+1)].rearrange("k p m -> p k m")
                        inc(e.dma_start(out=dst, in_=src) if e else None, "LD")
                        mark(f"ternst:{mat}:{ch}", "LD")
                    step("sy", [w("S", f"tern:{mat}:{ch}")], tern_st)

            def emit_agw1():
                def agw1(e):
                    ins = e.collective_compute("AllGather", ALU.bypass,
                                               replica_groups=[list(range(8))],
                                               ins=[cc_w1_in[:]], outs=[cc_w1_out[:]]) if e else None
                    inc(ins, "CC")
                    mark("ag_w1", "CC")
                step("gp", [w("LD", "ternst:0:7")], agw1)
            hooks["agw1"] = emit_agw1

            def emit_agw2():
                def agw2(e):
                    ins = e.collective_compute("AllGather", ALU.bypass,
                                               replica_groups=[list(range(8))],
                                               ins=[cc_w2_in[:]], outs=[cc_w2_out[:]]) if e else None
                    inc(ins, "CC")
                    mark("ag_w2", "CC")
                step("gp", [w("LD", "ternst:1:7")], agw2)
                if dbg:
                    def dmp_w1(e):
                        inc(e.dma_start(out=o_dw1[:], in_=cc_w1_out[:]) if e else None, "SO")
                    step("sy", [w("CC", "ag_w1")], dmp_w1)
                    def dmp_th(e):
                        inc(e.dma_start(out=o_dth[:], in_=c_TH) if e else None, "SO")
                    step("sy", [w("LS", "p0_th")], dmp_th)
            hooks["agw2"] = emit_agw2

        def wave_f1(gw):
            beta, ww = gw % 2, gw // 2

            def xsc_ld(e):
                inc(e.dma_start(out=ARA[:, 0:2054].rearrange("p (a t) -> p a t", a=2),
                                in_=i_xsc[gw]) if e else None, "LD")
                mark(f"xsc:{gw}", "LD")
            if gw == 0:
                waits = [w("LD", "ternst:1:7")]
            elif gw % 4 == 0:
                waits = [w("V", f"a_q2:{gw-1}"), w("LS", f"xqt:{gw-1}")]
                if not noffn:
                    waits.append(w("V", "f_oc:0:3:3"))
                    waits.append(w("SO", "f_ost:0:3:3"))
            else:
                waits = [w("V", f"yev:{gw-1}:1")]
            step("sy", waits, xsc_ld)

            def xsq(e):
                for dt in range(NDT):
                    inc(e.activation(XSQd(dt), ARA[:, dt*1027+3:dt*1027+1027], ACT.Square,
                                     bias=0.0, scale=1.0) if e else None, "S")
                mark(f"xsq:{gw}", "S")
            sw = [w("LD", f"xsc:{gw}")]
            if gw == 0:
                sw.append(w("S", "tern:1:7"))
            step("sc", sw, xsq)

            def smm(e):
                for hf in range(2):
                    for dt in range(NDT):
                        ins = e.matmul(PSA[0:1, hf*512:(hf+1)*512], c_ones,
                                       ARA[:, dt*1027+3+hf*512:dt*1027+3+(hf+1)*512],
                                       start=(dt == 0), stop=(dt == 1)) if e else None
                        inc(ins, "T")
                for hf in range(2):
                    for dt in range(NDT):
                        ins = e.matmul(PSB[0:1, hf*512:(hf+1)*512], c_ones,
                                       FSC[:, dt*1024+hf*512:dt*1024+(hf+1)*512],
                                       start=(dt == 0), stop=(dt == 1)) if e else None
                        inc(ins, "T")
                mark(f"stat_mm:{gw}", "T")
            tw_ = [w("S", f"xsq:{gw}"), w("LD", f"xsc:{gw}")]
            if gw == 0:
                tw_.append(w("S", "p0_gpart"))
            else:
                tw_.append(w("S", f"pj_evac:{gw-1}"))
            step("te", tw_, smm)

            def sevac(e):
                inc(e.copy(stb(0), PSA[0:1, 0:1024]) if e else None, "S")
                inc(e.copy(stb(1), PSB[0:1, 0:1024]) if e else None, "S")
                mark(f"stat_evac:{gw}", "S")
            sw2 = [w("T", f"stat_mm:{gw}")]
            if gw == 0:
                sw2.append(w("LD", "ternst:1:7"))
            step("sc", sw2, sevac)

            def apack(e):
                inc(e.dma_start(out=cc_a_in.rearrange("a t -> (a t)").unsqueeze(0),
                                in_=STT2[0:1, 0:2048]) if e else None, "LD")
                mark(f"apack:{gw}", "LD")
            step("sy", [w("S", f"stat_evac:{gw}")], apack)

            def ar_a(e):
                ins = e.collective_compute("AllReduce", ALU.add,
                                           replica_groups=[list(range(8))],
                                           ins=[cc_a_in[:]], outs=[cc_a_out[:]]) if e else None
                inc(ins, "CC")
                mark(f"ar_a:{gw}", "CC")
            step("gp", [w("LD", f"apack:{gw}")], ar_a)

        def wave_f2(gw):
            beta, ww = gw % 2, gw // 2

            def aload(e):
                inc(e.dma_start(out=STT2[0:1, 0:2048],
                                in_=cc_a_out.rearrange("a t -> (a t)").unsqueeze(0)) if e else None, "LD")
                mark(f"aload:{gw}", "LD")
            step("sy", [w("CC", f"ar_a:{gw}")], aload)

            def statv(e):
                if e:
                    e.tensor_scalar_mul(stb(2), stb(0), 1.0 / 2048.0)
                    e.tensor_scalar_mul(stb(3), stb(1), 1.0 / 2048.0)
                    e.tensor_mul(stb(0), stb(2), stb(2))
                    ins = e.tensor_sub(stb(1), stb(3), stb(0))
                else:
                    ins = None
                inc(ins, "V", 1)
                mark(f"stat_v:{gw}", "V")
            step("ve", [w("LD", f"aload:{gw}")], statv)

            def rstd(e):
                if e:
                    e.activation(stb(0), stb(1), ACT.Ln, bias=EPS1, scale=1.0)
                    ins = e.activation(stb(3), stb(0), ACT.Exp, bias=0.0, scale=-0.5)
                else:
                    ins = None
                inc(ins, "S", 1)
                mark(f"rstd:{gw}", "S")
            step("sc", [w("V", f"stat_v:{gw}")], rstd)

            if dbg:
                def dmp_st(e):
                    inc(e.dma_start(out=o_dst[gw], in_=STT2[0:1, 2048:4096]) if e else None, "SO")
                step("sy", [w("S", f"rstd:{gw}")], dmp_st)

            def stscr_w(e):
                inc(e.dma_start(out=st_scr[gw].rearrange("a t -> (a t)").unsqueeze(0),
                                in_=STT2[0:1, 2048:4096]) if e else None, "LD")
                mark(f"stscr:{gw}", "LD")
            step("sy", [w("S", f"rstd:{gw}")], stscr_w)

            def bcast_mr(e):
                inc(e.dma_start(out=MB[:, 3:1027],
                                in_=st_scr[gw][0:1, :].squeeze(0).partition_broadcast(P))
                    if e else None, "LS")
                inc(e.dma_start(out=RB[:, 3:1027],
                                in_=st_scr[gw][1:2, :].squeeze(0).partition_broadcast(P))
                    if e else None, "LS")
                if ww > 0:
                    inc(e.dma_start(out=MB[:, 0:3],
                                    in_=st_scr[gw-2][0:1, 1021:1024].squeeze(0).partition_broadcast(P))
                        if e else None, "LS")
                    inc(e.dma_start(out=RB[:, 0:3],
                                    in_=st_scr[gw-2][1:2, 1021:1024].squeeze(0).partition_broadcast(P))
                        if e else None, "LS")
                mark(f"bcast_mr:{gw}", "LS")
            step("sc", [w("LD", f"stscr:{gw}")], bcast_mr)

            if ww == 0:
                def memh(e):
                    if e:
                        e.memset(MB[:, 0:3], 0.0)
                        e.memset(RB[:, 0:3], 0.0)
                step("ve", [], memh)

            def xhat(e):
                for dt in range(NDT):
                    if e:
                        e.tensor_sub(XHsd(dt), XHATd(dt), MB)
                        ins = e.tensor_mul(XHATd(dt), XHsd(dt), RB)
                    else:
                        ins = None
                    inc(ins, "V", 1)
                mark(f"xhat:{gw}", "V")
            step("ve", [w("LS", f"bcast_mr:{gw}"), w("LD", f"xsc:{gw}")], xhat)

            for dt in range(NDT):
                def conv_dt(e, dt=dt):
                    xh = XHATd(dt)
                    if e:
                        e.tensor_scalar_mul(CV1, xh[:, 0:1024], c_cw[:, dt*4:dt*4+1])
                        e.scalar_tensor_tensor(CV2, xh[:, 1:1025], c_cw[:, dt*4+1:dt*4+2], CV1, ALU.mult, ALU.add)
                        e.scalar_tensor_tensor(CV1, xh[:, 2:1026], c_cw[:, dt*4+2:dt*4+3], CV2, ALU.mult, ALU.add)
                        ins = e.scalar_tensor_tensor(CV2, xh[:, 3:1027], c_cw[:, dt*4+3:dt*4+4], CV1, ALU.mult, ALU.add)
                    else:
                        ins = None
                    inc(ins, "V", 1)
                    mark(f"conv:{gw}:{dt}", "V")
                wv = [w("V", f"xhat:{gw}")]
                if dt > 0:
                    wv.append(w("S", f"silu:{gw}:{dt-1}"))
                step("ve", wv, conv_dt)

                def silu_dt(e, dt=dt):
                    inc(e.activation(XCd(dt), CV2, ACT.Silu, bias=c_cb[:, dt:dt+1], scale=1.0)
                        if e else None, "S")
                    mark(f"silu:{gw}:{dt}", "S")
                step("sc", [w("V", f"conv:{gw}:{dt}")], silu_dt)

            def xcb_c(e):
                inc(e.copy(XCB, XC) if e else None, "S")
                mark(f"xcb:{gw}", "S")
            xcbw = [w("S", f"silu:{gw}:1")]
            if gw > 0:
                xcbw.append(w("LD", f"ypack:{gw-1}"))
            step("sc", xcbw, xcb_c)

            if dbg:
                def dmp_xh(e):
                    inc(e.dma_start(out=o_dxh[gw], in_=XHAT) if e else None, "SO")
                step("sy", [w("V", f"xhat:{gw}")], dmp_xh)
                def dmp_xc(e):
                    inc(e.dma_start(out=o_dxc[gw], in_=XC) if e else None, "SO")
                step("sy", [w("S", f"silu:{gw}:1")], dmp_xc)
                def dmp_pj(e):
                    inc(e.dma_start(out=o_dpj[gw], in_=PJ1) if e else None, "SO")
                step("sy", [w("S", f"pj_evac:{gw}")], dmp_pj)

            def projmm(e):
                XCBv = XCB.rearrange("p (a t) -> p a t", a=2)
                for hf in range(2):
                    for dt in range(NDT):
                        ins = e.matmul(PSA[:, hf*512:(hf+1)*512], cb_xpw[:, dt*160:dt*160+128],
                                       XCBv[:, dt, hf*512:(hf+1)*512],
                                       start=(dt == 0), stop=(dt == 1)) if e else None
                        inc(ins, "T")
                for hf in range(2):
                    for dt in range(NDT):
                        ins = e.matmul(PSB[0:32, hf*512:(hf+1)*512], cb_xpw[:, dt*160+128:dt*160+160],
                                       XCBv[:, dt, hf*512:(hf+1)*512],
                                       start=(dt == 0), stop=(dt == 1)) if e else None
                        inc(ins, "T")
                mark(f"proj_mm:{gw}", "T")
            step("te", [w("S", f"xcb:{gw}"), w("S", f"stat_evac:{gw}")], projmm)

            def pjevac(e):
                inc(e.copy(PJ1, PSA[:, 0:1024]) if e else None, "S")
                inc(e.copy(PJ2, PSB[0:32, 0:1024]) if e else None, "S")
                mark(f"pj_evac:{gw}", "S")
            step("sc", [w("T", f"proj_mm:{gw}")], pjevac)

            def bpack(e):
                inc(e.dma_start(out=cc_b_in[0:128, :], in_=PJ1) if e else None, "LD")
                inc(e.dma_start(out=cc_b_in[128:160, :], in_=PJ2) if e else None, "LD")
                mark(f"bpack:{gw}", "LD")
            step("sy", [w("S", f"pj_evac:{gw}")], bpack)

            def ar_b(e):
                ins = e.collective_compute("AllReduce", ALU.add,
                                           replica_groups=[list(range(8))],
                                           ins=[cc_b_in[:]], outs=[cc_b_out[:]]) if e else None
                inc(ins, "CC")
                mark(f"ar_b:{gw}", "CC")
            arbw = [w("LD", f"bpack:{gw}")]
            if gw > 0:
                arbw.append(w("LS", f"bc:{gw-1}:15"))
            step("gp", arbw, ar_b)
            if gw == 0:
                hooks["agw1"]()

            if dbg:
                def dmp_bo(e):
                    inc(e.dma_start(out=o_dbo[gw], in_=cc_b_out[:]) if e else None, "SO")
                step("sy", [w("CC", f"ar_b:{gw}")], dmp_bo)

        def wave(gw):
            beta, ww = gw % 2, gw // 2

            def dtild(e):
                inc(e.dma_start(out=DTI.rearrange("p (a t) -> p a t", a=2),
                                in_=cc_b_out[0:128, :].rearrange("p (a t) -> p a t", a=2))
                    if e else None, "LG")
                mark(f"dti:{gw}", "LG")
            step("gp", [w("CC", f"ar_b:{gw}")], dtild)

            for dt in range(NDT):
                def dtpmm(e, dt=dt):
                    DTIv = DTI.rearrange("p (a t) -> p a t", a=2)
                    for hf in range(2):
                        ins = e.matmul(PSC[:, hf*512:(hf+1)*512], cb_dtw[:, dt*128:(dt+1)*128],
                                       DTIv[:, hf, :], start=True, stop=True) if e else None
                        inc(ins, "T")
                    mark(f"dtp_mm:{gw}:{dt}", "T")
                tw2 = [w("LG", f"dti:{gw}")]
                if dt == 0:
                    if gw > 0:
                        tw2.append(w("S", f"a_trev:{gw-1}:15"))
                else:
                    tw2.append(w("V", f"delta:{gw}:0"))
                step("te", tw2, dtpmm)

                def sp_s(e, dt=dt):
                    if e:
                        e.activation(SP1, PSC[:, 0:1024], ACT.Abs, bias=c_dtb[:, dt:dt+1], scale=1.0)
                        e.activation(SP2, SP1, ACT.Exp, bias=0.0, scale=-1.0)
                        ins = e.activation(SP1, SP2, ACT.Ln, bias=1.0, scale=1.0)
                    else:
                        ins = None
                    inc(ins, "S", 1)
                    mark(f"sp_ln:{gw}:{dt}", "S")
                step("sc", [w("T", f"dtp_mm:{gw}:{dt}")], sp_s)

                def sp_v(e, dt=dt):
                    if e:
                        e.tensor_scalar(SP2, PSC[:, 0:1024], c_dtb[:, dt:dt+1], 0.0, ALU.add, ALU.max)
                        ins = e.tensor_add(DLTd(dt), SP1, SP2)
                    else:
                        ins = None
                    inc(ins, "V", 1)
                    mark(f"delta:{gw}:{dt}", "V")
                step("ve", [w("S", f"sp_ln:{gw}:{dt}")], sp_v)

                def u_v(e, dt=dt):
                    inc(e.tensor_mul(UUd(dt), DLTd(dt), XCd(dt)) if e else None, "V")
                    mark(f"u:{gw}:{dt}", "V")
                step("ve", [w("V", f"delta:{gw}:{dt}")], u_v)

            if dbg:
                def dmp_dl(e):
                    inc(e.dma_start(out=o_ddl[gw], in_=ARA[:, 6156:8204]) if e else None, "SO")
                step("sy", [w("V", f"delta:{gw}:1")], dmp_dl)

            if dbg:
                def dmp_pre(e):
                    inc(e.dma_start(out=o_ds3[gw], in_=STATE[:, 0:64]) if e else None, "SO")
                    mark(f"dmp_pre:{gw}", "SO")
                step("sy", [w("V", f"delta:{gw}:1")], dmp_pre)

            for n in range(NN):
                pg = n % 2

                def bc_n(e, n=n, pg=pg):
                    if e:
                        i1 = e.dma_start(out=BRO[pg],
                                         in_=cc_b_out[128+n:129+n, :].squeeze(0).partition_broadcast(P))
                        i2 = e.dma_start(out=CRO[pg],
                                         in_=cc_b_out[144+n:145+n, :].squeeze(0).partition_broadcast(P))
                        i1.then_inc(sems["LS"], 16)
                        i2.then_inc(sems["LS"], 16)
                    cnt["LS"] += 32
                    mark(f"bc:{gw}:{n}", "LS")
                wv = [w("CC", f"ar_b:{gw}")]
                if n < 2 and gw > 0:
                    wv.append(w("V", f"a_x2dt:{gw-1}"))
                if n >= 2:
                    wv.append(w("V", f"hc:{gw}:1:{n-2}"))
                step("sc", wv, bc_n)

                for dt in range(NDT):
                    col = (beta * 2 + dt) * 16 + n

                    def aexp(e, n=n, dt=dt, col=col):
                        if e and n >= 1:
                            e.copy(STATE[:, col-1:col], HT[dt][:, 1023:1024])
                        inc(e.activation(AT[dt], DLTd(dt), ACT.Exp, bias=0.0,
                                         scale=c_A[:, dt*16+n:dt*16+n+1]) if e else None, "S")
                        mark(f"aexp:{gw}:{dt}:{n}", "S")
                    wv2 = [w("V", f"delta:{gw}:{dt}")]
                    if n >= 1:
                        wv2.append(w("V", f"scan:{gw}:{dt}:{n-1}"))
                    if dbg and n == 1:
                        wv2.append(w("SO", f"dmp_h0:{gw}"))
                    step("sc", wv2, aexp)

                    def scanv(e, n=n, dt=dt, pg=pg, col=col):
                        if e:
                            e.tensor_mul(BT[dt], UUd(dt), BRO[pg])
                            e.tensor_tensor_scan(HT[dt], AT[dt], BT[dt],
                                                 STATE[:, col:col+1], ALU.mult, ALU.add)
                            sc_ins = e.tensor_mul(HCT[dt], HT[dt], CRO[pg])
                        else:
                            sc_ins = None
                        inc(sc_ins, "V", 1)
                        mark(f"scan:{gw}:{dt}:{n}", "V")
                        mark(f"hc:{gw}:{dt}:{n}", "V")
                    wv3 = [w("S", f"aexp:{gw}:{dt}:{n}"), w("LS", f"bc:{gw}:{n}"),
                           w("V", f"u:{gw}:{dt}")]
                    if n == 0 and gw > 0:
                        wv3.append(w("V", f"a_q2:{gw-1}"))
                        wv3.append(w("LS", f"xqt:{gw-1}"))
                    if n >= 1:
                        wv3.append(w("T", f"yacc:{gw}:{dt}:{n-1}"))
                    if dbg and n == 0 and dt == 0:
                        wv3.append(w("SO", f"dmp_pre:{gw}"))
                    step("ve", wv3, scanv)

                    def yacc(e, n=n, dt=dt):
                        PSY = PSD if dt == 0 else PSC
                        for hf in range(2):
                            ins = e.matmul(PSY[:, hf*512:(hf+1)*512], c_idb,
                                           HCT[dt][:, hf*512:(hf+1)*512],
                                           start=(n == 0), stop=(n == NN-1)) if e else None
                            inc(ins, "T")
                        mark(f"yacc:{gw}:{dt}:{n}", "T")
                    wv4 = [w("V", f"hc:{gw}:{dt}:{n}")]
                    if n == 0:
                        if dt == 0:
                            if gw > 0:
                                wv4.append(w("V", f"yev:{gw-1}:0"))
                        else:
                            wv4.append(w("V", f"delta:{gw}:1"))
                    step("te", wv4, yacc)

            if dbg:
                def dmp_h0(e):
                    inc(e.dma_start(out=o_dh0[gw][:, 0:2048], in_=ARB[:, 4096:6144]) if e else None, "SO")
                    inc(e.dma_start(out=o_dh0[gw][:, 2048:4096], in_=ARB[:, 0:2048]) if e else None, "SO")
                    inc(e.dma_start(out=o_dh0[gw][:, 4096:6144], in_=ARB[:, 2048:4096]) if e else None, "SO")
                    mark(f"dmp_h0:{gw}", "SO")
                step("sy", [w("V", f"hc:{gw}:1:0")], dmp_h0)

                def dmp_ab(e):
                    inc(e.dma_start(out=o_dab[gw][:, 0:2048], in_=ARB[:, 0:2048]) if e else None, "SO")
                    inc(e.dma_start(out=o_dab[gw][:, 2048:4096], in_=ARB[:, 2048:4096]) if e else None, "SO")
                    inc(e.dma_start(out=o_dhc[gw][:, 0:2048], in_=ARB[:, 6144:8192]) if e else None, "SO")
                    inc(e.dma_start(out=o_dhc[gw][:, 2048:2112], in_=STATE[:, 0:64]) if e else None, "SO")
                step("sy", [w("T", f"yacc:{gw}:1:{NN-1}")], dmp_ab)

            for dt in range(NDT):
                def yev(e, dt=dt):
                    PSY = PSD if dt == 0 else PSC
                    ins = e.scalar_tensor_tensor(YYd(dt), XHATd(dt)[:, 3:1027],
                                                 c_dp[:, dt:dt+1], PSY[:, 0:1024],
                                                 ALU.mult, ALU.add) if e else None
                    inc(ins, "V", 1)
                    mark(f"yev:{gw}:{dt}", "V")
                step("ve", [w("T", f"yacc:{gw}:{dt}:{NN-1}")], yev)

            if dbg:
                def dmp_yy(e):
                    inc(e.dma_start(out=o_dyy[gw], in_=YY) if e else None, "SO")
                    mark(f"dmp_yy:{gw}", "SO")
                step("sy", [w("V", f"yev:{gw}:1")], dmp_yy)

            if dbg:
                def dmp_s1(e):
                    inc(e.dma_start(out=o_ds2[gw][0], in_=STATE[:, 0:64]) if e else None, "SO")
                step("sy", [w("V", f"yev:{gw}:1")], dmp_s1)

            def yyb_c(e):
                inc(e.tensor_copy(YYB, YY) if e else None, "V")
                mark(f"yyb:{gw}", "V")
            step("ve", [w("V", f"yev:{gw}:1")], yyb_c)

            def ypack(e):
                inc(e.dma_start(out=cc_y_in.rearrange("(a p) t -> p a t", p=P),
                                in_=YYB.rearrange("p (a t) -> p a t", a=2)) if e else None, "LD")
                mark(f"ypack:{gw}", "LD")
            step("sy", [w("V", f"yyb:{gw}")], ypack)

            def ag_y(e):
                ins = e.collective_compute("AllGather", ALU.bypass,
                                           replica_groups=[list(range(8))],
                                           ins=[cc_y_in[:]], outs=[cc_y_out[:]]) if e else None
                inc(ins, "CC")
                mark(f"ag_y:{gw}", "CC")
            step("gp", [w("LD", f"ypack:{gw}")], ag_y)
            if gw + 1 < NW and (gw + 1) % 4 != 0:
                wave_f1(gw + 1)
            if gw == 0:
                hooks["agw2"]()

            def yread(e):
                if e:
                    rank = e.partition_id()
                    src = cc_y_out[:, :, bass.ts(rank, P)]
                    src = src.rearrange("r (a p) t -> p r a t", p=P)
                    dst = YASB.rearrange("p (r a t) -> p r a t", r=8, a=2)
                    ins = e.dma_start(out=dst, in_=src)
                else:
                    ins = None
                inc(ins, "LD")
                mark(f"yread:{gw}", "LD")
                inc(e.dma_start(out=XPT.rearrange("p (k t) -> p k t", k=16),
                                in_=i_xpt[gw]) if e else None, "LD")
                mark(f"xptld:{gw}", "LD")
            step("sy", [w("CC", f"ag_y:{gw}")], yread)

            def a_x2dt(e):
                inc(e.tensor_add(A_X2DT, XPT, YASB) if e else None, "V")
                mark(f"a_x2dt:{gw}", "V")
            a2w = [w("LD", f"xptld:{gw}"), w("V", f"yyb:{gw}")]
            if dbg:
                a2w.append(w("SO", f"dmp_yy:{gw}"))
            step("ve", a2w, a_x2dt)

            for k in range(16):
                def a_tr(e, k=k):
                    pg = k % 2
                    ins = e.transpose(PSC[:, pg*512:pg*512+128],
                                      A_X2DT[:, k*128:(k+1)*128], c_id) if e else None
                    inc(ins, "T")
                    mark(f"a_tr:{gw}:{k}", "T")
                wv = [w("V", f"a_x2dt:{gw}")]
                if k == 0:
                    wv.append(w("V", f"yev:{gw}:1"))
                if k >= 2:
                    wv.append(w("S", f"a_trev:{gw}:{k-2}"))
                step("te", wv, a_tr)

                def a_trev(e, k=k):
                    pg = k % 2
                    inc(e.copy(A_X2[:, k*128:(k+1)*128], PSC[:, pg*512:pg*512+128]) if e else None, "S")
                    mark(f"a_trev:{gw}:{k}", "S")
                step("sc", [w("T", f"a_tr:{gw}:{k}")], a_trev)

            if gw + 1 < NW and (gw + 1) % 4 != 0:
                wave_f2(gw + 1)

            def x2st(e):
                inc(e.dma_start(out=x2_scr[gw], in_=A_X2) if e else None, "LD")
                if dbg:
                    inc(e.dma_start(out=o_x2[gw], in_=A_X2) if e else None, "SO")
                mark(f"x2st:{gw}", "LD")
            step("sy", [w("S", f"a_trev:{gw}:15")], x2st)

            def a_sq(e):
                if e:
                    for dts in range(NDT):
                        c15 = (beta * 2 + dts) * 16 + 15
                        e.copy(STATE[:, c15:c15+1], HT[dts][:, 1023:1024])
                inc(e.activation(A_SQ, A_X2, ACT.Square, bias=0.0, scale=1.0,
                                 accum_out=SCL[:, 48:49]) if e else None, "S")
                mark(f"a_sq:{gw}", "S")
            step("sc", [w("S", f"a_trev:{gw}:15")], a_sq)

            def a_v(e):
                if e:
                    e.reduce_sum(SCL[:, 49:50], A_X2, axis=AX.X)
                    e.reduce_max(SCL[:, 50:51], A_X2, axis=AX.X)
                    ins = e.tensor_reduce(SCL[:, 51:52], A_X2, AX.X, ALU.min)
                else:
                    ins = None
                inc(ins, "V", 1)
                mark(f"a_v:{gw}", "V")
            avw = [w("S", f"a_sq:{gw}")]
            if dbg and gw > 0:
                avw.append(w("SO", f"dmp_scl:{gw-1}"))
            step("ve", avw, a_v)

            def a_b1w(e):
                for cc_ in range(4):
                    inc(e.dma_start(out=qs_a[cc_].unsqueeze(1), in_=SCL[:, 48+cc_:49+cc_]) if e else None, "LD")
                mark(f"a_b1w:{gw}", "LD")
            ab1w = [w("V", f"a_v:{gw}")]
            if dbg and gw > 0:
                ab1w.append(w("SO", f"dmp_aq:{gw-1}"))
            step("sy", ab1w, a_b1w)

            def a_b1(e):
                for cc_ in range(4):
                    inc(e.dma_start(out=aq(cc_), in_=qs_a[cc_].unsqueeze(0)) if e else None, "LD")
                mark(f"a_b1:{gw}", "LD")
            step("sy", [w("LD", f"a_b1w:{gw}")], a_b1)

            def a_rm1(e):
                if e:
                    e.tensor_scalar_mul(aq(4), aq(1), 1.0 / 2048.0)
                    e.tensor_scalar_mul(aq(5), aq(0), 1.0 / 2048.0)
                    e.tensor_mul(aq(6), aq(4), aq(4))
                    ins = e.tensor_sub(aq(5), aq(5), aq(6))
                else:
                    ins = None
                inc(ins, "V", 1)
                mark(f"a_rm1:{gw}", "V")
            step("ve", [w("LD", f"a_b1:{gw}")], a_rm1)

            def a_rs1(e):
                if e:
                    e.activation(aq(6), aq(5), ACT.Ln, bias=EPS1, scale=1.0)
                    ins = e.activation(aq(7), aq(6), ACT.Exp, bias=0.0, scale=-0.5)
                else:
                    ins = None
                inc(ins, "S", 1)
                mark(f"a_rs1:{gw}", "S")
            step("sc", [w("V", f"a_rm1:{gw}")], a_rs1)

            def a_rm2(e):
                if e:
                    e.tensor_mul(aq(8), aq(7), aq(7))
                    ins = e.tensor_mul(aq(9), aq(5), aq(8))
                else:
                    ins = None
                inc(ins, "V", 1)
                mark(f"a_rm2:{gw}", "V")
            step("ve", [w("S", f"a_rs1:{gw}")], a_rm2)

            def a_rs2(e):
                if e:
                    e.activation(aq(10), aq(9), ACT.Ln, bias=EPS2, scale=1.0)
                    ins = e.activation(aq(11), aq(10), ACT.Exp, bias=0.0, scale=-0.5)
                else:
                    ins = None
                inc(ins, "S", 1)
                mark(f"a_rs2:{gw}", "S")
            step("sc", [w("V", f"a_rm2:{gw}")], a_rs2)

            def a_rm3(e):
                if e:
                    e.tensor_mul(aq(12), aq(7), aq(11))
                    e.tensor_sub(aq(13), aq(2), aq(4))
                    e.tensor_sub(aq(14), aq(4), aq(3))
                    e.tensor_max(aq(15), aq(13), aq(14))
                    ins = e.tensor_mul(aq(13), aq(15), aq(12))
                else:
                    ins = None
                inc(ins, "V", 1)
                mark(f"a_rm3:{gw}", "V")
            step("ve", [w("S", f"a_rs2:{gw}")], a_rm3)

            def a_rs3(e):
                if e:
                    e.activation(aq(14), aq(13), ACT.Ln, bias=0.0, scale=1.0)
                    ins = e.activation(aq(15), aq(14), ACT.Exp, bias=0.0, scale=-1.0)
                else:
                    ins = None
                inc(ins, "S", 1)
                mark(f"a_rs3:{gw}", "S")
            step("sc", [w("V", f"a_rm3:{gw}")], a_rs3)

            def a_rm4(e):
                if e:
                    e.tensor_mul(aq(14), aq(12), aq(15))
                    e.tensor_scalar(aq(15), aq(13), SG[0:1, 4:5], None, ALU.mult)
                    e.tensor_scalar_mul(aq(13), aq(14), 128.0)
                    e.tensor_mul(aq(12), aq(4), aq(13))
                    ins = e.tensor_scalar_mul(aq(14), aq(12), -1.0)
                else:
                    ins = None
                inc(ins, "V", 1)
                mark(f"a_rm4:{gw}", "V")
            step("ve", [w("S", f"a_rs3:{gw}")], a_rm4)

            if dbg:
                def dmp_aq(e):
                    inc(e.dma_start(out=o_daq[gw], in_=STT2[0:1, 4096:6144]) if e else None, "SO")
                    mark(f"dmp_aq:{gw}", "SO")
                step("sy", [w("V", f"a_rm4:{gw}")], dmp_aq)

            def a_b2w(e):
                for i_ in range(3):
                    inc(e.dma_start(out=qs_b[i_].unsqueeze(0), in_=aq(13+i_)) if e else None, "LD")
                mark(f"a_b2w:{gw}", "LD")
            step("sy", [w("V", f"a_rm4:{gw}")], a_b2w)

            def a_b2(e):
                for i_, dst in ((0, scl(0, gw)), (1, scl(8, gw)), (2, scl(16, gw))):
                    inc(e.dma_start(out=dst, in_=qs_b[i_].unsqueeze(1)) if e else None, "LD")
                mark(f"a_scl:{gw}", "LD")
            step("sy", [w("LD", f"a_b2w:{gw}")], a_b2)

            def a_q2(e):
                if e:
                    e.tensor_scalar(A_SQ, A_X2, scl(0, gw), scl(8, gw), ALU.mult, ALU.add)
                    e.tensor_scalar(A_X2, A_SQ, CLIP_HI, CLIP_LO, ALU.min, ALU.max)
                    ins = e.tensor_scalar(A_XQB, A_X2, scl(16, gw), None, ALU.mult)
                else:
                    ins = None
                inc(ins, "V", 1)
                mark(f"a_q2:{gw}", "V")
            q2w = [w("LD", f"a_scl:{gw}"), w("LD", f"x2st:{gw}")]
            if dbg and gw > 0:
                q2w.append(w("SO", f"dmp_xq:{gw-1}"))
            step("ve", q2w, a_q2)

            if dbg:
                def dmp_s2(e):
                    inc(e.dma_start(out=o_ds2[gw][1], in_=GRD2[:, 0:64]) if e else None, "SO")
                step("sy", [w("V", f"a_q2:{gw}")], dmp_s2)

            def xqt(e):
                out_v = XQS[:, gw*2048:(gw+1)*2048].rearrange("p (k t) -> p k t", k=16)
                inc(e.dma_start_transpose(out_v, A_XQB) if e else None, "LS")
                mark(f"xqt:{gw}", "LS")
            step("sc", [w("V", f"a_q2:{gw}")], xqt)
            if dbg:
                def dmp_scl(e):
                    inc(e.dma_start(out=o_dscl[gw], in_=SCL[:, 0:64]) if e else None, "SO")
                    mark(f"dmp_scl:{gw}", "SO")
                step("sy", [w("LD", f"a_scl:{gw}")], dmp_scl)
                def dmp_xq(e):
                    inc(e.dma_start(out=o_dxq[gw], in_=A_XQB) if e else None, "SO")
                    mark(f"dmp_xq:{gw}", "SO")
                step("sy", [w("V", f"a_q2:{gw}")], dmp_xq)

        def ffn(H):
            for o in range(8):
                def w1ld(e, o=o):
                    src = cc_w1_out[o].rearrange("k p m -> p k m")
                    dst = WS[o % 2].rearrange("p (k m) -> p k m", k=16)
                    inc(e.dma_start(out=dst, in_=src) if e else None, "LG")
                    mark(f"f_w1:{H}:{o}", "LG")
                wv = [w("CC", "ag_w1")]
                if o < 2:
                    wv.append(w("V", f"yev:{4*H+3}:1"))
                    wv.append(w("V", f"a_q2:{4*H+3}"))
                    wv.append(w("LS", f"xqt:{4*H+3}"))
                else:
                    wv.append(w("S", f"f_sq:{H}:{o-2}:3"))
                step("gp", wv, w1ld)

                for s in range(4):
                    idx = o * 4 + s

                    def mm1(e, o=o, s=s, idx=idx):
                        PS = PSA if idx % 2 == 0 else PSB
                        wsv = WS[o % 2]
                        for k in range(16):
                            lhsT = XQS[:, ((4*H+s)*16+k)*128:((4*H+s)*16+k+1)*128]
                            for hf in range(2):
                                ins = e.matmul(PS[:, hf*512:(hf+1)*512], lhsT,
                                               wsv[:, k*1024+hf*512:k*1024+(hf+1)*512],
                                               start=(k == 0), stop=(k == 15)) if e else None
                                inc(ins, "T")
                        mark(f"f_mm1:{H}:{o}:{s}", "T")
                    wv = [w("LG", f"f_w1:{H}:{o}"), w("LS", f"xqt:{4*H+s}")]
                    if idx >= 2:
                        wv.append(w("S", f"f_sq:{H}:{(idx-2)//4}:{(idx-2)%4}"))
                    else:
                        wv.append(w("S", f"pj_evac:{4*H+3}"))
                    step("te", wv, mm1)

                    def gelu_s(e, o=o, s=s, idx=idx):
                        PS = PSA if idx % 2 == 0 else PSB
                        inc(e.activation(HBc(s, o), PS[:, 0:1024], ACT.Gelu, bias=0.0, scale=1.0,
                                         accum_out=GST[:, s*32+o:s*32+o+1]) if e else None, "S")
                        mark(f"f_gelu:{H}:{o}:{s}", "S")
                        inc(e.activation(F_A1, HBc(s, o), ACT.Square, bias=0.0, scale=1.0,
                                         accum_out=GST[:, s*32+8+o:s*32+8+o+1]) if e else None, "S")
                        mark(f"f_sq:{H}:{o}:{s}", "S")
                    step("sc", [w("T", f"f_mm1:{H}:{o}:{s}")], gelu_s)

                    def mmx(e, o=o, s=s):
                        if e:
                            e.reduce_max(GST[:, s*32+16+o:s*32+17+o], HBc(s, o), axis=AX.X)
                            ins = e.tensor_reduce(GST[:, s*32+24+o:s*32+25+o], HBc(s, o),
                                                  AX.X, ALU.min)
                        else:
                            ins = None
                        inc(ins, "V", 1)
                        mark(f"f_mmx:{H}:{o}:{s}", "V")
                    step("ve", [w("S", f"f_gelu:{H}:{o}:{s}")], mmx)

            for s in range(4):
                def f_hv(e, s=s):
                    if e:
                        e.reduce_sum(SCL[:, 48:49], GST[:, s*32:s*32+8], axis=AX.X)
                        e.reduce_sum(SCL[:, 49:50], GST[:, s*32+8:s*32+16], axis=AX.X)
                        e.reduce_max(SCL[:, 50:51], GST[:, s*32+16:s*32+24], axis=AX.X)
                        ins = e.tensor_reduce(SCL[:, 51:52], GST[:, s*32+24:s*32+32], AX.X, ALU.min)
                    else:
                        ins = None
                    inc(ins, "V", 1)
                    mark(f"f_hv:{H}:{s}", "V")
                step("ve", [w("V", f"f_mmx:{H}:7:{s}"), w("S", f"f_sq:{H}:7:{s}")], f_hv)

                if dbg:
                    def dmp_fh1(e, s=s):
                        inc(e.dma_start(out=o_fh1[4*H+s], in_=HBs(s)) if e else None, "SO")
                        mark(f"dmp_fh1:{H}:{s}", "SO")
                    step("sy", [w("S", f"f_sq:{H}:7:{s}"), w("V", f"f_mmx:{H}:7:{s}")], dmp_fh1)
                    def dmp_fgs(e, s=s):
                        inc(e.dma_start(out=o_fgs[4*H+s], in_=GST[:, s*32:(s+1)*32]) if e else None, "SO")
                    step("sy", [w("V", f"f_hv:{H}:{s}")], dmp_fgs)

                def f_hb1w(e, s=s):
                    for cc_ in range(4):
                        inc(e.dma_start(out=qs_a[cc_].unsqueeze(1), in_=SCL[:, 48+cc_:49+cc_]) if e else None, "LD")
                    mark(f"f_hb1w:{H}:{s}", "LD")
                step("sy", [w("V", f"f_hv:{H}:{s}")], f_hb1w)

                def f_hb1(e, s=s):
                    for cc_ in range(4):
                        inc(e.dma_start(out=aq(cc_), in_=qs_a[cc_].unsqueeze(0)) if e else None, "LD")
                    mark(f"f_hb1:{H}:{s}", "LD")
                step("sy", [w("LD", f"f_hb1w:{H}:{s}")], f_hb1)

                def f_hrm1(e, s=s):
                    if e:
                        e.tensor_scalar_mul(aq(4), aq(0), 1.0 / 8192.0)
                        e.tensor_scalar_mul(aq(5), aq(1), 1.0 / 8192.0)
                        e.tensor_mul(aq(6), aq(4), aq(4))
                        ins = e.tensor_sub(aq(5), aq(5), aq(6))
                    else:
                        ins = None
                    inc(ins, "V", 1)
                    mark(f"f_hrm1:{H}:{s}", "V")
                step("ve", [w("LD", f"f_hb1:{H}:{s}")], f_hrm1)

                def f_hrs1(e, s=s):
                    if e:
                        e.activation(aq(6), aq(5), ACT.Ln, bias=EPS2, scale=1.0)
                        ins = e.activation(aq(7), aq(6), ACT.Exp, bias=0.0, scale=-0.5)
                    else:
                        ins = None
                    inc(ins, "S", 1)
                    mark(f"f_hrs1:{H}:{s}", "S")
                step("sc", [w("V", f"f_hrm1:{H}:{s}")], f_hrs1)

                def f_hrm2(e, s=s):
                    if e:
                        e.tensor_sub(aq(8), aq(2), aq(4))
                        e.tensor_sub(aq(9), aq(4), aq(3))
                        e.tensor_max(aq(10), aq(8), aq(9))
                        ins = e.tensor_mul(aq(13), aq(10), aq(7))
                    else:
                        ins = None
                    inc(ins, "V", 1)
                    mark(f"f_hrm2:{H}:{s}", "V")
                step("ve", [w("S", f"f_hrs1:{H}:{s}")], f_hrm2)

                def f_hrs2(e, s=s):
                    if e:
                        e.activation(aq(11), aq(13), ACT.Ln, bias=0.0, scale=1.0)
                        ins = e.activation(aq(12), aq(11), ACT.Exp, bias=0.0, scale=-1.0)
                    else:
                        ins = None
                    inc(ins, "S", 1)
                    mark(f"f_hrs2:{H}:{s}", "S")
                step("sc", [w("V", f"f_hrm2:{H}:{s}")], f_hrs2)

                def f_hrm3(e, s=s):
                    if e:
                        e.tensor_mul(aq(14), aq(7), aq(12))
                        e.tensor_scalar(aq(15), aq(13), SG[0:1, 5:6], None, ALU.mult)
                        e.tensor_scalar_mul(aq(13), aq(14), 128.0)
                        e.tensor_mul(aq(12), aq(4), aq(13))
                        ins = e.tensor_scalar_mul(aq(14), aq(12), -1.0)
                    else:
                        ins = None
                    inc(ins, "V", 1)
                    mark(f"f_hrm3:{H}:{s}", "V")
                step("ve", [w("S", f"f_hrs2:{H}:{s}")], f_hrm3)

                def f_hb2w(e, s=s):
                    for i_ in range(3):
                        inc(e.dma_start(out=qs_b[i_].unsqueeze(0), in_=aq(13+i_)) if e else None, "LD")
                    mark(f"f_hb2w:{H}:{s}", "LD")
                step("sy", [w("V", f"f_hrm3:{H}:{s}")], f_hb2w)

                def f_hb2(e, s=s):
                    for i_, dst in ((0, scl(24, s)), (1, scl(32, s)), (2, scl(40, s))):
                        inc(e.dma_start(out=dst, in_=qs_b[i_].unsqueeze(1)) if e else None, "LD")
                    mark(f"f_hscl:{H}:{s}", "LD")
                step("sy", [w("LD", f"f_hb2w:{H}:{s}")], f_hb2)

                for chk in range(4):
                    def f_hq(e, s=s, chk=chk):
                        hv = ARAb[:, s*8192+chk*2048:s*8192+(chk+1)*2048]
                        if e:
                            e.tensor_scalar(F_A, hv, scl(24, s), scl(32, s), ALU.mult, ALU.add)
                            e.tensor_scalar(F_B, F_A, CLIP_HI, CLIP_LO, ALU.min, ALU.max)
                            ins = e.tensor_scalar(hv, F_B, scl(40, s), None, ALU.mult)
                        else:
                            ins = None
                        inc(ins, "V", 1)
                        mark(f"f_hq:{H}:{s}:{chk}", "V")
                    fhqw = [w("LD", f"f_hscl:{H}:{s}")]
                    if dbg and chk == 0:
                        fhqw.append(w("SO", f"dmp_fh1:{H}:{s}"))
                    step("ve", fhqw, f_hq)

            for s in range(4):
                if dbg:
                    def dmp_fh2(e, s=s):
                        inc(e.dma_start(out=o_fh2[4*H+s], in_=HBs(s)) if e else None, "SO")
                        mark(f"dmp_fh2:{H}:{s}", "SO")
                    step("sy", [w("V", f"f_hq:{H}:{s}:3")], dmp_fh2)

                def f_ht(e, s=s):
                    out_v = ARBb[:, s*8192:(s+1)*8192].rearrange("p (c t) -> p c t", c=64)
                    inc(e.dma_start_transpose(out_v, ARAb[:, s*8192:(s+1)*8192])
                        if e else None, "LS")
                    mark(f"f_ht:{H}:{s}", "LS")
                step("sc", [w("V", f"f_hq:{H}:{s}:3"), w("T", f"f_mm1:{H}:7:3")], f_ht)

            PSQ = [PSA[:, 0:512], PSA[:, 512:1024], PSB[:, 0:512], PSB[:, 512:1024]]
            for dc in range(4):
                for kh in range(2):
                    pg = (dc * 2 + kh) % 2

                    def w2ld(e, dc=dc, kh=kh, pg=pg):
                        for rr in range(2):
                            r = 2 * dc + rr
                            src = cc_w2_out[r, kh*32:(kh+1)*32].rearrange("kk p m -> p kk m")
                            dst = WS2[pg].rearrange("p (kk m) -> p kk m", kk=32)[:, :, rr*256:(rr+1)*256]
                            inc(e.dma_start(out=dst, in_=src) if e else None, "LG")
                        mark(f"f_w2:{H}:{dc}:{kh}", "LG")
                    wv = [w("CC", "ag_w2"), w("LS", f"f_ht:{H}:3")]
                    if dbg and dc == 0 and kh == 0:
                        wv.append(w("SO", f"dmp_fh2:{H}:3"))
                    if dc * 2 + kh >= 2:
                        prev = dc * 2 + kh - 2
                        wv.append(w("T", f"f_mm2:{H}:{prev//2}:{prev%2}:3"))
                    step("gp", wv, w2ld)

                    for s in range(4):
                        def mm2(e, dc=dc, kh=kh, s=s, pg=pg):
                            for kk in range(32):
                                lhsT = ARBb[:, s*8192+(kh*32+kk)*128:s*8192+(kh*32+kk+1)*128]
                                ins = e.matmul(PSQ[s], lhsT,
                                               WS2[pg][:, kk*512:(kk+1)*512],
                                               start=(kh == 0 and kk == 0),
                                               stop=(kh == 1 and kk == 31)) if e else None
                                inc(ins, "T")
                            mark(f"f_mm2:{H}:{dc}:{kh}:{s}", "T")
                        wv = [w("LG", f"f_w2:{H}:{dc}:{kh}")]
                        if dc == 0 and kh == 0:
                            wv.append(w("S", f"f_gelu:{H}:7:3"))
                        if dc > 0 and kh == 0:
                            wv.append(w("V", f"f_oc:{H}:{dc-1}:{s}"))
                        step("te", wv, mm2)

                for s in range(4):
                    pg2 = (dc * 4 + s) % 2

                    def x2c_ld(e, dc=dc, s=s, pg2=pg2):
                        inc(e.dma_start(out=X2C[:, pg2*512:(pg2+1)*512],
                                        in_=x2_scr[4*H+s][:, dc*512:(dc+1)*512]) if e else None, "LD")
                        mark(f"f_x2c:{H}:{dc}:{s}", "LD")
                    wv = [w("LD", f"x2st:{4*H+s}"), w("V", f"f_hq:{H}:3:3")]
                    if dc * 4 + s >= 2:
                        prev = dc * 4 + s - 2
                        wv.append(w("V", f"f_oc:{H}:{prev//4}:{prev%4}"))
                    step("sy", wv, x2c_ld)

                    def f_oc(e, dc=dc, s=s, pg2=pg2):
                        ins = e.tensor_add(OUTP[:, pg2*512:(pg2+1)*512], PSQ[s],
                                           X2C[:, pg2*512:(pg2+1)*512]) if e else None
                        inc(ins, "V", 1)
                        mark(f"f_oc:{H}:{dc}:{s}", "V")
                    ocw = [w("T", f"f_mm2:{H}:{dc}:1:{s}"), w("LD", f"f_x2c:{H}:{dc}:{s}")]
                    if dc * 4 + s >= 2:
                        prev = dc * 4 + s - 2
                        ocw.append(w("SO", f"f_ost:{H}:{prev//4}:{prev%4}"))
                    step("ve", ocw, f_oc)

                    def f_ost(e, dc=dc, s=s, pg2=pg2):
                        inc(e.dma_start(out=o_out[4*H+s][:, dc*512:(dc+1)*512],
                                        in_=OUTP[:, pg2*512:(pg2+1)*512]) if e else None, "SO")
                        mark(f"f_ost:{H}:{dc}:{s}", "SO")
                    step("sy", [w("V", f"f_oc:{H}:{dc}:{s}")], f_ost)

        p0()
        wave_f1(0)
        wave_f2(0)
        for gw in range(4):
            wave(gw)
        if not noffn:
            ffn(0)
        wave_f1(4)
        wave_f2(4)
        for gw in range(4, 8):
            wave(gw)
        if not noffn:
            ffn(1)

        def fin(e):
            mark("final_out", "SO")
        step("sy", [w("SO", "f_ost:1:3:3") if not noffn else w("LD", "x2st:7")], fin)

        for eng, waits, fn in steps:
            fn(None)
        for eng, waits, fn in steps:
            for (sem, key, d) in waits:
                assert key in M, f"wait on undefined mark {key}"
                assert M[key][0] == sem, f"wait sem mismatch: {key} marked on {M[key][0]}, waited on {sem}"
        mode["dry"] = False

        def run_engine(e, name):
            for k in cnt:
                pass
            for eng, waits, fn in steps:
                if eng != name:
                    continue
                for (sem, key, d) in waits:
                    e.wait_ge(sems[sem], M[key] + d)
                fn(e)

        emitted = {"done": False}

        def run_all(engines):
            for k in cnt:
                cnt[k] = 0
            for eng, waits, fn in steps:
                e = engines.get(eng)
                if e is not None:
                    for (sem, key, d) in waits:
                        msem, mval = M[key]
                        assert msem == sem, (key, msem, sem)
                        e.wait_ge(sems[sem], mval + d)
                fn(e)

        @blk.sync
        def _(eng):
            run_all({"sy": eng})

        @blk.scalar
        def _(eng):
            run_all({"sc": eng})

        @blk.vector
        def _(eng):
            run_all({"ve": eng})

        @blk.tensor
        def _(eng):
            run_all({"te": eng})

        @blk.gpsimd
        def _(eng):
            run_all({"gp": eng})

    return nc


def make_inmaps(x, conv_w, conv_b, xproj_w, dtproj_w, dtproj_b, A_log, D_param,
                w_fc, w_proj):
    f = np.float32
    x = np.asarray(x, f)
    xp3 = np.concatenate([np.zeros((2, 3, D), f), x], axis=1)
    xpwT = np.ascontiguousarray(np.asarray(xproj_w, f).T)
    w_fcT = np.ascontiguousarray(np.asarray(w_fc, f).T)
    w_projT = np.ascontiguousarray(np.asarray(w_proj, f).T)
    ims = []
    for c in range(8):
        d0 = 256 * c
        xsc = np.empty((NW, P, NDT, 1027), f)
        xpt = np.empty((NW, P, 16, P), f)
        for gw in range(NW):
            beta, ww = gw % 2, gw // 2
            blkx = xp3[beta, 1024*ww:1024*ww+1027, d0:d0+256]
            xsc[gw] = blkx.T.reshape(NDT, P, 1027).transpose(1, 0, 2)
            stri = x[beta, 1024*ww+128*c:1024*ww+128*c+128, :]
            xpt[gw] = stri.T.reshape(16, P, P).transpose(1, 0, 2)
        ws1 = w_fcT[:, 1024*c:1024*(c+1)].reshape(16, P, 1024).transpose(1, 0, 2)
        ws2 = w_projT[:, 256*c:256*(c+1)].reshape(64, P, 256).transpose(1, 0, 2)
        ims.append({
            "xsc": np.ascontiguousarray(xsc),
            "xpt": np.ascontiguousarray(xpt),
            "ws1": np.ascontiguousarray(ws1),
            "ws2": np.ascontiguousarray(ws2),
            "xpw": np.ascontiguousarray(xpwT[d0:d0+256].reshape(NDT, P, 160).transpose(1, 0, 2)),
            "dtw": np.ascontiguousarray(
                np.asarray(dtproj_w, f)[d0:d0+256, :].T.reshape(P, NDT, P)),
            "dtb": np.ascontiguousarray(np.asarray(dtproj_b, f)[d0:d0+256].reshape(NDT, P).T),
            "alog": np.ascontiguousarray(np.asarray(A_log, f)[d0:d0+256].reshape(NDT, P, NN).transpose(1, 0, 2)),
            "cwv": np.ascontiguousarray(np.asarray(conv_w, f)[d0:d0+256, 0, :].reshape(NDT, P, 4).transpose(1, 0, 2)),
            "cbv": np.ascontiguousarray(np.asarray(conv_b, f)[d0:d0+256].reshape(NDT, P).T),
            "dpv": np.ascontiguousarray(np.asarray(D_param, f)[d0:d0+256].reshape(NDT, P).T),
            "onesf": np.ones((P, 1), f),
            "cvals": np.broadcast_to(np.array([1e-5, 1e-6, -1.0, -0.5, 0, 0, 0, 0], f), (P, 8)).copy(),
            "identf": np.eye(P, dtype=f),
        })
    return ims


def kernel(L0_attn_x, L0_attn_conv_w, L0_attn_conv_b, L0_attn_xproj_w,
           L0_attn_dtproj_w, L0_attn_dtproj_b, L0_attn_A_log, L0_attn_D_param,
           L0_ffn_w_fc, L0_ffn_w_proj, dbg=False, noffn=False):
    ims = make_inmaps(L0_attn_x, L0_attn_conv_w, L0_attn_conv_b, L0_attn_xproj_w,
                      L0_attn_dtproj_w, L0_attn_dtproj_b, L0_attn_A_log,
                      L0_attn_D_param, L0_ffn_w_fc, L0_ffn_w_proj)
    nc = build(dbg=dbg, noffn=noffn)
    res = run_bass_kernel_spmd(nc, ims, core_ids=list(range(8)))
    out = np.empty((2, 4096, D), np.float32)
    for c in range(8):
        r = res.results[c]
        for gw in range(NW):
            beta, ww = gw % 2, gw // 2
            out[beta, 1024*ww+128*c:1024*ww+128*c+128, :] = r["out"][gw]
    return out
